# version 05
# baseline (speedup 1.0000x reference)
import numpy as np
import math
from contextlib import ExitStack

import concourse.bass as bass
import concourse.mybir as mybir
from concourse.bass_utils import run_bass_kernel_spmd

F32 = mybir.dt.float32
BF16 = mybir.dt.bfloat16
AF = mybir.ActivationFunctionType
ALU = mybir.AluOpType
AX = mybir.AxisListType

S = 2048
D = 2048
SH = S // 2
NEG = -1e30
LN_EPS = 1e-5
A_EPS = 1e-6

H_QK, H_V, DK, DV = 8, 16, 128, 128
FUSED = 4096
NE, NR, HF = 8, 6, 1024
NH_B, HD_B = 16, 128

DEBUG_TAPS = False
STAGE = 9


def ceil_div(a, c):
    return (a + c - 1) // c


class Eng:
    def __init__(self, nc, st, name):
        self.h = st.enter_context(nc.semaphore(name))
        self.n = 0

    def inc(self, ins, k=1):
        ins.then_inc(self.h, k)
        self.n += k
        return self.n


class B:
    def __init__(self, nc, block, st):
        self.nc = nc
        self.block = block
        self.st = st
        self.ten = Eng(nc, st, "s_ten")
        self.vec = Eng(nc, st, "s_vec")
        self.act = Eng(nc, st, "s_act")
        self.gpd = Eng(nc, st, "s_gpd")
        self.syd = Eng(nc, st, "s_syd")
        self.acd = Eng(nc, st, "s_acd")
        self.consts = {}
        self.bufs = {}
        self.gates = {}
        self.marks = []

    def mark(self, name):
        try:
            self.marks.append((name, self.nc.n_instructions()))
        except Exception:
            pass

    def sec(self, eng, fn):
        getattr(self.block, eng)(fn)

    _uid = 0

    def _u(self, name):
        B._uid += 1
        return f"{name}_{B._uid}"

    def sb(self, name, shape, dtype, stack=None):
        return (stack or self.st).enter_context(
            self.nc.sbuf_tensor(self._u(name), shape, dtype))

    def ps(self, name, shape, dtype, stack=None):
        return (stack or self.st).enter_context(
            self.nc.psum_tensor(self._u(name), shape, dtype))

    def gate_wait(self, e, name):
        g = self.gates.get(name)
        if g is not None:
            e.wait_ge(g[0].h, g[1])

    def gate_set(self, name, eng):
        self.gates[name] = (eng, eng.n)

    def vsp(self, v):
        v.tensor_copy(self.consts["spacer"][:, 0:32], self.consts["spacer"][:, 32:64])

    def ssp(self, s):
        s.activation(self.consts["spacer"][:, 64:96], self.consts["spacer"][:, 96:128],
                     AF.Abs)

    def barrier(self):
        engs = [self.ten, self.vec, self.act, self.gpd, self.syd, self.acd]
        snap = [(e, e.n) for e in engs]
        for sec_name in ("tensor", "vector", "scalar", "gpsimd", "sync"):
            def _(e, snap=snap):
                for (eng, n) in snap:
                    if n:
                        e.wait_ge(eng.h, n)
            self.sec(sec_name, _)


def host_constants():
    c = {}
    c["c_id_bf"] = np.eye(128, dtype=np.float32)
    c["c_id_f32"] = np.eye(128, dtype=np.float32)
    c["c_ones_bf"] = np.ones((1, 128), dtype=np.float32)
    c["c_ones_f32"] = np.ones((1, 128), dtype=np.float32)
    c["c_onescol_bf"] = np.ones((128, 1), dtype=np.float32)
    c["c_onesrow_bf"] = np.ones((1, 512), dtype=np.float32)
    p = np.arange(128, dtype=np.float32)[:, None]
    f = np.arange(128, dtype=np.float32)[None, :]
    c["c_mask_sl"] = np.where(f < p, 0.0, NEG).astype(np.float32)
    c["c_mask_iu"] = np.where(p <= f, 0.0, NEG).astype(np.float32)
    c["c_mask_su"] = np.where(f > p, NEG, 0.0).astype(np.float32)
    slopes = 2.0 ** (-8.0 / np.arange(1, NH_B + 1, dtype=np.float32))
    Fg = np.arange(S, dtype=np.float32)[None, :]
    c["c_alibi_t"] = (slopes[:, None, None] * (Fg - p)[None]).astype(np.float32)
    fd = np.arange(512, dtype=np.float32)[None, :]
    rel = fd - 384.0 - p
    c["c_alibi_d"] = np.where(rel[None] > 0, NEG,
                              slopes[:, None, None] * rel[None]).astype(np.float32)
    qt = np.arange(16, dtype=np.float32)[None, :]
    c["c_nmsh"] = (slopes[:, None] * 128.0 * qt)[:, None, :].astype(np.float32)
    return c


CONST_SHAPES = {
    "c_id_bf": ((128, 128), BF16), "c_id_f32": ((128, 128), F32),
    "c_ones_bf": ((1, 128), BF16), "c_ones_f32": ((1, 128), F32),
    "c_onescol_bf": ((128, 1), BF16), "c_onesrow_bf": ((1, 512), BF16),
    "c_mask_sl": ((128, 128), F32),
    "c_mask_iu": ((128, 128), F32), "c_mask_su": ((128, 128), F32),
    "c_alibi_t": ((NH_B, 128, S), F32), "c_nmsh": ((NH_B, 1, 16), F32),
    "c_alibi_d": ((NH_B, 128, 512), F32),
}


def declare_io(nc, debug_taps=False):
    io = {}
    def din(name, shape, dtype=F32):
        io[name] = nc.dram_tensor(name, list(shape), dtype, kind="ExternalInput").ap()
    din("x_in", (S, D))
    din("w_qkv", (FUSED, D)); din("w_gate", (D, D)); din("w_beta", (H_V, D))
    din("w_alpha", (H_V, D)); din("log_A", (1, H_V)); din("dt_bias", (1, H_V))
    din("conv_w", (FUSED, 4)); din("rms_w", (1, DV)); din("w_out", (D, D))
    din("Wqkv_b", (D, 3 * D)); din("Wo_b", (D, D))
    din("Wg", (D, NE)); din("bg", (1, NE))
    din("Wgate", (NR, D, HF)); din("bgate", (NR, HF))
    din("Wup", (NR, D, HF)); din("bup", (NR, HF))
    din("Wdown", (NR, HF, D)); din("bdown", (NR, D))
    din("Ws_up", (D, D)); din("bs_up", (1, D))
    din("Ws_down", (D, D)); din("bs_down", (1, D))
    for name, (shp, dt) in CONST_SHAPES.items():
        io[name] = nc.dram_tensor(name, list(shp), dt, kind="ExternalInput").ap()
    io["out"] = nc.dram_tensor("out", [S, D], F32, kind="ExternalOutput").ap()
    io["res"] = nc.dram_tensor("res", [S, D], F32, kind="Internal").ap()
    io["wqkvT"] = nc.dram_tensor("wqkvT", [D, FUSED], BF16, kind="Internal").ap()
    io["wgateT"] = nc.dram_tensor("wgateT", [D, D], BF16, kind="Internal").ap()
    io["woutT"] = nc.dram_tensor("woutT", [D, D], BF16, kind="Internal").ap()
    io["rowscr"] = nc.dram_tensor("rowscr", [64, S], F32, kind="Internal").ap()
    if debug_taps:
        io["dbg_hp"] = nc.dram_tensor("dbg_hp", [128, NR * 8 * SH], BF16,
                                      kind="ExternalOutput").ap()
        io["dbg_hs"] = nc.dram_tensor("dbg_hs", [128, 16 * SH], BF16,
                                      kind="ExternalOutput").ap()
        io["dbg_q"] = nc.dram_tensor("dbg_q", [128, 8 * SH], F32, kind="ExternalOutput").ap()
        io["dbg_k"] = nc.dram_tensor("dbg_k", [128, 8 * SH], F32, kind="ExternalOutput").ap()
        io["dbg_v"] = nc.dram_tensor("dbg_v", [128, 16 * SH], F32, kind="ExternalOutput").ap()
        io["dbg_ba"] = nc.dram_tensor("dbg_ba", [128, 2 * 128], F32, kind="ExternalOutput").ap()
        io["dbg_ctx"] = nc.dram_tensor("dbg_ctx", [128, 16 * SH], F32, kind="ExternalOutput").ap()
    return io


def load_consts(b, io):
    C = b.consts
    C["id_bf"] = b.sb("id_bf", [128, 128], BF16)
    C["id_f32"] = b.sb("id_f32", [128, 128], F32)
    C["ones_bf"] = b.sb("ones_bf", [1, 128], BF16)
    C["ones_f32"] = b.sb("ones_f32", [1, 128], F32)
    C["onescol_bf"] = b.sb("onescol_bf", [128, 1], BF16)
    C["onesrow_bf"] = b.sb("onesrow_bf", [1, 512], BF16)
    C["mask_sl"] = b.sb("mask_sl", [128, 128], F32)
    C["mask_iu"] = b.sb("mask_iu", [128, 128], F32)
    C["mask_su"] = b.sb("mask_su", [128, 128], F32)
    C["rmsw_bc"] = b.sb("rmsw_bc", [128, DV], F32)
    C["dtb_bc"] = b.sb("dtb_bc", [128, H_V], F32)
    C["negA_bc"] = b.sb("negA_bc", [128, H_V], F32)
    C["bg_bc"] = b.sb("bg_bc", [128, NE], F32)
    C["spacer"] = b.sb("spacer", [128, 128], F32)

    def _(g):
        loads = [(C["id_bf"], io["c_id_bf"]), (C["id_f32"], io["c_id_f32"]),
                 (C["ones_bf"], io["c_ones_bf"]), (C["ones_f32"], io["c_ones_f32"]),
                 (C["onescol_bf"], io["c_onescol_bf"]),
                 (C["onesrow_bf"], io["c_onesrow_bf"]), (C["mask_sl"], io["c_mask_sl"]),
                 (C["mask_iu"], io["c_mask_iu"]), (C["mask_su"], io["c_mask_su"])]
        for dst, src in loads:
            b.gpd.inc(g.dma_start(dst[:, :], src), 16)
        b.gpd.inc(g.dma_start(C["rmsw_bc"][:, :], io["rms_w"].to_broadcast([128, DV])), 16)
        b.gpd.inc(g.dma_start(C["dtb_bc"][:, :], io["dt_bias"].to_broadcast([128, H_V])), 16)
        b.gpd.inc(g.dma_start(C["negA_bc"][:, :], io["log_A"].to_broadcast([128, H_V])), 16)
        b.gpd.inc(g.dma_start(C["bg_bc"][:, :], io["bg"].to_broadcast([128, NE])), 16)
    b.sec("gpsimd", _)
    gn = b.gpd.n

    CVALS = [LN_EPS, A_EPS, 1.0, -0.5 * math.log(128.0), 0.0]
    C["cvals"] = b.sb("cvals", [128, len(CVALS)], F32)

    def _(v):
        v.memset(C["spacer"][:, :], 0.0)
        for i, val in enumerate(CVALS):
            ins = v.memset(C["cvals"][:, i:i + 1], val)
        b.vec.inc(ins)
    b.sec("vector", _)
    for i, val in enumerate(CVALS):
        b.nc.const_aps.register(F32, val, C["cvals"][:, i:i + 1])

    def _(s):
        s.wait_ge(b.gpd.h, gn)
        s.wait_ge(b.vec.h, b.vec.n)
        b.act.inc(s.activation(C["negA_bc"][:, :], C["negA_bc"][:, :], AF.Exp))
    b.sec("scalar", _)
    an = b.act.n

    def _(v):
        v.wait_ge(b.act.h, an)
        b.vec.inc(v.tensor_scalar(C["negA_bc"][:, :], C["negA_bc"][:, :], -1.0, None, ALU.mult))
    b.sec("vector", _)


def emit_wprep_attn_a(b, io, st):
    b.bufs["wbaT"] = b.sb("wbaT", [128, 16 * 32], BF16)
    wsm = b.sb("pp_wsm", [16, 2 * D], BF16, st)
    psb = b.ps("pp_psb", [128, 1024], BF16, st)
    stage = b.sb("pp_stage", [128, 1024], BF16, st)
    wstage = [b.sb(f"pp_w{i}", [128, D], BF16, st) for i in range(2)]

    def _(g):
        b.gpd.inc(g.dma_start(wsm[:, 0:D], io["w_beta"]), 16)
        b.gpd.inc(g.dma_start(wsm[:, D:2 * D], io["w_alpha"]), 16)
    b.sec("gpsimd", _)
    gn = b.gpd.n

    def _(t):
        t.wait_ge(b.gpd.h, gn)
        for kt in range(16):
            t.transpose(psb[:, 32 * kt:32 * kt + 16], wsm[:, 128 * kt:128 * (kt + 1)],
                        b.consts["id_bf"][0:16, 0:16])
            ins = t.transpose(psb[:, 32 * kt + 16:32 * kt + 32],
                              wsm[:, D + 128 * kt:D + 128 * (kt + 1)],
                              b.consts["id_bf"][0:16, 0:16])
        b.ten.inc(ins)
    b.sec("tensor", _)
    tn = b.ten.n

    def _(v):
        v.wait_ge(b.ten.h, tn)
        b.vec.inc(v.tensor_copy(b.bufs["wbaT"][:, :], psb[:, 0:512]))
    b.sec("vector", _)

    jobs = [(io["w_qkv"], io["wqkvT"], 32), (io["w_gate"], io["wgateT"], 16),
            (io["w_out"], io["woutT"], 16)]
    for (src, dst, nrt) in jobs:
        for rt in range(nrt):
            buf = wstage[rt % 2]
            bname = f"ppw{rt % 2}"

            def _(g, buf=buf, src=src, rt=rt, bname=bname):
                b.gate_wait(g, bname)
                b.gpd.inc(g.dma_start(buf[:, :], src[128 * rt:128 * (rt + 1), :]), 16)
            b.sec("gpsimd", _)
            g_n = b.gpd.n

            for half in range(2):
                def _(t, buf=buf, half=half, g_n=g_n):
                    t.wait_ge(b.gpd.h, g_n)
                    t.wait_ge(b.vec.h, b.vec.n)
                    for j in range(8):
                        dt_ = half * 8 + j
                        ins = t.transpose(psb[:, 128 * j:128 * (j + 1)],
                                          buf[:, 128 * dt_:128 * (dt_ + 1)],
                                          b.consts["id_bf"][:, :])
                    b.ten.inc(ins)
                b.sec("tensor", _)
                t_n = b.ten.n

                def _(v, t_n=t_n):
                    v.wait_ge(b.ten.h, t_n)
                    v.wait_ge(b.syd.h, b.syd.n)
                    b.vec.inc(v.tensor_copy(stage[:, :], psb[:, :]))
                b.sec("vector", _)
                v_n = b.vec.n

                def _(sy, dst=dst, rt=rt, half=half, v_n=v_n):
                    sy.wait_ge(b.vec.h, v_n)
                    dst3 = dst[:, 128 * rt:128 * (rt + 1)].rearrange(
                        "(j p) c -> p j c", p=128)[:, 8 * half:8 * half + 8, :]
                    b.syd.inc(sy.dma_start(
                        dst3, stage[:, :].rearrange("p (j c) -> p j c", j=8)), 16)
                b.sec("sync", _)
            b.gate_set(bname, b.ten)


def emit_ln(b, io, src_ap, s0, n_tiles, xlnt, xlnt_cols, st, want_lo=None,
            tile_cb=None):
    nxs = 1 if want_lo is not None else 2
    xs = [b.sb(f"ln_xs{i}", [128, D], F32, st) for i in range(nxs)]
    xf = b.sb("ln_xf", [128, D], F32, st) if want_lo is not None else None
    xb = b.sb("ln_xb", [128, D], BF16, st)
    xlo = b.sb("ln_xlo", [128, D], BF16, st) if want_lo is not None else None
    sqd = xlo if want_lo is not None else b.sb("ln_sqd", [128, D], BF16, st)
    s1 = b.sb("ln_s1", [128, 1], F32, st)
    ssq = b.sb("ln_ssq", [128, 1], F32, st)
    mean = b.sb("ln_mean", [128, 1], F32, st)
    var = b.sb("ln_var", [128, 1], F32, st)
    rstd = b.sb("ln_rstd", [128, 1], F32, st)
    nmr = b.sb("ln_nmr", [128, 1], F32, st)
    negmean = b.sb("ln_negmean", [128, 1], F32, st)
    psb = b.ps("ln_psb", [128, 1024], BF16, st)

    for it in range(n_tiles):
        buf = xs[it % nxs]
        bname = f"lnx{it % nxs}"

        def _(sy, buf=buf, it=it, bname=bname):
            b.gate_wait(sy, bname)
            b.syd.inc(sy.dma_start(buf[:, :],
                                   src_ap[s0 + 128 * it:s0 + 128 * (it + 1), :]), 16)
        b.sec("sync", _)
        d_n = b.syd.n

        def _(v, buf=buf, d_n=d_n):
            v.wait_ge(b.syd.h, d_n)
            b.gate_wait(v, "ln_var")
            b.vec.inc(v.reduce_sum(s1[:, :], buf[:, :], axis=AX.X))
        b.sec("vector", _)
        v_n = b.vec.n

        def _(s, buf=buf, v_n=v_n):
            s.wait_ge(b.vec.h, v_n)
            s.activation(sqd[:, :], buf[:, :], AF.Square)
            b.act.inc(s.activation(var[:, :], s1[:, :], AF.Square, scale=1.0 / D))
            b.gate_set("ln_var", b.act)
        b.sec("scalar", _)
        a_n = b.act.n

        def _(v, a_n=a_n):
            v.wait_ge(b.act.h, a_n)
            v.reduce_sum(ssq[:, :], sqd[:, :], axis=AX.X)
            v.tensor_scalar(negmean[:, :], s1[:, :], -1.0 / D, None, ALU.mult)
            b.vec.inc(v.scalar_tensor_tensor(var[:, :], ssq[:, :], 1.0 / D, var[:, :],
                                             ALU.mult, ALU.subtract))
        b.sec("vector", _)
        v_n = b.vec.n

        def _(s, v_n=v_n):
            s.wait_ge(b.vec.h, v_n)
            b.gate_wait(s, "ln_nrm")
            s.activation(rstd[:, :], var[:, :], AF.Ln, bias=LN_EPS)
            b.ssp(s)
            b.act.inc(s.activation(rstd[:, :], rstd[:, :], AF.Exp, scale=-0.5))
        b.sec("scalar", _)
        a_n = b.act.n

        def _(v, buf=buf, bname=bname, a_n=a_n):
            v.wait_ge(b.act.h, a_n)
            b.gate_wait(v, "ln_xb")
            v.tensor_tensor(nmr[:, :], negmean[:, :], rstd[:, :], ALU.mult)
            b.vsp(v)
            if want_lo is None:
                ins = v.tensor_scalar(xb[:, :], buf[:, :], rstd[:, :], nmr[:, :],
                                      ALU.mult, ALU.add)
            else:
                v.tensor_scalar(xf[:, :], buf[:, :], rstd[:, :], nmr[:, :],
                                ALU.mult, ALU.add)
                v.tensor_copy(xb[:, :], xf[:, :])
                ins = v.tensor_tensor(xlo[:, :], xf[:, :], xb[:, :], ALU.subtract)
            b.vec.inc(ins)
            b.gate_set(bname, b.vec)
            b.gate_set("ln_nrm", b.vec)
            b.vsp(v)
            b.vec.inc(v.tensor_copy(b.consts["spacer"][:, 0:32],
                                    b.consts["spacer"][:, 32:64]))
        b.sec("vector", _)
        v_n = b.vec.n

        srcs = [(xb, xlnt, xlnt_cols, 128 * it)] + (
            [(xlo, want_lo, 128, 0)] if want_lo is not None else [])
        for si, (tm, ft, fcols, coff) in enumerate(srcs):
            for grp in range(2):
                def _(t, tm=tm, grp=grp, v_n=v_n, si=si):
                    t.wait_ge(b.vec.h, b.vec.n)
                    t.wait_ge(b.vec.h, v_n)
                    if si == 1 and grp == 0:
                        b.gate_wait(t, "ln_lo_free")
                    for j in range(8):
                        dt_ = grp * 8 + j
                        ins = t.transpose(psb[:, 128 * j:128 * (j + 1)],
                                          tm[:, 128 * dt_:128 * (dt_ + 1)],
                                          b.consts["id_bf"][:, :])
                    b.ten.inc(ins)
                b.sec("tensor", _)
                t_n = b.ten.n
                if si == len(srcs) - 1 and grp == 1:
                    b.gate_set("ln_xb", b.ten)

                def _(v, ft=ft, grp=grp, t_n=t_n, fcols=fcols, coff=coff):
                    v.wait_ge(b.ten.h, t_n)
                    dst = ft[:, :].rearrange("p (j c) -> p j c", c=fcols)[
                        :, 8 * grp:8 * grp + 8, coff:coff + 128]
                    b.vec.inc(v.tensor_copy(dst,
                                            psb[:, :].rearrange("p (j c) -> p j c", j=8)))
                b.sec("vector", _)
        if tile_cb is not None:
            tile_cb(it)
    for gname in ("ln_var", "ln_nrm", "ln_xb", "ln_lo_free", "lnx0", "lnx1"):
        b.gates.pop(gname, None)


def emit_attn_a(b, io):
    C = b.consts
    stage = STAGE
    sa = ExitStack()
    XLNT = b.sb("XLNT", [128, 16 * SH], BF16, sa)
    QH = b.sb("QH", [128, H_QK * SH], BF16, sa)
    KH = b.sb("KH", [128, H_QK * SH], BF16, sa)
    VTOK = b.sb("VTOK", [128, 8 * 2048], BF16, sa)
    CTX = b.sb("CTX", [128, H_V * SH], BF16, sa)
    Sf = b.sb("Sf", [128, H_V * DV], F32, sa)
    Sbf = b.sb("Sbf", [128, H_V * DV], BF16, sa)
    TAILS = b.sb("TAILS", [128, 32 * 3], F32, sa)
    BETA = b.sb("BETA", [128, 16 * 8], F32, sa)
    NBETA = b.sb("NBETA", [128, 16 * 8], F32, sa)
    LCT = b.sb("LCT", [128, 16 * 8], F32, sa)
    GT = b.sb("GT", [128, 16], F32, sa)
    NGT = b.sb("NGT", [128, 16], F32, sa)
    TET = b.sb("TET", [128, 16], F32, sa)
    laT = b.sb("laT", [16, 128], F32, sa)
    lcumT = b.sb("lcumT", [16, SH], F32, sa)
    ZR16 = b.sb("ZR16", [16, 128], F32, sa)

    def _(v):
        v.memset(Sf[:, :], 0.0)
        v.memset(Sbf[:, :], 0.0)
        v.memset(TAILS[:, :], 0.0)
        b.vec.inc(v.memset(ZR16[:, :], 0.0))
    b.sec("vector", _)

    for half in range(2):
        s0 = half * SH
        b.mark(f"a1.ln{half}")
        ln_st = ExitStack()
        emit_ln(b, io, io["x_in"], s0, 8, XLNT, SH, ln_st)
        b.barrier()
        ln_st.close()
        xlnt_ready = b.vec.n

        if stage < 2:
            continue
        b.mark(f"a1.ba{half}")
        ba_st = ExitStack()
        ps_ba = b.ps("ps_ba", [128, 512], F32, ba_st)
        ps_tr = b.ps("ps_tr", [128, 512], F32, ba_st)
        zz = b.sb("ba_z", [128, 16], F32, ba_st)
        az = b.sb("ba_az", [128, 16], F32, ba_st)
        sp = b.sb("ba_sp", [128, 16], F32, ba_st)
        for c in range(8):
            def _(t, c=c):
                if c == 0:
                    t.wait_ge(b.vec.h, xlnt_ready)
                else:
                    t.wait_ge(b.vec.h, b.vec.n)
                    t.wait_ge(b.act.h, b.act.n)
                for k in range(16):
                    ins = t.matmul(ps_ba[:, 0:32], XLNT[:, k * SH + 128 * c:k * SH + 128 * (c + 1)],
                                   b.bufs["wbaT"][:, 32 * k:32 * (k + 1)],
                                   start=(k == 0), stop=(k == 15))
                b.ten.inc(ins)
            b.sec("tensor", _)
            t_n = b.ten.n

            def _(s, c=c, t_n=t_n):
                s.wait_ge(b.ten.h, t_n)
                b.act.inc(s.activation(BETA[:, 16 * c:16 * (c + 1)], ps_ba[:, 0:16], AF.Sigmoid))
            b.sec("scalar", _)
            a_n = b.act.n

            def _(v, c=c, a_n=a_n):
                v.wait_ge(b.act.h, a_n)
                v.tensor_scalar(NBETA[:, 16 * c:16 * (c + 1)], BETA[:, 16 * c:16 * (c + 1)],
                                -1.0, None, ALU.mult)
                b.vec.inc(v.tensor_tensor(zz[:, :], ps_ba[:, 16:32], C["dtb_bc"][:, :], ALU.add))
            b.sec("vector", _)
            v_n = b.vec.n

            def _(s, v_n=v_n):
                s.wait_ge(b.vec.h, v_n)
                s.activation(az[:, :], zz[:, :], AF.Abs)
                b.ssp(s)
                s.activation(az[:, :], az[:, :], AF.Exp, scale=-1.0)
                b.ssp(s)
                b.act.inc(s.activation(az[:, :], az[:, :], AF.Ln, bias=1.0))
            b.sec("scalar", _)
            a_n = b.act.n

            def _(v, c=c, a_n=a_n):
                v.wait_ge(b.act.h, a_n)
                v.tensor_scalar(sp[:, :], zz[:, :], 0.0, None, ALU.max)
                b.vsp(v)
                v.tensor_tensor(sp[:, :], sp[:, :], az[:, :], ALU.add)
                b.vsp(v)
                b.vec.inc(v.tensor_tensor(sp[:, :], sp[:, :], C["negA_bc"][:, :], ALU.mult))
            b.sec("vector", _)
            v_n = b.vec.n

            def _(t, c=c):
                t.wait_ge(b.vec.h, b.vec.n)
                b.ten.inc(t.transpose(ps_tr[0:16, 0:128], sp[:, :], C["id_f32"][:, :]))
            b.sec("tensor", _)
            t_n = b.ten.n

            def _(v, c=c, t_n=t_n):
                v.wait_ge(b.ten.h, t_n)
                v.tensor_copy(laT[:, :], ps_tr[0:16, 0:128])
                b.vsp(v)
                b.vec.inc(v.tensor_tensor_scan(lcumT[:, 128 * c:128 * (c + 1)],
                                               laT[:, :], ZR16[:, :],
                                               0.0, ALU.add, ALU.add))
            b.sec("vector", _)

        for c in range(8):
            def _(t, c=c):
                t.wait_ge(b.vec.h, b.vec.n)
                b.ten.inc(t.transpose(ps_tr[:, 128:144], lcumT[:, 128 * c:128 * (c + 1)],
                                      C["id_f32"][0:16, 0:16]))
            b.sec("tensor", _)
            t_n = b.ten.n

            def _(v, c=c, t_n=t_n):
                v.wait_ge(b.ten.h, t_n)
                b.vec.inc(v.tensor_copy(LCT[:, 16 * c:16 * (c + 1)], ps_tr[:, 128:144]))
            b.sec("vector", _)
        v_n = b.vec.n

        def _(sy, v_n=v_n):
            sy.wait_ge(b.vec.h, v_n)
            b.syd.inc(sy.dma_start(io["rowscr"][0:16, 0:SH], lcumT[:, :]), 16)
        b.sec("sync", _)
        lcum_hbm = b.syd.n
        b.barrier()
        ba_st.close()

        if DEBUG_TAPS and half == 0:
            def _(g):
                g.wait_ge(b.vec.h, b.vec.n)
                b.gpd.inc(g.dma_start(io["dbg_ba"][:, 0:128], BETA[:, :]), 16)
                b.gpd.inc(g.dma_start(io["dbg_ba"][:, 128:256], LCT[:, :]), 16)
                if STAGE == 2:
                    b.gpd.inc(g.dma_start(io["dbg_v"], XLNT[:, :]), 16)
                    b.gpd.inc(g.dma_start(io["dbg_q"][:, 0:512], b.bufs["wbaT"][:, :]), 16)
            b.sec("gpsimd", _)
            b.barrier()

        if stage < 3:
            continue
        b.mark(f"a1.proj{half}")
        pj_st = ExitStack()
        wk = [b.sb(f"pj_wk{i}", [128, 16 * 128], BF16, pj_st) for i in range(2)]
        cw = b.sb("pj_cw", [128, 4 * 2], F32, pj_st)
        fwin = b.sb("pj_fwin", [128, 516], F32, pj_st)
        acc = [b.sb(f"pj_acc{i}", [128, 512], F32, pj_st) for i in range(2)]
        raw = b.sb("pj_raw", [128, 512], BF16, pj_st)
        q2 = b.sb("pj_q2", [128, 512], BF16, pj_st)
        ps_f = [b.ps(f"ps_f{i}", [128, 512], F32, pj_st) for i in range(2)]
        ps_r = b.ps("ps_r", [128, 512], F32, pj_st)
        ps_b = b.ps("ps_bc", [128, 512], F32, pj_st)
        row_f = b.sb("row_f", [1, 512], F32, pj_st)
        vstg = b.sb("pj_vstg", [128, 512], BF16, pj_st)
        ps_vb = b.ps("pj_psvb", [128, 512], BF16, pj_st)

        for ft in range(32):
            wb = wk[ft % 2]
            wname = f"pjw{ft % 2}"

            def _(g, ft=ft, wb=wb, wname=wname):
                b.gate_wait(g, wname)
                src3 = io["wqkvT"][:, 128 * ft:128 * (ft + 1)].rearrange(
                    "(k p) c -> p k c", p=128)
                b.gpd.inc(g.dma_start(wb[:, :].rearrange("p (k c) -> p k c", k=16), src3), 16)
                b.gpd.inc(g.dma_start(cw[:, 4 * (ft % 2):4 * (ft % 2) + 4],
                                      io["conv_w"][128 * ft:128 * (ft + 1), :]), 16)
            b.sec("gpsimd", _)
            g_n = b.gpd.n

            for sc in range(2):
                pf = ps_f[sc]
                soff = 512 * sc

                def _(t, ft=ft, sc=sc, pf=pf, soff=soff, g_n=g_n):
                    t.wait_ge(b.gpd.h, g_n)
                    if ft == 0 and sc == 0:
                        t.wait_ge(b.vec.h, xlnt_ready)
                    b.gate_wait(t, f"psf{sc}")
                    for k in range(16):
                        ins = t.matmul(pf[:, :], wb[:, 128 * k:128 * (k + 1)],
                                       XLNT[:, k * SH + soff:k * SH + soff + 512],
                                       start=(k == 0), stop=(k == 15))
                    b.ten.inc(ins)
                b.sec("tensor", _)
                t_n = b.ten.n
                if sc == 1:
                    b.gate_set(wname, b.ten)

                cwc = 4 * (ft % 2)

                def _(v, ft=ft, sc=sc, pf=pf, t_n=t_n):
                    v.wait_ge(b.ten.h, t_n)
                    v.tensor_copy(fwin[:, 0:3], TAILS[:, 3 * ft:3 * ft + 3])
                    b.vec.inc(v.tensor_copy(fwin[:, 3:515], pf[:, :]))
                    b.gate_set(f"psf{sc}", b.vec)
                b.sec("vector", _)
                v_n = b.vec.n

                def _(v, ft=ft, cwc=cwc):
                    v.tensor_scalar(acc[0][:, :], fwin[:, 3:515], cw[:, cwc + 3:cwc + 4],
                                    None, ALU.mult)
                    v.scalar_tensor_tensor(acc[1][:, :], fwin[:, 2:514],
                                           cw[:, cwc + 2:cwc + 3], acc[0][:, :],
                                           ALU.mult, ALU.add)
                    v.tensor_copy(TAILS[:, 3 * ft:3 * ft + 3], fwin[:, 512:515])
                    b.vec.inc(v.scalar_tensor_tensor(acc[0][:, :], fwin[:, 1:513],
                                                     cw[:, cwc + 1:cwc + 2], acc[1][:, :],
                                                     ALU.mult, ALU.add))
                    b.vec.inc(v.scalar_tensor_tensor(acc[1][:, :], fwin[:, 0:512],
                                                     cw[:, cwc + 0:cwc + 1], acc[0][:, :],
                                                     ALU.mult, ALU.add))
                b.sec("vector", _)
                v_n = b.vec.n

                if ft >= 16:
                    hh = ft - 16

                    def _(s, v_n=v_n):
                        s.wait_ge(b.vec.h, v_n)
                        b.gate_wait(s, "vstg")
                        b.act.inc(s.activation(vstg[:, :], acc[1][:, :], AF.Silu))
                    b.sec("scalar", _)
                    a_n = b.act.n

                    def _(t, a_n=a_n):
                        t.wait_ge(b.act.h, a_n)
                        b.gate_wait(t, "pjpsb")
                        for j in range(4):
                            ins = t.transpose(ps_vb[:, 128 * j:128 * (j + 1)],
                                              vstg[:, 128 * j:128 * (j + 1)],
                                              b.consts["id_bf"][:, :])
                        b.ten.inc(ins)
                    b.sec("tensor", _)
                    t_n = b.ten.n
                    b.gate_set("vstg", b.ten)

                    def _(v, hh=hh, sc=sc, t_n=t_n):
                        v.wait_ge(b.ten.h, t_n)
                        dst = VTOK[:, :].rearrange("p (c x) -> p c x", c=8)[
                            :, 4 * sc:4 * sc + 4, 128 * hh:128 * (hh + 1)]
                        b.vec.inc(v.tensor_copy(
                            dst, ps_vb[:, :].rearrange("p (j x) -> p j x", j=4)))
                    b.sec("vector", _)
                    b.gate_set("pjpsb", b.vec)
                else:
                    isq = ft < 8
                    dstb = QH if isq else KH
                    hq = ft % 8

                    def _(s, v_n=v_n):
                        s.wait_ge(b.vec.h, v_n)
                        s.activation(raw[:, :], acc[1][:, :], AF.Silu)
                        b.act.inc(s.activation(q2[:, :], raw[:, :], AF.Square))
                    b.sec("scalar", _)
                    a_n = b.act.n

                    def _(t, a_n=a_n):
                        t.wait_ge(b.act.h, a_n)
                        b.gate_wait(t, "psr")
                        b.ten.inc(t.matmul(ps_r[0:1, :], C["onescol_bf"][:, :], q2[:, :],
                                           start=True, stop=True))
                    b.sec("tensor", _)
                    t_n = b.ten.n

                    def _(s, isq=isq, t_n=t_n):
                        s.wait_ge(b.ten.h, t_n)
                        s.activation(row_f[:, :], ps_r[0:1, :], AF.Ln, bias=A_EPS)
                        b.ssp(s)
                        bias = -0.5 * math.log(128.0) if isq else 0.0
                        b.act.inc(s.activation(row_f[:, :], row_f[:, :], AF.Exp,
                                               scale=-0.5, bias=bias))
                    b.sec("scalar", _)
                    a_n = b.act.n
                    b.gate_set("psr", b.act)

                    def _(t, a_n=a_n):
                        t.wait_ge(b.act.h, a_n)
                        b.gate_wait(t, "psb")
                        b.ten.inc(t.matmul(ps_b[:, :], C["ones_f32"][:, :], row_f[:, :],
                                           start=True, stop=True))
                    b.sec("tensor", _)
                    t_n = b.ten.n

                    def _(v, dstb=dstb, hq=hq, sc=sc, t_n=t_n):
                        v.wait_ge(b.ten.h, t_n)
                        b.vec.inc(v.tensor_tensor(
                            dstb[:, hq * SH + 512 * sc:hq * SH + 512 * (sc + 1)],
                            raw[:, :], ps_b[:, :], ALU.mult))
                    b.sec("vector", _)
                    b.gate_set("psb", b.vec)
        b.barrier()
        pj_st.close()
        qkv_ready_v = b.vec.n
        qkv_ready_a = b.act.n

        if DEBUG_TAPS and half == 0:
            def _(g):
                g.wait_ge(b.vec.h, b.vec.n)
                g.wait_ge(b.act.h, b.act.n)
                b.gpd.inc(g.dma_start(io["dbg_q"], QH[:, :]), 16)
                b.gpd.inc(g.dma_start(io["dbg_k"], KH[:, :]), 16)
                b.gpd.inc(g.dma_start(io["dbg_v"], VTOK[:, :]), 16)
            b.sec("gpsimd", _)
            b.barrier()

        if stage < 4:
            continue
        b.mark(f"a1.scan{half}")
        sc_st = ExitStack()
        BAL = [b.sb("sc_bal0", [128, 2048], F32, sc_st)]
        DT = b.sb("sc_dt", [128, 512], F32, sc_st)
        SCR = b.sb("sc_scr", [128, 512], F32, sc_st)
        NALL = b.sb("sc_nall", [128, 2048], BF16, sc_st)
        WALL = b.sb("sc_wall", [128, 2048], BF16, sc_st)
        QG = b.sb("sc_qg", [128, 2048], BF16, sc_st)
        EA = b.sb("sc_ea", [128, 16 * 128], BF16, sc_st)
        EB = b.sb("sc_eb", [128, 16 * 128], BF16, sc_st)
        FB = b.sb("sc_fb", [128, 16 * 128], BF16, sc_st)
        RK = [b.sb(f"sc_rk{i}", [128, 16 * 128], BF16, sc_st) for i in range(2)]
        KTG = b.sb("sc_ktg", [128, 16 * 128], BF16, sc_st)
        CT4 = b.sb("sc_ct4", [128, 512], BF16, sc_st)
        ssq4 = b.sb("sc_ssq4", [128, 4], F32, sc_st)
        rstd4 = b.sb("sc_rstd4", [128, 4], F32, sc_st)
        ps0 = b.ps("sc_ps0", [128, 512], F32, sc_st)
        ps1 = b.ps("sc_ps1", [128, 512], F32, sc_st)
        psA = b.ps("sc_psA", [128, 512], F32, sc_st)
        psB = b.ps("sc_psB", [128, 512], F32, sc_st)
        psC = b.ps("sc_psC", [128, 512], F32, sc_st)
        ps4 = b.ps("sc_ps4", [128, 512], F32, sc_st)
        ps5 = b.ps("sc_ps5", [128, 512], F32, sc_st)
        psb = b.ps("sc_psb", [128, 1024], BF16, sc_st)

        for c in range(8):
            co = 128 * c
            bal = BAL[0]
            balname = "bal0"

            def _(sy, c=c, bal=bal, balname=balname, lcum_hbm=lcum_hbm):
                sy.wait_ge(b.syd.h, lcum_hbm)
                b.gate_wait(sy, balname)
                src = io["rowscr"][0:16, 128 * c:128 * (c + 1)].unsqueeze(
                    0).to_broadcast([128, 16, 128])
                b.syd.inc(sy.dma_start(
                    bal[:, :].rearrange("p (h s) -> p h s", h=16), src), 16)
            b.sec("sync", _)
            bal_n = b.syd.n

            def _(s, c=c, bal_n=bal_n):
                s.wait_ge(b.syd.h, bal_n)
                b.gate_wait(s, "gt")
                b.act.inc(s.activation(GT[:, :], LCT[:, 16 * c:16 * (c + 1)], AF.Exp))
            b.sec("scalar", _)
            a_n = b.act.n

            def _(v, c=c, bal=bal, a_n=a_n):
                v.wait_ge(b.act.h, a_n)
                v.tensor_scalar(NGT[:, :], GT[:, :], -1.0, None, ALU.mult)
                b.vec.inc(v.tensor_tensor(TET[:, :], bal[:, 127::128],
                                          LCT[:, 16 * c:16 * (c + 1)], ALU.subtract))
            b.sec("vector", _)
            v_n = b.vec.n

            def _(s, v_n=v_n):
                s.wait_ge(b.vec.h, v_n)
                b.act.inc(s.activation(TET[:, :], TET[:, :], AF.Exp))
            b.sec("scalar", _)
            eg_n = b.act.n

            for hg in range(4):
                go = 512 * hg

                def _(t, c=c, hg=hg, co=co):
                    if c == 0 and hg == 0:
                        t.wait_ge(b.vec.h, qkv_ready_v)
                        t.wait_ge(b.act.h, qkv_ready_a)
                    b.gate_wait(t, "sc01")
                    for hh in range(4):
                        h = 4 * hg + hh
                        hq = h // 2
                        ksl = KH[:, hq * SH + co:hq * SH + co + 128]
                        t.matmul(ps0[:, 128 * hh:128 * (hh + 1)], ksl, ksl,
                                 start=True, stop=True)
                        ins = t.matmul(ps1[:, 128 * hh:128 * (hh + 1)], ksl,
                                       QH[:, hq * SH + co:hq * SH + co + 128],
                                       start=True, stop=True)
                    b.ten.inc(ins)
                b.sec("tensor", _)
                t_n = b.ten.n

                def _(v, c=c, hg=hg, go=go, bal=bal, t_n=t_n, eg_n=eg_n):
                    v.wait_ge(b.ten.h, t_n)
                    v.wait_ge(b.act.h, eg_n)
                    lcslab = LCT[:, 16 * c + 4 * hg:16 * c + 4 * hg + 4].unsqueeze(
                        2).to_broadcast([128, 4, 128])
                    bal3 = bal[:, go:go + 512].rearrange("p (h s) -> p h s", h=4)
                    v.tensor_tensor(DT[:, :].rearrange("p (h s) -> p h s", h=4),
                                    lcslab, bal3, ALU.subtract)
                    msl = C["mask_sl"][:, :].unsqueeze(1).to_broadcast([128, 4, 128])
                    b.vec.inc(v.tensor_tensor(SCR[:, :].rearrange("p (h s) -> p h s", h=4),
                                              DT[:, :].rearrange("p (h s) -> p h s", h=4),
                                              msl, ALU.add))
                b.sec("vector", _)
                v_n = b.vec.n

                def _(s, v_n=v_n, bal=bal, go=go):
                    s.wait_ge(b.vec.h, v_n)
                    s.activation(SCR[:, :], SCR[:, :], AF.Exp)
                    b.act.inc(s.activation(bal[:, go:go + 512], bal[:, go:go + 512], AF.Exp))
                b.sec("scalar", _)
                a_n = b.act.n

                def _(v, c=c, hg=hg, go=go, a_n=a_n):
                    v.wait_ge(b.act.h, a_n)
                    v.tensor_tensor(SCR[:, :], SCR[:, :], ps0[:, :], ALU.mult)
                    nbsl = NBETA[:, 16 * c + 4 * hg:16 * c + 4 * hg + 4].unsqueeze(
                        2).to_broadcast([128, 4, 128])
                    v.tensor_tensor(NALL[:, go:go + 512].rearrange("p (h s) -> p h s", h=4),
                                    SCR[:, :].rearrange("p (h s) -> p h s", h=4),
                                    nbsl, ALU.mult)
                    miu = C["mask_iu"][:, :].unsqueeze(1).to_broadcast([128, 4, 128])
                    b.vec.inc(v.tensor_tensor(SCR[:, :].rearrange("p (h s) -> p h s", h=4),
                                              miu, DT[:, :].rearrange("p (h s) -> p h s", h=4),
                                              ALU.subtract))
                b.sec("vector", _)
                v_n = b.vec.n

                def _(s, v_n=v_n):
                    s.wait_ge(b.vec.h, v_n)
                    b.act.inc(s.activation(SCR[:, :], SCR[:, :], AF.Exp))
                b.sec("scalar", _)
                a_n = b.act.n

                def _(v, c=c, hg=hg, go=go, co=co, a_n=a_n):
                    v.wait_ge(b.act.h, a_n)
                    v.tensor_tensor(WALL[:, go:go + 512], SCR[:, :], ps1[:, :], ALU.mult)
                    for hh in range(4):
                        h = 4 * hg + hh
                        hq = h // 2
                        ins = v.tensor_tensor(QG[:, 128 * h:128 * (h + 1)],
                                              QH[:, hq * SH + co:hq * SH + co + 128],
                                              bal[:, 128 * h:128 * (h + 1)], ALU.mult)
                    b.vec.inc(ins)
                b.sec("vector", _)
                b.gate_set("sc01", b.vec)
            b.gate_set("gt", b.vec)
            build_v = b.vec.n

            for grp in range(4):
                def _(t, co=co, grp=grp, build_v=build_v):
                    t.wait_ge(b.vec.h, b.vec.n)
                    for j in range(8):
                        idx = grp * 8 + j
                        if idx < 16:
                            src = NALL[:, 128 * idx:128 * (idx + 1)]
                        else:
                            hq_ = (idx - 16) // 2
                            src = KH[:, hq_ * SH + co:hq_ * SH + co + 128]
                        ins = t.transpose(psb[:, 128 * j:128 * (j + 1)], src,
                                          C["id_bf"][:, :])
                    b.ten.inc(ins)
                b.sec("tensor", _)
                t_n = b.ten.n

                def _(v, grp=grp, t_n=t_n):
                    v.wait_ge(b.ten.h, t_n)
                    if grp < 2:
                        b.vec.inc(v.tensor_copy(EA[:, 1024 * grp:1024 * (grp + 1)], psb[:, :]))
                    else:
                        g2 = grp - 2
                        for j in range(8):
                            h = g2 * 8 + j
                            ins = v.tensor_scalar(KTG[:, 128 * h:128 * (h + 1)],
                                                  psb[:, 128 * j:128 * (j + 1)],
                                                  TET[:, h:h + 1], None, ALU.mult)
                        b.vec.inc(ins)
                b.sec("vector", _)
            tr_v = b.vec.n

            for wave in range(4):
                def _(t, co=co, wave=wave, tr_v=tr_v):
                    t.wait_ge(b.vec.h, tr_v)
                    b.gate_wait(t, "ps4")
                    for hh in range(4):
                        h = 4 * wave + hh
                        hq = h // 2
                        ins = t.matmul(ps4[:, 128 * hh:128 * (hh + 1)],
                                       KH[:, hq * SH + co:hq * SH + co + 128],
                                       Sbf[:, 128 * h:128 * (h + 1)],
                                       start=True, stop=True)
                    b.ten.inc(ins)
                b.sec("tensor", _)
                t_n = b.ten.n

                def _(v, c=c, wave=wave, t_n=t_n):
                    v.wait_ge(b.ten.h, t_n)
                    for hh in range(4):
                        h = 4 * wave + hh
                        v.scalar_tensor_tensor(SCR[:, 128 * hh:128 * (hh + 1)],
                                               ps4[:, 128 * hh:128 * (hh + 1)],
                                               NGT[:, h:h + 1],
                                               VTOK[:, 2048 * c + 128 * h:2048 * c + 128 * (h + 1)],
                                               ALU.mult, ALU.add)
                    for hh in range(4):
                        h = 4 * wave + hh
                        ins = v.tensor_scalar(RK[0][:, 128 * h:128 * (h + 1)],
                                              SCR[:, 128 * hh:128 * (hh + 1)],
                                              BETA[:, 16 * c + h:16 * c + h + 1],
                                              None, ALU.mult)
                    b.vec.inc(ins)
                b.sec("vector", _)
                b.gate_set("ps4", b.vec)

            ecur, fcur = EA, NALL
            enxt, fnxt = EB, FB
            psets = [(psA, psB, psC), (ps0, ps1, ps4)]
            for lvl in range(7):
                rk_in = RK[lvl % 2]
                rk_out = RK[(lvl + 1) % 2]
                for wave in range(4):
                    wo = 512 * wave
                    spar = (4 * lvl + wave) % 2
                    pA, pB, pC = psets[spar]

                    def _(t, lvl=lvl, wave=wave, wo=wo, ecur=ecur, fcur=fcur,
                          rk_in=rk_in, pA=pA, pB=pB, pC=pC, spar=spar):
                        b.gate_wait(t, f"invs{spar}")
                        if spar == 1:
                            b.gate_wait(t, "ps4")
                            b.gate_wait(t, "sc01")
                        for hh in range(4):
                            h = 4 * wave + hh
                            esl = ecur[:, 128 * h:128 * (h + 1)]
                            ins = t.matmul(pA[:, 128 * hh:128 * (hh + 1)], esl,
                                           rk_in[:, 128 * h:128 * (h + 1)],
                                           start=True, stop=True)
                            if lvl < 6:
                                fsl = fcur[:, 128 * h:128 * (h + 1)]
                                t.matmul(pB[:, 128 * hh:128 * (hh + 1)], esl, fsl,
                                         start=True, stop=True)
                                ins = t.matmul(pC[:, 128 * hh:128 * (hh + 1)], fsl, esl,
                                               start=True, stop=True)
                        b.ten.inc(ins)
                    b.sec("tensor", _)
                    t_n = b.ten.n

                    def _(v, lvl=lvl, wo=wo, t_n=t_n, rk_in=rk_in, rk_out=rk_out,
                          enxt=enxt, fnxt=fnxt, pA=pA, pB=pB, pC=pC, spar=spar):
                        v.wait_ge(b.ten.h, t_n)
                        ins = v.tensor_tensor(rk_out[:, wo:wo + 512], pA[:, :],
                                              rk_in[:, wo:wo + 512], ALU.add)
                        if lvl < 6:
                            v.tensor_copy(fnxt[:, wo:wo + 512], pB[:, :])
                            ins = v.tensor_copy(enxt[:, wo:wo + 512], pC[:, :])
                        b.vec.inc(ins)
                        b.gate_set(f"invs{spar}", b.vec)
                        if spar == 1:
                            b.gate_set("sc01", b.vec)
                            b.gate_set("ps4", b.vec)
                    b.sec("vector", _)
                ecur, fcur, enxt, fnxt = enxt, fnxt, ecur, fcur
                if lvl == 0:
                    fnxt = FB if fcur is not FB else NALL
            U = RK[1]
            inv_v = b.vec.n

            for wave in range(4):
                wo = 512 * wave

                def _(t, wave=wave, wo=wo, inv_v=inv_v):
                    t.wait_ge(b.vec.h, inv_v)
                    b.gate_wait(t, "ps5")
                    for hh in range(4):
                        h = 4 * wave + hh
                        t.matmul(ps5[:, 128 * hh:128 * (hh + 1)],
                                 QG[:, 128 * h:128 * (h + 1)],
                                 Sbf[:, 128 * h:128 * (h + 1)], start=True, stop=False)
                        ins = t.matmul(ps5[:, 128 * hh:128 * (hh + 1)],
                                       WALL[:, 128 * h:128 * (h + 1)],
                                       U[:, 128 * h:128 * (h + 1)], start=False, stop=True)
                    b.ten.inc(ins)
                b.sec("tensor", _)
                t_n = b.ten.n

                def _(s, t_n=t_n):
                    s.wait_ge(b.ten.h, t_n)
                    for hh in range(4):
                        ins = s.activation(SCR[:, 128 * hh:128 * (hh + 1)],
                                           ps5[:, 128 * hh:128 * (hh + 1)], AF.Square,
                                           accum_out=ssq4[:, hh:hh + 1])
                    b.act.inc(ins)
                b.sec("scalar", _)
                a_n = b.act.n

                def _(s, a_n=a_n):
                    b.ssp(s)
                    s.activation(rstd4[:, :], ssq4[:, :], AF.Ln, scale=1.0 / DV, bias=A_EPS)
                    b.ssp(s)
                    b.act.inc(s.activation(rstd4[:, :], rstd4[:, :], AF.Exp, scale=-0.5))
                b.sec("scalar", _)
                a_n = b.act.n

                def _(v, wave=wave, a_n=a_n):
                    v.wait_ge(b.act.h, a_n)
                    rmb = C["rmsw_bc"][:, :].unsqueeze(1).to_broadcast([128, 4, 128])
                    v.tensor_tensor(SCR[:, :].rearrange("p (h s) -> p h s", h=4),
                                    ps5[:, :].rearrange("p (h s) -> p h s", h=4),
                                    rmb, ALU.mult)
                    b.vsp(v)
                    for hh in range(4):
                        ins = v.tensor_scalar(CT4[:, 128 * hh:128 * (hh + 1)],
                                              SCR[:, 128 * hh:128 * (hh + 1)],
                                              rstd4[:, hh:hh + 1], None, ALU.mult)
                    b.vec.inc(ins)
                b.sec("vector", _)
                v_n = b.vec.n
                b.gate_set("ps5", b.vec)

                def _(t, wave=wave, v_n=v_n):
                    t.wait_ge(b.vec.h, v_n)
                    b.gate_wait(t, "psbo")
                    for hh in range(4):
                        ins = t.transpose(psb[:, 512 + 128 * hh:512 + 128 * (hh + 1)],
                                          CT4[:, 128 * hh:128 * (hh + 1)], C["id_bf"][:, :])
                    b.ten.inc(ins)
                b.sec("tensor", _)
                t_n = b.ten.n

                def _(v, c=c, wave=wave, co=co, t_n=t_n):
                    v.wait_ge(b.ten.h, t_n)
                    dst = CTX[:, :].rearrange("p (h s) -> p h s", h=16)[
                        :, 4 * wave:4 * wave + 4, co:co + 128]
                    b.vec.inc(v.tensor_copy(
                        dst, psb[:, 512:1024].rearrange("p (h s) -> p h s", h=4)))
                b.sec("vector", _)
                b.gate_set("psbo", b.vec)

            for wave in range(4):
                def _(t, wave=wave):
                    t.wait_ge(b.vec.h, b.vec.n)
                    b.gate_wait(t, "ps4")
                    for hh in range(4):
                        h = 4 * wave + hh
                        ins = t.matmul(ps4[:, 128 * hh:128 * (hh + 1)],
                                       KTG[:, 128 * h:128 * (h + 1)],
                                       U[:, 128 * h:128 * (h + 1)], start=True, stop=True)
                    b.ten.inc(ins)
                b.sec("tensor", _)
                t_n = b.ten.n

                def _(v, wave=wave, t_n=t_n, bal=bal):
                    v.wait_ge(b.ten.h, t_n)
                    for hh in range(4):
                        h = 4 * wave + hh
                        v.scalar_tensor_tensor(Sf[:, 128 * h:128 * (h + 1)],
                                               Sf[:, 128 * h:128 * (h + 1)],
                                               bal[:, 128 * h + 127:128 * h + 128],
                                               ps4[:, 128 * hh:128 * (hh + 1)],
                                               ALU.mult, ALU.add)
                    for hh in range(4):
                        h = 4 * wave + hh
                        ins = v.tensor_copy(Sbf[:, 128 * h:128 * (h + 1)],
                                            Sf[:, 128 * h:128 * (h + 1)])
                    b.vec.inc(ins)
                b.sec("vector", _)
                b.gate_set("ps4", b.vec)
            b.gate_set(balname, b.vec)

        b.barrier()
        sc_st.close()
        scan_v = b.vec.n

        if DEBUG_TAPS and half == 0:
            def _(g):
                g.wait_ge(b.vec.h, b.vec.n)
                b.gpd.inc(g.dma_start(io["dbg_ctx"], CTX[:, :]), 16)
            b.sec("gpsimd", _)
            b.barrier()

        if stage < 5:
            continue
        b.mark(f"a1.gate{half}")
        gt_st = ExitStack()
        wg = [b.sb(f"gt_wg{i}", [128, 16 * 128], BF16, gt_st) for i in range(2)]
        gsb = [b.sb(f"gt_gsb{i}", [128, 512], BF16, gt_st) for i in range(2)]
        ps_g = [b.ps(f"gt_psg{i}", [128, 512], F32, gt_st) for i in range(2)]
        for ft in range(16):
            wb = wg[ft % 2]
            wname = f"gtw{ft % 2}"

            def _(g, ft=ft, wb=wb, wname=wname):
                b.gate_wait(g, wname)
                src3 = io["wgateT"][:, 128 * ft:128 * (ft + 1)].rearrange(
                    "(k p) c -> p k c", p=128)
                b.gpd.inc(g.dma_start(wb[:, :].rearrange("p (k c) -> p k c", k=16), src3), 16)
            b.sec("gpsimd", _)
            g_n = b.gpd.n

            for sc in range(2):
                pg = ps_g[sc]
                gb = gsb[sc]

                def _(t, ft=ft, sc=sc, wb=wb, pg=pg, g_n=g_n):
                    t.wait_ge(b.gpd.h, g_n)
                    b.gate_wait(t, f"psg{sc}")
                    for k in range(16):
                        ins = t.matmul(pg[:, :], wb[:, 128 * k:128 * (k + 1)],
                                       XLNT[:, k * SH + 512 * sc:k * SH + 512 * (sc + 1)],
                                       start=(k == 0), stop=(k == 15))
                    b.ten.inc(ins)
                b.sec("tensor", _)
                t_n = b.ten.n
                if sc == 1:
                    b.gate_set(wname, b.ten)

                def _(s, sc=sc, pg=pg, gb=gb, t_n=t_n):
                    s.wait_ge(b.ten.h, t_n)
                    b.gate_wait(s, f"gsb{sc}")
                    b.act.inc(s.activation(gb[:, :], pg[:, :], AF.Silu))
                b.sec("scalar", _)
                a_n = b.act.n
                b.gate_set(f"psg{sc}", b.act)

                def _(v, ft=ft, sc=sc, gb=gb, a_n=a_n):
                    v.wait_ge(b.act.h, a_n)
                    sl = CTX[:, ft * SH + 512 * sc:ft * SH + 512 * (sc + 1)]
                    b.vec.inc(v.tensor_tensor(sl, gb[:, :], sl, ALU.mult))
                b.sec("vector", _)
                b.gate_set(f"gsb{sc}", b.vec)
        b.barrier()
        gt_st.close()

        if stage < 6:
            continue
        b.mark(f"a1.out{half}")
        op_st = ExitStack()
        wo = [b.sb(f"op_wo{i}", [128, 16 * 128], BF16, op_st) for i in range(2)]
        osb = [b.sb(f"op_osb{i}", [128, 512], F32, op_st) for i in range(2)]
        rsb = [b.sb(f"op_rsb{i}", [128, 512], F32, op_st) for i in range(2)]
        ot = [b.sb(f"op_ot{i}", [128, 512], F32, op_st) for i in range(2)]
        ps_o = [b.ps(f"op_pso{i}", [128, 512], F32, op_st) for i in range(2)]
        ps_t = [b.ps(f"op_pst{i}", [128, 512], F32, op_st) for i in range(2)]
        for dt_ in range(16):
            wb = wo[dt_ % 2]
            wname = f"opw{dt_ % 2}"

            def _(g, dt_=dt_, wb=wb, wname=wname):
                b.gate_wait(g, wname)
                src3 = io["woutT"][:, 128 * dt_:128 * (dt_ + 1)].rearrange(
                    "(k p) c -> p k c", p=128)
                b.gpd.inc(g.dma_start(wb[:, :].rearrange("p (k c) -> p k c", k=16), src3), 16)
            b.sec("gpsimd", _)
            g_n = b.gpd.n

            for sc in range(2):
                soff = 512 * sc
                rb = rsb[sc]

                def _(sy, dt_=dt_, sc=sc, s0=s0, soff=soff, rb=rb):
                    b.gate_wait(sy, f"oprs{sc}")
                    src3 = io["x_in"][s0 + soff:s0 + soff + 512,
                                      128 * dt_:128 * (dt_ + 1)].rearrange(
                        "(j p) c -> p j c", p=128)
                    b.syd.inc(sy.dma_start(rb[:, :].rearrange("p (j c) -> p j c", j=4),
                                           src3), 16)
                b.sec("sync", _)
                r_n = b.syd.n

                po = ps_o[sc]
                pt = ps_t[sc]
                ob = osb[sc]
                otb = ot[sc]

                def _(t, dt_=dt_, sc=sc, wb=wb, soff=soff, po=po, g_n=g_n):
                    t.wait_ge(b.gpd.h, g_n)
                    b.gate_wait(t, f"pso{sc}")
                    for k in range(16):
                        ins = t.matmul(po[:, :], wb[:, 128 * k:128 * (k + 1)],
                                       CTX[:, k * SH + soff:k * SH + soff + 512],
                                       start=(k == 0), stop=(k == 15))
                    b.ten.inc(ins)
                b.sec("tensor", _)
                t_n = b.ten.n
                if sc == 1:
                    b.gate_set(wname, b.ten)

                def _(v, sc=sc, po=po, ob=ob, t_n=t_n):
                    v.wait_ge(b.ten.h, t_n)
                    b.gate_wait(v, f"osb{sc}")
                    b.vec.inc(v.tensor_copy(ob[:, :], po[:, :]))
                b.sec("vector", _)
                v_n = b.vec.n
                b.gate_set(f"pso{sc}", b.vec)

                def _(t, sc=sc, ob=ob, pt=pt, v_n=v_n):
                    t.wait_ge(b.vec.h, v_n)
                    b.gate_wait(t, f"pst{sc}")
                    for j in range(4):
                        ins = t.transpose(pt[:, 128 * j:128 * (j + 1)],
                                          ob[:, 128 * j:128 * (j + 1)], C["id_f32"][:, :])
                    b.ten.inc(ins)
                b.sec("tensor", _)
                t_n = b.ten.n
                b.gate_set(f"osb{sc}", b.ten)

                def _(v, sc=sc, pt=pt, otb=otb, t_n=t_n, r_n=r_n):
                    v.wait_ge(b.ten.h, t_n)
                    v.wait_ge(b.syd.h, r_n)
                    b.gate_wait(v, f"opot{sc}")
                    b.vec.inc(v.tensor_tensor(otb[:, :], pt[:, :], rb[:, :], ALU.add))
                b.sec("vector", _)
                v_n = b.vec.n
                b.gate_set(f"pst{sc}", b.vec)
                b.gate_set(f"oprs{sc}", b.vec)

                def _(sy, dt_=dt_, sc=sc, s0=s0, soff=soff, otb=otb, v_n=v_n):
                    sy.wait_ge(b.vec.h, v_n)
                    dst3 = io["res"][s0 + soff:s0 + soff + 512,
                                     128 * dt_:128 * (dt_ + 1)].rearrange(
                        "(j p) c -> p j c", p=128)
                    b.syd.inc(sy.dma_start(dst3, otb[:, :].rearrange("p (j c) -> p j c", j=4)),
                              16)
                b.sec("sync", _)
                b.gate_set(f"opot{sc}", b.syd)
        b.barrier()
        op_st.close()
    sa.close()


def emit_ffn(b, io, src_res, dst_res, tag):
    C = b.consts
    sa = ExitStack()
    HP = b.sb("fHP", [128, NR * 8 * SH], BF16, sa)
    HS = b.sb("fHS", [128, 16 * SH], BF16, sa)
    wg_hi = b.sb("f_wghi", [128, 16 * NE], BF16, sa)
    wg_lo = b.sb("f_wglo", [128, 16 * NE], BF16, sa)
    bias_c = b.sb("f_biasc", [128, 128], F32, sa)

    pp_st = ExitStack()
    wgf = b.sb("f_wgf", [128, 16 * NE], F32, pp_st)
    bst = b.sb("f_bst", [6, 2 * HF], F32, pp_st)
    bst2 = b.sb("f_bst2", [1, 2 * D], F32, pp_st)
    ps_p = b.ps("f_psp", [128, 512], F32, pp_st)

    def _(g):
        src3 = io["Wg"][:, :].rearrange("(k p) c -> p k c", p=128)
        b.gpd.inc(g.dma_start(wgf[:, :].rearrange("p (k c) -> p k c", k=16), src3), 16)
        b.gpd.inc(g.dma_start(bst[:, 0:HF], io["bgate"]), 16)
        b.gpd.inc(g.dma_start(bst[:, HF:2 * HF], io["bup"]), 16)
        b.gpd.inc(g.dma_start(bst2[:, 0:D], io["bs_up"]), 16)
        b.gpd.inc(g.dma_start(bst2[:, D:2 * D], io["bs_down"]), 16)
    b.sec("gpsimd", _)
    g_n = b.gpd.n

    def _(t, g_n=g_n):
        t.wait_ge(b.gpd.h, g_n)
        for ft in range(8):
            t.transpose(ps_p[:, 6 * ft:6 * (ft + 1)], bst[:, 128 * ft:128 * (ft + 1)],
                        C["id_f32"][0:6, 0:6])
            t.transpose(ps_p[:, 48 + 6 * ft:48 + 6 * (ft + 1)],
                        bst[:, HF + 128 * ft:HF + 128 * (ft + 1)], C["id_f32"][0:6, 0:6])
        for ft in range(16):
            t.transpose(ps_p[:, 96 + ft:97 + ft], bst2[:, 128 * ft:128 * (ft + 1)],
                        C["id_f32"][0:1, 0:1])
            ins = t.transpose(ps_p[:, 112 + ft:113 + ft],
                              bst2[:, D + 128 * ft:D + 128 * (ft + 1)],
                              C["id_f32"][0:1, 0:1])
        b.ten.inc(ins)
    b.sec("tensor", _)
    t_n = b.ten.n

    def _(v, g_n=g_n, t_n=t_n):
        v.wait_ge(b.ten.h, t_n)
        v.tensor_copy(bias_c[:, :], ps_p[:, 0:128])
        v.tensor_copy(wg_hi[:, :], wgf[:, :])
        b.vsp(v)
        b.vec.inc(v.tensor_tensor(wg_lo[:, :], wgf[:, :], wg_hi[:, :], ALU.subtract))
    b.sec("vector", _)
    b.barrier()
    pp_st.close()

    for half in range(2):
        s0 = half * SH
        b.mark(f"{tag}.ln{half}")
        xh_st = ExitStack()
        XLNT = b.sb("fXLNT", [128, 16 * SH], BF16, xh_st)
        ln_st = ExitStack()
        XLO_t = b.sb("fXLOt", [128, 16 * 128], BF16, ln_st)
        ps_r = b.ps("fr_ps", [128, 512], F32, ln_st)
        ps_t = b.ps("fr_pst", [128, 512], F32, ln_st)
        g8 = b.sb("fr_g8", [128, 8], F32, ln_st)
        g8b = b.sb("fr_g8b", [128, 8], F32, ln_st)
        e8 = b.sb("fr_e8", [128, 8], F32, ln_st)
        m1 = b.sb("fr_m1", [128, 1], F32, ln_st)
        m2 = b.sb("fr_m2", [128, 1], F32, ln_st)
        mx = b.sb("fr_mx", [128, 1], F32, ln_st)
        s8 = b.sb("fr_s8", [128, 1], F32, ln_st)
        keep = b.sb("fr_keep", [128, 8], F32, ln_st)
        w_tok = b.sb("fr_wtok", [128, 8], F32, ln_st)
        wrowT = b.sb("fr_wrowT", [8, SH], F32, ln_st)

        def router_tile(it):
            def _(t, it=it):
                t.wait_ge(b.vec.h, b.vec.n)
                b.gate_wait(t, "fr_ps")
                soff = it * 128
                for ki in range(16):
                    lhs_hi = XLNT[:, ki * SH + soff:ki * SH + soff + 128]
                    lhs_lo = XLO_t[:, 128 * ki:128 * (ki + 1)]
                    t.matmul(ps_r[:, 0:8], lhs_hi, wg_hi[:, 8 * ki:8 * (ki + 1)],
                             start=(ki == 0), stop=False)
                    t.matmul(ps_r[:, 0:8], lhs_hi, wg_lo[:, 8 * ki:8 * (ki + 1)],
                             start=False, stop=False)
                    ins = t.matmul(ps_r[:, 0:8], lhs_lo, wg_hi[:, 8 * ki:8 * (ki + 1)],
                                   start=False, stop=(ki == 15))
                b.ten.inc(ins)
                b.gate_set("ln_lo_free", b.ten)
            b.sec("tensor", _)
            t_n = b.ten.n

            def _(v, t_n=t_n):
                v.wait_ge(b.ten.h, t_n)
                v.tensor_tensor(g8[:, :], ps_r[:, 0:8], C["bg_bc"][:, :], ALU.add)
                b.vsp(v)
                v.tensor_reduce(m1[:, :], g8[:, :], axis=AX.X, op=ALU.min)
                v.tensor_reduce(mx[:, :], g8[:, :], axis=AX.X, op=ALU.max)
                b.vsp(v)
                v.tensor_scalar(g8b[:, :], g8[:, :], m1[:, :], None, ALU.is_equal)
                v.tensor_scalar(mx[:, :], mx[:, :], -1.0, None, ALU.mult)
                b.vsp(v)
                v.scalar_tensor_tensor(g8b[:, :], g8b[:, :], 1e30, g8[:, :],
                                       ALU.mult, ALU.add)
                b.vsp(v)
                b.vec.inc(v.tensor_reduce(m2[:, :], g8b[:, :], axis=AX.X, op=ALU.min))
            b.sec("vector", _)
            v_n = b.vec.n
            b.gate_set("fr_ps", b.vec)

            def _(s, v_n=v_n):
                s.wait_ge(b.vec.h, v_n)
                b.act.inc(s.activation(e8[:, :], g8[:, :], AF.Exp, bias=mx[:, :]))
            b.sec("scalar", _)
            a_n = b.act.n

            def _(v, a_n=a_n):
                v.wait_ge(b.act.h, a_n)
                v.tensor_scalar(keep[:, :], g8[:, :], m2[:, :], None, ALU.is_gt)
                b.vsp(v)
                v.tensor_tensor(e8[:, :], e8[:, :], keep[:, :], ALU.mult)
                b.vsp(v)
                v.tensor_reduce(s8[:, :], e8[:, :], axis=AX.X, op=ALU.add)
                b.vsp(v)
                v.reciprocal(s8[:, :], s8[:, :])
                b.vsp(v)
                b.vec.inc(v.tensor_scalar(w_tok[:, :], e8[:, :], s8[:, :], None, ALU.mult))
            b.sec("vector", _)
            v_n = b.vec.n

            def _(t, it=it, v_n=v_n):
                t.wait_ge(b.vec.h, v_n)
                b.gate_wait(t, "fr_pst")
                b.ten.inc(t.transpose(ps_t[0:8, 0:128], w_tok[:, :], C["id_f32"][:, :]))
            b.sec("tensor", _)
            t_n = b.ten.n

            def _(v, it=it, t_n=t_n):
                v.wait_ge(b.ten.h, t_n)
                b.vec.inc(v.tensor_copy(wrowT[:, 128 * it:128 * (it + 1)], ps_t[0:8, 0:128]))
            b.sec("vector", _)
            b.gate_set("fr_pst", b.vec)

        emit_ln(b, io, src_res, s0, 8, XLNT, SH, ln_st, want_lo=XLO_t,
                tile_cb=router_tile)
        v_n = b.vec.n

        def _(sy, v_n=v_n):
            sy.wait_ge(b.vec.h, v_n)
            b.syd.inc(sy.dma_start(io["rowscr"][32:40, 0:SH], wrowT[:, :]), 16)
        b.sec("sync", _)
        wrow_hbm = b.syd.n

        if DEBUG_TAPS and half == 0 and tag == "f1":
            def _(g):
                g.wait_ge(b.vec.h, b.vec.n)
                b.gpd.inc(g.dma_start(io["dbg_ba"][:, 0:8], w_tok[:, :]), 16)
            b.sec("gpsimd", _)

        b.barrier()
        ln_st.close()

        b.mark(f"{tag}.up{half}")
        up_st = ExitStack()
        wkg = [b.sb(f"fu_wg{i}", [128, 16 * 128], BF16, up_st) for i in range(2)]
        wku = [b.sb(f"fu_wu{i}", [128, 16 * 128], BF16, up_st) for i in range(2)]
        wbc = b.sb("fu_wbc", [128, 512 * 2], F32, up_st)
        wrow_sb = b.sb("fu_wrow", [1, 512], F32, up_st)
        sg = [b.sb(f"fu_sg{i}", [128, 512], BF16, up_st) for i in range(2)]
        ps_g = [b.ps(f"fu_psg{i}", [128, 512], F32, up_st) for i in range(2)]
        ps_u = [b.ps(f"fu_psu{i}", [128, 512], F32, up_st) for i in range(2)]
        ps_w = b.ps("fu_psw", [128, 512], F32, up_st)

        for ft in range(16):
            wb = wkg[ft % 2]
            wname = f"fuw{ft % 2}"

            def _(g, ft=ft, wb=wb, wname=wname):
                b.gate_wait(g, wname)
                src3 = io["Ws_up"][:, 128 * ft:128 * (ft + 1)].rearrange(
                    "(k p) c -> p k c", p=128)
                b.gpd.inc(g.dma_start(wb[:, :].rearrange("p (k c) -> p k c", k=16), src3), 16)
            b.sec("gpsimd", _)
            g_n = b.gpd.n

            for sc in range(2):
                pb = ps_g[sc]

                def _(t, ft=ft, sc=sc, wb=wb, pb=pb, g_n=g_n):
                    t.wait_ge(b.gpd.h, g_n)
                    b.gate_wait(t, f"fu_psg{sc}")
                    for k in range(16):
                        ins = t.matmul(pb[:, :], wb[:, 128 * k:128 * (k + 1)],
                                       XLNT[:, k * SH + 512 * sc:k * SH + 512 * (sc + 1)],
                                       start=(k == 0), stop=(k == 15))
                    b.ten.inc(ins)
                b.sec("tensor", _)
                t_n = b.ten.n
                if sc == 1:
                    b.gate_set(wname, b.ten)

                def _(s, ft=ft, sc=sc, pb=pb, t_n=t_n):
                    s.wait_ge(b.ten.h, t_n)
                    b.act.inc(s.activation(HS[:, ft * SH + 512 * sc:ft * SH + 512 * (sc + 1)],
                                           pb[:, :], AF.Silu, bias=bias_c[:, 96 + ft:97 + ft]))
                b.sec("scalar", _)
                b.gate_set(f"fu_psg{sc}", b.act)

        for n in range(NR):
            for ft in range(8):
                wbg_ = wkg[ft % 2]
                wbu_ = wku[ft % 2]
                wname = f"fuw{ft % 2}"
                uname = f"fuu{ft % 2}"

                def _(g, n=n, ft=ft, wbg_=wbg_, wbu_=wbu_, wname=wname, uname=uname):
                    b.gate_wait(g, wname)
                    b.gate_wait(g, uname)
                    src3 = io["Wgate"][n][:, 128 * ft:128 * (ft + 1)].rearrange(
                        "(k p) c -> p k c", p=128)
                    b.gpd.inc(g.dma_start(
                        wbg_[:, :].rearrange("p (k c) -> p k c", k=16), src3), 16)
                    src3 = io["Wup"][n][:, 128 * ft:128 * (ft + 1)].rearrange(
                        "(k p) c -> p k c", p=128)
                    b.gpd.inc(g.dma_start(
                        wbu_[:, :].rearrange("p (k c) -> p k c", k=16), src3), 16)
                b.sec("gpsimd", _)
                g_n = b.gpd.n

                for sc in range(2):
                    if ft == 0:
                        def _(sy, n=n, sc=sc, wrow_hbm=wrow_hbm):
                            sy.wait_ge(b.syd.h, wrow_hbm)
                            b.gate_wait(sy, f"fu_wrow")
                            b.syd.inc(sy.dma_start(
                                wrow_sb[:, :], io["rowscr"][32 + n:33 + n,
                                                            512 * sc:512 * (sc + 1)]), 16)
                        b.sec("sync", _)
                        d_n = b.syd.n

                        def _(t, sc=sc, d_n=d_n):
                            t.wait_ge(b.syd.h, d_n)
                            b.gate_wait(t, "fu_psw")
                            b.ten.inc(t.matmul(ps_w[:, :], C["ones_f32"][:, :],
                                               wrow_sb[:, :], start=True, stop=True))
                        b.sec("tensor", _)
                        t_n = b.ten.n
                        b.gate_set("fu_wrow", b.ten)

                        def _(v, sc=sc, t_n=t_n):
                            v.wait_ge(b.ten.h, t_n)
                            b.gate_wait(v, f"fu_wbc{sc}")
                            b.vec.inc(v.tensor_copy(wbc[:, 512 * sc:512 * (sc + 1)],
                                                    ps_w[:, :]))
                        b.sec("vector", _)
                        b.gate_set("fu_psw", b.vec)

                    pg = ps_g[sc]
                    pu = ps_u[sc]
                    sgb = sg[sc]

                    def _(t, n=n, ft=ft, sc=sc, wbg_=wbg_, wbu_=wbu_, pg=pg, pu=pu,
                          g_n=g_n):
                        t.wait_ge(b.gpd.h, g_n)
                        if ft == 0:
                            t.wait_ge(b.vec.h, b.vec.n)
                        b.gate_wait(t, f"fu_psg{sc}")
                        b.gate_wait(t, f"fu_psu{sc}")
                        for k in range(16):
                            t.matmul(pg[:, :], wbg_[:, 128 * k:128 * (k + 1)],
                                     XLNT[:, k * SH + 512 * sc:k * SH + 512 * (sc + 1)],
                                     start=(k == 0), stop=(k == 15))
                        for k in range(16):
                            ins = t.matmul(pu[:, :], wbu_[:, 128 * k:128 * (k + 1)],
                                           XLNT[:, k * SH + 512 * sc:k * SH + 512 * (sc + 1)],
                                           start=(k == 0), stop=(k == 15))
                        b.ten.inc(ins)
                    b.sec("tensor", _)
                    t_n = b.ten.n
                    if sc == 1:
                        b.gate_set(wname, b.ten)
                        b.gate_set(uname, b.ten)

                    def _(s, n=n, ft=ft, sc=sc, pg=pg, sgb=sgb, t_n=t_n):
                        s.wait_ge(b.ten.h, t_n)
                        b.gate_wait(s, f"fu_sg{sc}")
                        b.act.inc(s.activation(sgb[:, :], pg[:, :], AF.Silu,
                                               bias=bias_c[:, 6 * ft + n:6 * ft + n + 1]))
                    b.sec("scalar", _)
                    a_n = b.act.n
                    b.gate_set(f"fu_psg{sc}", b.act)

                    def _(v, n=n, ft=ft, sc=sc, pu=pu, sgb=sgb, a_n=a_n):
                        v.wait_ge(b.act.h, a_n)
                        dst = HP[:, (n * 8 + ft) * SH + 512 * sc:(n * 8 + ft) * SH + 512 * (sc + 1)]
                        v.scalar_tensor_tensor(dst, pu[:, :],
                                               bias_c[:, 48 + 6 * ft + n:48 + 6 * ft + n + 1],
                                               wbc[:, 512 * sc:512 * (sc + 1)],
                                               ALU.add, ALU.mult)
                        b.vsp(v)
                        b.vec.inc(v.tensor_tensor(dst, sgb[:, :], dst, ALU.mult))
                    b.sec("vector", _)
                    b.gate_set(f"fu_psu{sc}", b.vec)
                    b.gate_set(f"fu_sg{sc}", b.vec)
                    b.gate_set(f"fu_wbc{sc}", b.vec)
        b.barrier()
        if DEBUG_TAPS and half == 0 and tag == "f1":
            def _(sy):
                b.syd.inc(sy.dma_start(io["dbg_hp"], HP[:, :]), 16)
                b.syd.inc(sy.dma_start(io["dbg_hs"], HS[:, :]), 16)
            b.sec("sync", _)
            b.barrier()
        up_st.close()
        xh_st.close()

        b.mark(f"{tag}.down{half}")
        dn_st = ExitStack()
        wkd = [b.sb(f"fd_wd{i}", [128, (NR * 8 + 16) * 128], BF16, dn_st) for i in range(2)]
        osb = [b.sb(f"fd_osb{i}", [128, 512], F32, dn_st) for i in range(2)]
        rsb = [b.sb(f"fd_rsb{i}", [128, 512], F32, dn_st) for i in range(2)]
        ot = [b.sb(f"fd_ot{i}", [128, 512], F32, dn_st) for i in range(2)]
        ps_o = [b.ps(f"fd_pso{i}", [128, 512], F32, dn_st) for i in range(2)]
        ps_t2 = [b.ps(f"fd_pst{i}", [128, 512], F32, dn_st) for i in range(2)]
        bdown_bf = b.sb("f_bdownbf", [6, D], BF16, dn_st)
        w6 = b.sb("f_w6", [6, SH], BF16, dn_st)

        def _(g, wrow_hbm=wrow_hbm):
            g.wait_ge(b.syd.h, wrow_hbm)
            b.gpd.inc(g.dma_start(w6[:, :], io["rowscr"][32:38, 0:SH]), 16)
            b.gpd.inc(g.dma_start(bdown_bf[:, :], io["bdown"]), 16)
        b.sec("gpsimd", _)
        w6_n = b.gpd.n

        for dt_ in range(16):
            wb = wkd[dt_ % 2]
            wname = f"fdw{dt_ % 2}"

            def _(g, dt_=dt_, wb=wb, wname=wname):
                b.gate_wait(g, wname)
                for n in range(NR):
                    src3 = io["Wdown"][n][:, 128 * dt_:128 * (dt_ + 1)].rearrange(
                        "(k p) c -> p k c", p=128)
                    b.gpd.inc(g.dma_start(
                        wb[:, n * 1024:(n + 1) * 1024].rearrange("p (k c) -> p k c", k=8),
                        src3), 16)
                src3 = io["Ws_down"][:, 128 * dt_:128 * (dt_ + 1)].rearrange(
                    "(k p) c -> p k c", p=128)
                b.gpd.inc(g.dma_start(
                    wb[:, 6144:8192].rearrange("p (k c) -> p k c", k=16), src3), 16)
            b.sec("gpsimd", _)
            g_n = b.gpd.n

            for sc in range(2):
                soff = 512 * sc
                rb = rsb[sc]

                def _(sy, dt_=dt_, sc=sc, s0=s0, soff=soff, rb=rb):
                    b.gate_wait(sy, f"fdrs{sc}")
                    src3 = src_res[s0 + soff:s0 + soff + 512,
                                   128 * dt_:128 * (dt_ + 1)].rearrange(
                        "(j p) c -> p j c", p=128)
                    b.syd.inc(sy.dma_start(rb[:, :].rearrange("p (j c) -> p j c", j=4),
                                           src3), 16)
                b.sec("sync", _)
                r_n = b.syd.n

                po = ps_o[sc]
                pt2 = ps_t2[sc]
                ob = osb[sc]
                otb = ot[sc]

                def _(t, dt_=dt_, sc=sc, wb=wb, soff=soff, po=po, g_n=g_n, w6_n=w6_n):
                    t.wait_ge(b.gpd.h, g_n)
                    if dt_ == 0 and sc == 0:
                        t.wait_ge(b.gpd.h, w6_n)
                    b.gate_wait(t, f"fd_pso{sc}")
                    for n in range(NR):
                        for k in range(8):
                            t.matmul(po[:, :],
                                     wb[:, n * 1024 + 128 * k:n * 1024 + 128 * (k + 1)],
                                     HP[:, (n * 8 + k) * SH + soff:(n * 8 + k) * SH + soff + 512],
                                     start=(n == 0 and k == 0), stop=False)
                    for k in range(16):
                        t.matmul(po[:, :], wb[:, 6144 + 128 * k:6144 + 128 * (k + 1)],
                                 HS[:, k * SH + soff:k * SH + soff + 512],
                                 start=False, stop=False)
                    ins = t.matmul(po[:, :], bdown_bf[:, 128 * dt_:128 * (dt_ + 1)],
                                   w6[:, 512 * sc:512 * (sc + 1)],
                                   start=False, stop=True)
                    b.ten.inc(ins)
                b.sec("tensor", _)
                t_n = b.ten.n
                if sc == 1:
                    b.gate_set(wname, b.ten)

                def _(v, dt_=dt_, sc=sc, po=po, ob=ob, t_n=t_n):
                    v.wait_ge(b.ten.h, t_n)
                    b.gate_wait(v, f"fd_osb{sc}")
                    b.vec.inc(v.tensor_scalar(ob[:, :], po[:, :], 1.0,
                                              bias_c[:, 112 + dt_:113 + dt_],
                                              ALU.mult, ALU.add))
                b.sec("vector", _)
                v_n = b.vec.n
                b.gate_set(f"fd_pso{sc}", b.vec)

                def _(t, sc=sc, ob=ob, pt2=pt2, v_n=v_n):
                    t.wait_ge(b.vec.h, v_n)
                    b.gate_wait(t, f"fd_pst{sc}")
                    for j in range(4):
                        ins = t.transpose(pt2[:, 128 * j:128 * (j + 1)],
                                          ob[:, 128 * j:128 * (j + 1)], C["id_f32"][:, :])
                    b.ten.inc(ins)
                b.sec("tensor", _)
                t_n = b.ten.n
                b.gate_set(f"fd_osb{sc}", b.ten)

                def _(v, sc=sc, pt2=pt2, otb=otb, t_n=t_n, r_n=r_n):
                    v.wait_ge(b.ten.h, t_n)
                    v.wait_ge(b.syd.h, r_n)
                    b.gate_wait(v, f"fd_ot{sc}")
                    b.vec.inc(v.tensor_tensor(otb[:, :], pt2[:, :], rb[:, :], ALU.add))
                b.sec("vector", _)
                v_n = b.vec.n
                b.gate_set(f"fd_pst{sc}", b.vec)
                b.gate_set(f"fdrs{sc}", b.vec)

                def _(sy, dt_=dt_, sc=sc, s0=s0, soff=soff, otb=otb, v_n=v_n):
                    sy.wait_ge(b.vec.h, v_n)
                    dst3 = dst_res[s0 + soff:s0 + soff + 512,
                                   128 * dt_:128 * (dt_ + 1)].rearrange(
                        "(j p) c -> p j c", p=128)
                    b.syd.inc(sy.dma_start(dst3, otb[:, :].rearrange("p (j c) -> p j c", j=4)),
                              16)
                b.sec("sync", _)
                b.gate_set(f"fd_ot{sc}", b.syd)
        b.barrier()
        dn_st.close()
    sa.close()


def emit_attn_b(b, io, src_res, dst_res):
    C = b.consts
    sa = ExitStack()
    XLNT = b.sb("bXLNT", [128, 16 * S], BF16, sa)
    YT = b.sb("bYT", [128, 16 * S], BF16, sa)

    b.mark("ab.ln")
    ln_st = ExitStack()
    emit_ln(b, io, src_res, 0, 16, XLNT, S, ln_st)
    b.barrier()
    ln_st.close()

    at_st = ExitStack()
    QT2 = [b.sb("ab_qt0", [128, S], BF16, at_st)] * 2
    KT2 = [b.sb("ab_kt0", [128, S], BF16, at_st)] * 2
    VT2 = [b.sb("ab_vt0", [128, 16 * 128], BF16, at_st)] * 2
    WQ2 = [b.sb("ab_wq0", [128, 16 * 128], BF16, at_st)] * 2
    WK2 = [b.sb("ab_wk0", [128, 16 * 128], BF16, at_st)] * 2
    WV2 = [b.sb("ab_wv0", [128, 16 * 128], BF16, at_st)] * 2
    TH2 = [b.sb("ab_th0", [128, S], F32, at_st)] * 2
    SS2 = [b.sb(f"ab_ss{i}", [128, S], F32, at_st) for i in range(2)]
    PP2 = [b.sb(f"ab_pp{i}", [128, S], BF16, at_st) for i in range(2)]
    PT2 = [b.sb(f"ab_pt{i}", [128, S], BF16, at_st) for i in range(2)]
    DEN2 = [b.sb(f"ab_den{i}", [128, 1], F32, at_st) for i in range(2)]
    RR2 = [b.sb(f"ab_rr{i}", [128, 1], F32, at_st) for i in range(2)]
    DG2 = [b.sb(f"ab_dg{i}", [128, 128], BF16, at_st) for i in range(2)]
    NM2 = [b.sb(f"ab_nm{i}", [128, 16], F32, at_st) for i in range(2)]
    Q2B = b.sb("ab_q2b", [128, 512], BF16, at_st)
    row4 = b.sb("ab_row4", [1, 512], F32, at_st)
    KNC = b.sb("ab_knc", [1, 8], F32, at_st)
    NMSH = b.sb("ab_nmsh", [128, 16], F32, at_st)
    KNM = b.sb("ab_knm", [1, 1], F32, at_st)
    ps_q = b.ps("ab_psq", [128, 512], F32, at_st)
    ps_s = [b.ps(f"ab_pss{i}", [128, 512], F32, at_st) for i in range(2)]
    ps_ys = [b.ps(f"ab_psy{i}", [128, 512], F32, at_st) for i in range(2)]
    ps_pts = [b.ps(f"ab_pspt{i}", [128, 512], F32, at_st) for i in range(2)]
    psvb = b.ps("ab_psvb", [128, 512], BF16, at_st)
    vstg = b.sb("ab_vstg", [128, 512], BF16, at_st)

    for h in range(NH_B):
        b.mark(f"ab.h{h}")
        slope = 2.0 ** (-8.0 / (h + 1))
        par = h % 2
        QT, KT, VT = QT2[par], KT2[par], VT2[par]
        WQ, WK, WV, TH = WQ2[par], WK2[par], WV2[par], TH2[par]

        def _(g, h=h, par=par, WQ=WQ, WK=WK, WV=WV):
            b.gate_wait(g, "ab_w0")
            for (dst, coff) in ((WQ, 128 * h), (WK, D + 128 * h), (WV, 2 * D + 128 * h)):
                src3 = io["Wqkv_b"][:, coff:coff + 128].rearrange("(k p) c -> p k c", p=128)
                ins = g.dma_start(dst[:, :].rearrange("p (k c) -> p k c", k=16), src3)
                b.gpd.inc(ins, 16)
        b.sec("gpsimd", _)
        g_n = b.gpd.n

        def _(sy, h=h, par=par, TH=TH):
            b.gate_wait(sy, "ab_th0")
            b.syd.inc(sy.dma_start(TH[:, :], io["c_alibi_t"][h]), 16)
            b.gate_wait(sy, "ab_nmsh")
            b.syd.inc(sy.dma_start(NMSH[:, :],
                                   io["c_nmsh"][h].to_broadcast([128, 16])), 16)
        b.sec("sync", _)
        th_n = b.syd.n

        for (dst, W_, scale, bank, bname) in (
                (QT, WQ, 1.0 / math.sqrt(128.0), ps_q, "ab_psq"),
                (KT, WK, None, ps_pts[0], "ab_pt0")):
            for sc in range(4):
                def _(t, sc=sc, W_=W_, bank=bank, bname=bname, g_n=g_n):
                    t.wait_ge(b.gpd.h, g_n)
                    b.gate_wait(t, bname)
                    for k in range(16):
                        ins = t.matmul(bank[:, :], W_[:, 128 * k:128 * (k + 1)],
                                       XLNT[:, k * S + 512 * sc:k * S + 512 * (sc + 1)],
                                       start=(k == 0), stop=(k == 15))
                    b.ten.inc(ins)
                b.sec("tensor", _)
                t_n = b.ten.n

                def _(v, dst=dst, sc=sc, scale=scale, bank=bank, t_n=t_n):
                    v.wait_ge(b.ten.h, t_n)
                    if scale is None:
                        ins = v.tensor_copy(dst[:, 512 * sc:512 * (sc + 1)], bank[:, :])
                    else:
                        ins = v.tensor_scalar(dst[:, 512 * sc:512 * (sc + 1)], bank[:, :],
                                              scale, None, ALU.mult)
                    b.vec.inc(ins)
                b.sec("vector", _)
                b.gate_set(bname, b.vec)
        for sc in range(4):
            def _(t, sc=sc, WV=WV, g_n=g_n):
                t.wait_ge(b.gpd.h, g_n)
                b.gate_wait(t, "ab_psy0")
                for k in range(16):
                    ins = t.matmul(ps_ys[0][:, :], WV[:, 128 * k:128 * (k + 1)],
                                   XLNT[:, k * S + 512 * sc:k * S + 512 * (sc + 1)],
                                   start=(k == 0), stop=(k == 15))
                b.ten.inc(ins)
            b.sec("tensor", _)
            t_n = b.ten.n

            def _(v, sc=sc, t_n=t_n):
                v.wait_ge(b.ten.h, t_n)
                b.gate_wait(v, "ab_vstg")
                b.vec.inc(v.tensor_copy(vstg[:, :], ps_ys[0][:, :]))
            b.sec("vector", _)
            v_n = b.vec.n
            b.gate_set("ab_psy0", b.vec)

            def _(t, v_n=v_n):
                t.wait_ge(b.vec.h, v_n)
                b.gate_wait(t, "ab_psvb")
                for j in range(4):
                    ins = t.transpose(psvb[:, 128 * j:128 * (j + 1)],
                                      vstg[:, 128 * j:128 * (j + 1)], C["id_bf"][:, :])
                b.ten.inc(ins)
            b.sec("tensor", _)
            t_n = b.ten.n
            b.gate_set("ab_vstg", b.ten)

            def _(v, sc=sc, VT=VT, t_n=t_n):
                v.wait_ge(b.ten.h, t_n)
                b.vec.inc(v.tensor_copy(VT[:, 512 * sc:512 * (sc + 1)], psvb[:, :]))
            b.sec("vector", _)
            b.gate_set("ab_psvb", b.vec)
        qkv_v = b.vec.n
        b.gate_set("ab_w0", b.ten)

        NMh = NM2[par]
        for prt, buf in ((0, KT), (1, QT)):
            for sc in range(4):
                def _(s, prt=prt, buf=buf, sc=sc):
                    if prt == 0 and sc == 0:
                        s.wait_ge(b.vec.h, qkv_v)
                    b.gate_wait(s, "ab_q2b")
                    b.act.inc(s.activation(Q2B[:, :], buf[:, 512 * sc:512 * (sc + 1)],
                                           AF.Square))
                b.sec("scalar", _)
                a_n = b.act.n

                def _(t, a_n=a_n):
                    t.wait_ge(b.act.h, a_n)
                    b.gate_wait(t, "ab_psq")
                    b.ten.inc(t.matmul(ps_q[0:1, 0:512], C["onescol_bf"][:, :],
                                       Q2B[:, :], start=True, stop=True))
                b.sec("tensor", _)
                t_n = b.ten.n
                b.gate_set("ab_q2b", b.ten)

                def _(s, prt=prt, sc=sc, t_n=t_n):
                    s.wait_ge(b.ten.h, t_n)
                    b.ssp(s)
                    b.gate_wait(s, "ab_row4")
                    if prt == 0:
                        ins = s.activation(row4[:, :], ps_q[0:1, 0:512], AF.Identity)
                    else:
                        ins = s.activation(row4[:, :], ps_q[0:1, 0:512], AF.Sqrt)
                    b.act.inc(ins)
                b.sec("scalar", _)
                a_n = b.act.n
                b.gate_set("ab_psq", b.act)

                if prt == 0:
                    def _(v, sc=sc, a_n=a_n):
                        v.wait_ge(b.act.h, a_n)
                        b.vsp(v)
                        b.vec.inc(v.reduce_max(KNC[:, sc:sc + 1], row4[:, :], axis=AX.X))
                    b.sec("vector", _)
                    b.gate_set("ab_row4", b.vec)
                else:
                    def _(v, sc=sc, a_n=a_n):
                        v.wait_ge(b.act.h, a_n)
                        b.vsp(v)
                        b.vec.inc(v.tensor_scalar(row4[:, :], row4[:, :], KNM[0:1, 0:1],
                                                  None, ALU.mult))
                    b.sec("vector", _)
                    v_n = b.vec.n

                    def _(t, sc=sc, v_n=v_n):
                        t.wait_ge(b.vec.h, v_n)
                        b.gate_wait(t, "ab_psq")
                        for j in range(4):
                            ins = t.transpose(ps_q[:, 16 + 4 * sc + j:17 + 4 * sc + j],
                                              row4[0:1, 128 * j:128 * (j + 1)],
                                              C["id_f32"][0:1, 0:1])
                        b.ten.inc(ins)
                    b.sec("tensor", _)
                    b.gate_set("ab_row4", b.ten)
            if prt == 0:
                def _(v):
                    b.vsp(v)
                    b.vec.inc(v.reduce_max(KNC[:, 4:5], KNC[:, 0:4], axis=AX.X))
                b.sec("vector", _)
                v_n = b.vec.n

                def _(s, v_n=v_n):
                    s.wait_ge(b.vec.h, v_n)
                    b.ssp(s)
                    b.act.inc(s.activation(KNM[:, :], KNC[:, 4:5], AF.Sqrt))
                b.sec("scalar", _)
        t_n = b.ten.n

        def _(v, NMh=NMh, t_n=t_n, th_n=th_n):
            v.wait_ge(b.ten.h, t_n)
            v.wait_ge(b.syd.h, th_n)
            v.tensor_scalar(NMh[:, :], ps_q[:, 16:32], -1.05, None, ALU.mult)
            b.vsp(v)
            b.vec.inc(v.tensor_tensor(NMh[:, :], NMh[:, :], NMSH[:, :], ALU.subtract))
        b.sec("vector", _)
        nm_v = b.vec.n
        b.gate_set("ab_nmsh", b.vec)
        b.gate_set("ab_psq", b.vec)

        def scores_part(qt):
            par = qt % 2
            SS_ = SS2[par]
            K_ = 128 * (qt + 1)
            fs = max(0, K_ - 512)
            chunks = [(i, min(i + 512, K_ - 512)) for i in range(0, K_ - 512, 512)]
            chunks.append((fs, K_))
            for ci, (c0, c1) in enumerate(chunks):
                pss = ps_s[ci % 2]

                def _(t, qt=qt, c0=c0, c1=c1, pss=pss, ci=ci):
                    if qt == 0 and ci == 0:
                        t.wait_ge(b.vec.h, qkv_v)
                        t.wait_ge(b.syd.h, th_n)
                    b.gate_wait(t, f"ab_s{ci % 2}")
                    b.ten.inc(t.matmul(pss[:, 0:c1 - c0], QT[:, 128 * qt:128 * (qt + 1)],
                                       KT[:, c0:c1], start=True, stop=True))
                b.sec("tensor", _)
                t_n = b.ten.n
                final = (ci == len(chunks) - 1)

                def _(v, c0=c0, c1=c1, pss=pss, ci=ci, t_n=t_n, final=final,
                      SS_=SS_, par=par):
                    v.wait_ge(b.ten.h, t_n)
                    if ci == 0:
                        b.gate_wait(v, f"ab_ssr{par}")
                    ins = v.tensor_tensor(SS_[:, c0:c1], pss[:, 0:c1 - c0], TH[:, c0:c1],
                                          ALU.add)
                    if final:
                        b.vsp(v)
                        ins = v.tensor_tensor(SS_[:, c1 - 128:c1], SS_[:, c1 - 128:c1],
                                              C["mask_su"][:, :], ALU.add)
                    b.vec.inc(ins)
                b.sec("vector", _)
                b.gate_set(f"ab_s{ci % 2}", b.vec)
            return len(chunks)

        def tail_part(qt, nch):
            par = qt % 2
            SS_, PP_ = SS2[par], PP2[par]
            DEN_, RR_, DG_ = DEN2[par], RR2[par], DG2[par]
            K_ = 128 * (qt + 1)
            v_n = b.vec.n

            def _(s, qt=qt, K_=K_, v_n=v_n, par=par, SS_=SS_, PP_=PP_, DEN_=DEN_,
                  NMh=NMh):
                s.wait_ge(b.vec.h, v_n)
                b.gate_wait(s, f"ab_pp{par}")
                b.act.inc(s.activation(PP_[:, 0:K_], SS_[:, 0:K_], AF.Exp,
                                       bias=NMh[:, qt:qt + 1], accum_out=DEN_[:, :]))
                b.gate_set(f"ab_ssr{par}", b.act)
            b.sec("scalar", _)
            a_n = b.act.n

            def _(v, a_n=a_n, DEN_=DEN_, RR_=RR_, DG_=DG_):
                v.wait_ge(b.act.h, a_n)
                b.vsp(v)
                v.reciprocal(RR_[:, :], DEN_[:, :])
                b.vsp(v)
                b.vec.inc(v.tensor_scalar(DG_[:, :], C["id_bf"][:, :], RR_[:, :],
                                          None, ALU.mult))
            b.sec("vector", _)

        def pv(qt):
            par = qt % 2
            PP_, DG_, PT_ = PP2[par], DG2[par], PT2[par]
            psy = ps_ys[par]
            dg_v = b.vec.n
            nkt = qt + 1
            for tg in range(ceil_div(nkt, 4)):
                ptb = ps_pts[tg % 2]
                pname = f"ab_pt{tg % 2}"

                def _(t, qt=qt, tg=tg, nkt=nkt, PP_=PP_, DG_=DG_, ptb=ptb,
                      pname=pname, dg_v=dg_v):
                    t.wait_ge(b.vec.h, dg_v)
                    b.gate_wait(t, pname)
                    for j in range(min(4, nkt - 4 * tg)):
                        kt = 4 * tg + j
                        ins = t.matmul(ptb[:, 128 * j:128 * (j + 1)],
                                       PP_[:, 128 * kt:128 * (kt + 1)], DG_[:, :],
                                       start=True, stop=True)
                    b.ten.inc(ins)
                b.sec("tensor", _)
                t_n = b.ten.n
                if tg == ceil_div(nkt, 4) - 1:
                    b.gate_set(f"ab_pp{par}", b.ten)

                def _(s, tg=tg, nkt=nkt, PT_=PT_, ptb=ptb, t_n=t_n):
                    s.wait_ge(b.ten.h, t_n)
                    nj = min(4, nkt - 4 * tg)
                    b.act.inc(s.activation(PT_[:, 512 * tg:512 * tg + 128 * nj],
                                           ptb[:, 0:128 * nj], AF.Identity))
                b.sec("scalar", _)
                b.gate_set(pname, b.act)
            pt_a = b.act.n

            def _(t, qt=qt, nkt=nkt, PT_=PT_, psy=psy, par=par, pt_a=pt_a):
                t.wait_ge(b.act.h, pt_a)
                b.gate_wait(t, f"ab_psy{par}")
                for kt in range(nkt):
                    ins = t.matmul(psy[:, 0:128], VT[:, 128 * kt:128 * (kt + 1)],
                                   PT_[:, 128 * kt:128 * (kt + 1)],
                                   start=(kt == 0), stop=(kt == nkt - 1))
                b.ten.inc(ins)
            b.sec("tensor", _)
            t_n = b.ten.n

            def _(s, h=h, qt=qt, psy=psy, par=par, t_n=t_n):
                s.wait_ge(b.ten.h, t_n)
                b.act.inc(s.activation(YT[:, h * S + 128 * qt:h * S + 128 * (qt + 1)],
                                       psy[:, 0:128], AF.Identity))
            b.sec("scalar", _)
            b.gate_set(f"ab_psy{par}", b.act)

        nchs = {}
        for qt in range(17):
            if qt < 16:
                nchs[qt] = scores_part(qt)
            if qt >= 1:
                pv(qt - 1)
            if qt < 16:
                tail_part(qt, nchs[qt])
        b.gate_set("ab_th0", b.vec)
    b.barrier()
    if DEBUG_TAPS:
        def _(sy):
            b.syd.inc(sy.dma_start(io["dbg_hp"][:, 0:16 * S], YT[:, :]), 16)
        b.sec("sync", _)
        def _(g):
            b.gpd.inc(g.dma_start(io["dbg_hs"][:, 0:2048].bitcast(F32)[:, 0:1024]
                                  if False else io["dbg_q"][:, 0:2048], SS2[0][:, :]), 16)
            b.gpd.inc(g.dma_start(io["dbg_q"][:, 2048:4096], SS2[1][:, :]), 16)
            b.gpd.inc(g.dma_start(io["dbg_k"][:, 0:2048], PP2[0][:, :]), 16)
            b.gpd.inc(g.dma_start(io["dbg_k"][:, 2048:4096], PP2[1][:, :]), 16)
            b.gpd.inc(g.dma_start(io["dbg_ba"][:, 0:16], NM2[0][:, :]), 16)
            b.gpd.inc(g.dma_start(io["dbg_ba"][:, 16:32], NM2[1][:, :]), 16)
            b.gpd.inc(g.dma_start(io["dbg_ba"][:, 32:33], DEN2[0][:, :], allow_slow_non_contiguous=True), 16)
            b.gpd.inc(g.dma_start(io["dbg_ba"][:, 33:34], DEN2[1][:, :], allow_slow_non_contiguous=True), 16)
            b.gpd.inc(g.dma_start(io["dbg_ba"][:, 34:35], RR2[0][:, :], allow_slow_non_contiguous=True), 16)
            b.gpd.inc(g.dma_start(io["dbg_ba"][:, 35:36], RR2[1][:, :], allow_slow_non_contiguous=True), 16)
        b.sec("gpsimd", _)
        b.barrier()
    at_st.close()

    b.mark("ab.out")
    op_st = ExitStack()
    wo = [b.sb(f"bo_wo{i}", [128, 16 * 128], BF16, op_st) for i in range(2)]
    osb = [b.sb(f"bo_osb{i}", [128, 512], F32, op_st) for i in range(2)]
    rsb = [b.sb(f"bo_rsb{i}", [128, 512], F32, op_st) for i in range(2)]
    ot = [b.sb(f"bo_ot{i}", [128, 512], F32, op_st) for i in range(2)]
    ps_o = [b.ps(f"bo_pso{i}", [128, 512], F32, op_st) for i in range(2)]
    ps_t = [b.ps(f"bo_pst{i}", [128, 512], F32, op_st) for i in range(2)]
    for dt_ in range(16):
        wb = wo[dt_ % 2]
        wname = f"bow{dt_ % 2}"

        def _(g, dt_=dt_, wb=wb, wname=wname):
            b.gate_wait(g, wname)
            src3 = io["Wo_b"][:, 128 * dt_:128 * (dt_ + 1)].rearrange(
                "(k p) c -> p k c", p=128)
            b.gpd.inc(g.dma_start(wb[:, :].rearrange("p (k c) -> p k c", k=16), src3), 16)
        b.sec("gpsimd", _)
        g_n = b.gpd.n

        for sc in range(4):
            soff = 512 * sc
            rb = rsb[sc % 2]

            def _(sy, dt_=dt_, sc=sc, soff=soff, rb=rb):
                b.gate_wait(sy, f"bors{sc % 2}")
                src3 = src_res[soff:soff + 512, 128 * dt_:128 * (dt_ + 1)].rearrange(
                    "(j p) c -> p j c", p=128)
                b.syd.inc(sy.dma_start(rb[:, :].rearrange("p (j c) -> p j c", j=4), src3), 16)
            b.sec("sync", _)
            r_n = b.syd.n

            par2 = sc % 2
            po = ps_o[par2]
            pt = ps_t[par2]
            ob = osb[par2]
            otb = ot[par2]

            def _(t, dt_=dt_, sc=sc, wb=wb, soff=soff, po=po, par2=par2, g_n=g_n):
                t.wait_ge(b.gpd.h, g_n)
                b.gate_wait(t, f"bo_pso{par2}")
                for k in range(16):
                    ins = t.matmul(po[:, :], wb[:, 128 * k:128 * (k + 1)],
                                   YT[:, k * S + soff:k * S + soff + 512],
                                   start=(k == 0), stop=(k == 15))
                b.ten.inc(ins)
            b.sec("tensor", _)
            t_n = b.ten.n
            if sc == 3:
                b.gate_set(wname, b.ten)

            def _(v, po=po, ob=ob, par2=par2, t_n=t_n):
                v.wait_ge(b.ten.h, t_n)
                b.gate_wait(v, f"bo_osb{par2}")
                b.vec.inc(v.tensor_copy(ob[:, :], po[:, :]))
            b.sec("vector", _)
            v_n = b.vec.n
            b.gate_set(f"bo_pso{par2}", b.vec)

            def _(t, ob=ob, pt=pt, par2=par2, v_n=v_n):
                t.wait_ge(b.vec.h, v_n)
                b.gate_wait(t, f"bo_pst{par2}")
                for j in range(4):
                    ins = t.transpose(pt[:, 128 * j:128 * (j + 1)],
                                      ob[:, 128 * j:128 * (j + 1)], C["id_f32"][:, :])
                b.ten.inc(ins)
            b.sec("tensor", _)
            t_n = b.ten.n
            b.gate_set(f"bo_osb{par2}", b.ten)

            def _(v, pt=pt, otb=otb, par2=par2, t_n=t_n, r_n=r_n):
                v.wait_ge(b.ten.h, t_n)
                v.wait_ge(b.syd.h, r_n)
                b.gate_wait(v, f"bo_ot{par2}")
                b.vec.inc(v.tensor_tensor(otb[:, :], pt[:, :], rb[:, :], ALU.add))
            b.sec("vector", _)
            v_n = b.vec.n
            b.gate_set(f"bo_pst{par2}", b.vec)
            b.gate_set(f"bors{sc % 2}", b.vec)

            def _(sy, dt_=dt_, sc=sc, soff=soff, otb=otb, v_n=v_n):
                sy.wait_ge(b.vec.h, v_n)
                dst3 = dst_res[soff:soff + 512, 128 * dt_:128 * (dt_ + 1)].rearrange(
                    "(j p) c -> p j c", p=128)
                b.syd.inc(sy.dma_start(dst3, otb[:, :].rearrange("p (j c) -> p j c", j=4)), 16)
            b.sec("sync", _)
            b.gate_set(f"bo_ot{par2}", b.syd)
    b.barrier()
    op_st.close()
    sa.close()


def build_program(debug_taps=False, stage=9, layers=("a1", "f1", "ab", "f2")):
    global DEBUG_TAPS, STAGE
    DEBUG_TAPS = debug_taps
    STAGE = stage
    nc = bass.Bass("TRN2", target_bir_lowering=False)
    io = declare_io(nc, debug_taps)
    with ExitStack() as st:
        block = st.enter_context(nc.Block())
        b = B(nc, block, st)
        load_consts(b, io)
        if "a1" in layers:
            wp_st = ExitStack()
            b.mark("wprep")
            emit_wprep_attn_a(b, io, wp_st)
            b.barrier()
            wp_st.close()
            if STAGE >= 1:
                emit_attn_a(b, io)
        if "f1" in layers:
            emit_ffn(b, io, io["res"], io["res"], "f1")
        if "ab" in layers:
            emit_attn_b(b, io, io["res"], io["res"])
        if "f2" in layers:
            emit_ffn(b, io, io["res"], io["out"], "f2")
        b.barrier()
        if "f2" not in layers:
            cp_st = ExitStack()
            cp = b.sb("cp", [128, D], F32, cp_st)
            csrc = io["res"] if (STAGE >= 6 and ("a1" in layers or "f1" in layers)) \
                else io["x_in"]
            for it in range(16):
                def _(sy, it=it):
                    if it == 0:
                        sy.wait_ge(b.syd.h, b.syd.n)
                    b.syd.inc(sy.dma_start(cp[:, :], csrc[128 * it:128 * (it + 1), :]), 16)
                    sy.wait_ge(b.syd.h, b.syd.n)
                    b.syd.inc(sy.dma_start(io["out"][128 * it:128 * (it + 1), :], cp[:, :]),
                              16)
                    sy.wait_ge(b.syd.h, b.syd.n)
                b.sec("sync", _)
            cp_st.close()
        def _(sy):
            sy.wait_ge(b.syd.h, b.syd.n)
            sy.wait_ge(b.gpd.h, b.gpd.n)
        b.sec("sync", _)
    nc._phase_marks = b.marks
    return nc, io


def make_in_maps(inputs, consts):
    x = inputs["L0_attn_a_x"]
    base = {
        "w_qkv": inputs["L0_attn_a_w_qkv"], "w_gate": inputs["L0_attn_a_w_gate"],
        "w_beta": inputs["L0_attn_a_w_beta"], "w_alpha": inputs["L0_attn_a_w_alpha"],
        "log_A": np.ascontiguousarray(inputs["L0_attn_a_log_A"].reshape(1, H_V)),
        "dt_bias": np.ascontiguousarray(inputs["L0_attn_a_dt_bias"].reshape(1, H_V)),
        "conv_w": np.ascontiguousarray(inputs["L0_attn_a_conv_w"].reshape(FUSED, 4)),
        "rms_w": np.ascontiguousarray(inputs["L0_attn_a_rms_w"].reshape(1, DV)),
        "w_out": inputs["L0_attn_a_w_out"],
        "Wqkv_b": inputs["L0_attn_b_W_qkv"], "Wo_b": inputs["L0_attn_b_W_o"],
        "Wg": inputs["L0_ffn_Wg"],
        "bg": np.ascontiguousarray(inputs["L0_ffn_bg"].reshape(1, NE)),
        "Wgate": inputs["L0_ffn_Wgate"], "bgate": inputs["L0_ffn_bgate"],
        "Wup": inputs["L0_ffn_Wup"], "bup": inputs["L0_ffn_bup"],
        "Wdown": inputs["L0_ffn_Wdown"], "bdown": inputs["L0_ffn_bdown"],
        "Ws_up": inputs["L0_ffn_Ws_up"],
        "bs_up": np.ascontiguousarray(inputs["L0_ffn_bs_up"].reshape(1, D)),
        "Ws_down": inputs["L0_ffn_Ws_down"],
        "bs_down": np.ascontiguousarray(inputs["L0_ffn_bs_down"].reshape(1, D)),
    }
    import ml_dtypes
    cc = {}
    for k, v in consts.items():
        dt = CONST_SHAPES[k][1]
        cc[k] = v.astype(ml_dtypes.bfloat16) if dt == BF16 else v
    maps = []
    for c in range(8):
        m = dict(base)
        m.update(cc)
        m["x_in"] = np.ascontiguousarray(x[c])
        maps.append(m)
    return maps


def kernel(**inputs):
    nc, io = build_program()
    maps = make_in_maps(inputs, host_constants())
    res = run_bass_kernel_spmd(nc, maps, core_ids=list(range(8)))
    out = np.stack([r["out"] for r in res.results], axis=0)
    return out


# version 07
# speedup vs baseline: 1.2130x; 1.2130x over previous
import numpy as np
import math
from contextlib import ExitStack

import concourse.bass as bass
import concourse.mybir as mybir
from concourse.bass_utils import run_bass_kernel_spmd

F32 = mybir.dt.float32
BF16 = mybir.dt.bfloat16
AF = mybir.ActivationFunctionType
ALU = mybir.AluOpType
AX = mybir.AxisListType

S = 2048
D = 2048
SH = S // 2
NEG = -1e30
LN_EPS = 1e-5
A_EPS = 1e-6

H_QK, H_V, DK, DV = 8, 16, 128, 128
FUSED = 4096
NE, NR, HF = 8, 6, 1024
NH_B, HD_B = 16, 128

DEBUG_TAPS = False
STAGE = 9


def ceil_div(a, c):
    return (a + c - 1) // c


class Eng:
    def __init__(self, nc, st, name):
        self.h = st.enter_context(nc.semaphore(name))
        self.n = 0

    def inc(self, ins, k=1):
        ins.then_inc(self.h, k)
        self.n += k
        return self.n


class B:
    def __init__(self, nc, block, st):
        self.nc = nc
        self.block = block
        self.st = st
        self.ten = Eng(nc, st, "s_ten")
        self.vec = Eng(nc, st, "s_vec")
        self.act = Eng(nc, st, "s_act")
        self.gpd = Eng(nc, st, "s_gpd")
        self.syd = Eng(nc, st, "s_syd")
        self.acd = Eng(nc, st, "s_acd")
        self.consts = {}
        self.bufs = {}
        self.gates = {}
        self.marks = []

    def mark(self, name):
        try:
            self.marks.append((name, self.nc.n_instructions()))
        except Exception:
            pass

    def sec(self, eng, fn):
        getattr(self.block, eng)(fn)

    _uid = 0

    def _u(self, name):
        B._uid += 1
        return f"{name}_{B._uid}"

    def sb(self, name, shape, dtype, stack=None):
        return (stack or self.st).enter_context(
            self.nc.sbuf_tensor(self._u(name), shape, dtype))

    def ps(self, name, shape, dtype, stack=None):
        return (stack or self.st).enter_context(
            self.nc.psum_tensor(self._u(name), shape, dtype))

    def gate_wait(self, e, name):
        g = self.gates.get(name)
        if g is not None:
            e.wait_ge(g[0].h, g[1])

    def gate_set(self, name, eng):
        self.gates[name] = (eng, eng.n)

    def vsp(self, v):
        v.tensor_copy(self.consts["spacer"][:, 0:32], self.consts["spacer"][:, 32:64])

    def ssp(self, s):
        s.activation(self.consts["spacer"][:, 64:96], self.consts["spacer"][:, 96:128],
                     AF.Abs)

    def barrier(self):
        engs = [self.ten, self.vec, self.act, self.gpd, self.syd, self.acd]
        snap = [(e, e.n) for e in engs]
        for sec_name in ("tensor", "vector", "scalar", "gpsimd", "sync"):
            def _(e, snap=snap):
                for (eng, n) in snap:
                    if n:
                        e.wait_ge(eng.h, n)
            self.sec(sec_name, _)


def host_constants():
    c = {}
    c["c_id_bf"] = np.eye(128, dtype=np.float32)
    c["c_id_f32"] = np.eye(128, dtype=np.float32)
    c["c_ones_bf"] = np.ones((1, 128), dtype=np.float32)
    c["c_ones_f32"] = np.ones((1, 128), dtype=np.float32)
    c["c_onescol_bf"] = np.ones((128, 1), dtype=np.float32)
    c["c_onesrow_bf"] = np.ones((1, 512), dtype=np.float32)
    p = np.arange(128, dtype=np.float32)[:, None]
    f = np.arange(128, dtype=np.float32)[None, :]
    c["c_mask_sl"] = np.where(f < p, 0.0, NEG).astype(np.float32)
    c["c_mask_iu"] = np.where(p <= f, 0.0, NEG).astype(np.float32)
    c["c_mask_su"] = np.where(f > p, NEG, 0.0).astype(np.float32)
    slopes = 2.0 ** (-8.0 / np.arange(1, NH_B + 1, dtype=np.float32))
    Fg = np.arange(S, dtype=np.float32)[None, :]
    c["c_alibi_t"] = (slopes[:, None, None] * (Fg - p)[None]).astype(np.float32)
    fd = np.arange(512, dtype=np.float32)[None, :]
    rel = fd - 384.0 - p
    c["c_alibi_d"] = np.where(rel[None] > 0, NEG,
                              slopes[:, None, None] * rel[None]).astype(np.float32)
    return c


CONST_SHAPES = {
    "c_id_bf": ((128, 128), BF16), "c_id_f32": ((128, 128), F32),
    "c_ones_bf": ((1, 128), BF16), "c_ones_f32": ((1, 128), F32),
    "c_onescol_bf": ((128, 1), BF16), "c_onesrow_bf": ((1, 512), BF16),
    "c_mask_sl": ((128, 128), F32),
    "c_mask_iu": ((128, 128), F32), "c_mask_su": ((128, 128), F32),
    "c_alibi_t": ((NH_B, 128, S), F32),
    "c_alibi_d": ((NH_B, 128, 512), F32),
}


def declare_io(nc, debug_taps=False):
    io = {}
    def din(name, shape, dtype=F32):
        io[name] = nc.dram_tensor(name, list(shape), dtype, kind="ExternalInput").ap()
    din("x_in", (S, D))
    din("w_qkv", (FUSED, D)); din("w_gate", (D, D)); din("w_beta", (H_V, D))
    din("w_alpha", (H_V, D)); din("log_A", (1, H_V)); din("dt_bias", (1, H_V))
    din("conv_w", (FUSED, 4)); din("rms_w", (1, DV)); din("w_out", (D, D))
    din("Wqkv_b", (D, 3 * D)); din("Wo_b", (D, D))
    din("Wg", (D, NE)); din("bg", (1, NE))
    din("Wgate", (NR, D, HF)); din("bgate", (NR, HF))
    din("Wup", (NR, D, HF)); din("bup", (NR, HF))
    din("Wdown", (NR, HF, D)); din("bdown", (NR, D))
    din("Ws_up", (D, D)); din("bs_up", (1, D))
    din("Ws_down", (D, D)); din("bs_down", (1, D))
    for name, (shp, dt) in CONST_SHAPES.items():
        io[name] = nc.dram_tensor(name, list(shp), dt, kind="ExternalInput").ap()
    io["out"] = nc.dram_tensor("out", [S, D], F32, kind="ExternalOutput").ap()
    io["res"] = nc.dram_tensor("res", [S, D], F32, kind="Internal").ap()
    io["wqkvT"] = nc.dram_tensor("wqkvT", [D, FUSED], BF16, kind="Internal").ap()
    io["wgateT"] = nc.dram_tensor("wgateT", [D, D], BF16, kind="Internal").ap()
    io["woutT"] = nc.dram_tensor("woutT", [D, D], BF16, kind="Internal").ap()
    io["rowscr"] = nc.dram_tensor("rowscr", [64, S], F32, kind="Internal").ap()
    if debug_taps:
        io["dbg_hp"] = nc.dram_tensor("dbg_hp", [128, NR * 8 * SH], BF16,
                                      kind="ExternalOutput").ap()
        io["dbg_hs"] = nc.dram_tensor("dbg_hs", [128, 16 * SH], BF16,
                                      kind="ExternalOutput").ap()
        io["dbg_q"] = nc.dram_tensor("dbg_q", [128, 8 * SH], F32, kind="ExternalOutput").ap()
        io["dbg_k"] = nc.dram_tensor("dbg_k", [128, 8 * SH], F32, kind="ExternalOutput").ap()
        io["dbg_v"] = nc.dram_tensor("dbg_v", [128, 16 * SH], F32, kind="ExternalOutput").ap()
        io["dbg_ba"] = nc.dram_tensor("dbg_ba", [128, 2 * 128], F32, kind="ExternalOutput").ap()
        io["dbg_ctx"] = nc.dram_tensor("dbg_ctx", [128, 16 * SH], F32, kind="ExternalOutput").ap()
    return io


def load_consts(b, io):
    C = b.consts
    C["id_bf"] = b.sb("id_bf", [128, 128], BF16)
    C["id_f32"] = b.sb("id_f32", [128, 128], F32)
    C["ones_bf"] = b.sb("ones_bf", [1, 128], BF16)
    C["ones_f32"] = b.sb("ones_f32", [1, 128], F32)
    C["onescol_bf"] = b.sb("onescol_bf", [128, 1], BF16)
    C["onesrow_bf"] = b.sb("onesrow_bf", [1, 512], BF16)
    C["mask_sl"] = b.sb("mask_sl", [128, 128], F32)
    C["mask_iu"] = b.sb("mask_iu", [128, 128], F32)
    C["mask_su"] = b.sb("mask_su", [128, 128], F32)
    C["rmsw_bc"] = b.sb("rmsw_bc", [128, DV], F32)
    C["dtb_bc"] = b.sb("dtb_bc", [128, H_V], F32)
    C["negA_bc"] = b.sb("negA_bc", [128, H_V], F32)
    C["bg_bc"] = b.sb("bg_bc", [128, NE], F32)
    C["spacer"] = b.sb("spacer", [128, 128], F32)

    def _(g):
        loads = [(C["id_bf"], io["c_id_bf"]), (C["id_f32"], io["c_id_f32"]),
                 (C["ones_bf"], io["c_ones_bf"]), (C["ones_f32"], io["c_ones_f32"]),
                 (C["onescol_bf"], io["c_onescol_bf"]),
                 (C["onesrow_bf"], io["c_onesrow_bf"]), (C["mask_sl"], io["c_mask_sl"]),
                 (C["mask_iu"], io["c_mask_iu"]), (C["mask_su"], io["c_mask_su"])]
        for dst, src in loads:
            b.gpd.inc(g.dma_start(dst[:, :], src), 16)
        b.gpd.inc(g.dma_start(C["rmsw_bc"][:, :], io["rms_w"].to_broadcast([128, DV])), 16)
        b.gpd.inc(g.dma_start(C["dtb_bc"][:, :], io["dt_bias"].to_broadcast([128, H_V])), 16)
        b.gpd.inc(g.dma_start(C["negA_bc"][:, :], io["log_A"].to_broadcast([128, H_V])), 16)
        b.gpd.inc(g.dma_start(C["bg_bc"][:, :], io["bg"].to_broadcast([128, NE])), 16)
    b.sec("gpsimd", _)
    gn = b.gpd.n

    CVALS = [LN_EPS, A_EPS, 1.0, -0.5 * math.log(128.0), 0.0]
    C["cvals"] = b.sb("cvals", [128, len(CVALS)], F32)

    def _(v):
        v.memset(C["spacer"][:, :], 0.0)
        for i, val in enumerate(CVALS):
            ins = v.memset(C["cvals"][:, i:i + 1], val)
        b.vec.inc(ins)
    b.sec("vector", _)
    for i, val in enumerate(CVALS):
        b.nc.const_aps.register(F32, val, C["cvals"][:, i:i + 1])

    def _(s):
        s.wait_ge(b.gpd.h, gn)
        s.wait_ge(b.vec.h, b.vec.n)
        b.act.inc(s.activation(C["negA_bc"][:, :], C["negA_bc"][:, :], AF.Exp))
    b.sec("scalar", _)
    an = b.act.n

    def _(v):
        v.wait_ge(b.act.h, an)
        b.vec.inc(v.tensor_scalar(C["negA_bc"][:, :], C["negA_bc"][:, :], -1.0, None, ALU.mult))
    b.sec("vector", _)


def emit_wprep_attn_a(b, io, st):
    b.bufs["wbaT"] = b.sb("wbaT", [128, 16 * 32], BF16)
    wsm = b.sb("pp_wsm", [16, 2 * D], BF16, st)
    psb = b.ps("pp_psb", [128, 1024], BF16, st)
    stage = b.sb("pp_stage", [128, 1024], BF16, st)
    wstage = [b.sb(f"pp_w{i}", [128, D], BF16, st) for i in range(2)]

    def _(g):
        b.gpd.inc(g.dma_start(wsm[:, 0:D], io["w_beta"]), 16)
        b.gpd.inc(g.dma_start(wsm[:, D:2 * D], io["w_alpha"]), 16)
    b.sec("gpsimd", _)
    gn = b.gpd.n

    def _(t):
        t.wait_ge(b.gpd.h, gn)
        for kt in range(16):
            t.transpose(psb[:, 32 * kt:32 * kt + 16], wsm[:, 128 * kt:128 * (kt + 1)],
                        b.consts["id_bf"][0:16, 0:16])
            ins = t.transpose(psb[:, 32 * kt + 16:32 * kt + 32],
                              wsm[:, D + 128 * kt:D + 128 * (kt + 1)],
                              b.consts["id_bf"][0:16, 0:16])
        b.ten.inc(ins)
    b.sec("tensor", _)
    tn = b.ten.n

    def _(v):
        v.wait_ge(b.ten.h, tn)
        b.vec.inc(v.tensor_copy(b.bufs["wbaT"][:, :], psb[:, 0:512]))
    b.sec("vector", _)

    jobs = [(io["w_qkv"], io["wqkvT"], 32), (io["w_gate"], io["wgateT"], 16),
            (io["w_out"], io["woutT"], 16)]
    for (src, dst, nrt) in jobs:
        for rt in range(nrt):
            buf = wstage[rt % 2]
            bname = f"ppw{rt % 2}"

            def _(g, buf=buf, src=src, rt=rt, bname=bname):
                b.gate_wait(g, bname)
                b.gpd.inc(g.dma_start(buf[:, :], src[128 * rt:128 * (rt + 1), :]), 16)
            b.sec("gpsimd", _)
            g_n = b.gpd.n

            for half in range(2):
                def _(t, buf=buf, half=half, g_n=g_n):
                    t.wait_ge(b.gpd.h, g_n)
                    t.wait_ge(b.vec.h, b.vec.n)
                    for j in range(8):
                        dt_ = half * 8 + j
                        ins = t.transpose(psb[:, 128 * j:128 * (j + 1)],
                                          buf[:, 128 * dt_:128 * (dt_ + 1)],
                                          b.consts["id_bf"][:, :])
                    b.ten.inc(ins)
                b.sec("tensor", _)
                t_n = b.ten.n

                def _(v, t_n=t_n):
                    v.wait_ge(b.ten.h, t_n)
                    v.wait_ge(b.syd.h, b.syd.n)
                    b.vec.inc(v.tensor_copy(stage[:, :], psb[:, :]))
                b.sec("vector", _)
                v_n = b.vec.n

                def _(sy, dst=dst, rt=rt, half=half, v_n=v_n):
                    sy.wait_ge(b.vec.h, v_n)
                    dst3 = dst[:, 128 * rt:128 * (rt + 1)].rearrange(
                        "(j p) c -> p j c", p=128)[:, 8 * half:8 * half + 8, :]
                    b.syd.inc(sy.dma_start(
                        dst3, stage[:, :].rearrange("p (j c) -> p j c", j=8)), 16)
                b.sec("sync", _)
            b.gate_set(bname, b.ten)


def emit_ln(b, io, src_ap, s0, n_tiles, xlnt, xlnt_cols, st, want_lo=None,
            tile_cb=None):
    nxs = 1 if want_lo is not None else 2
    xs = [b.sb(f"ln_xs{i}", [128, D], F32, st) for i in range(nxs)]
    xf = b.sb("ln_xf", [128, D], F32, st) if want_lo is not None else None
    xb = b.sb("ln_xb", [128, D], BF16, st)
    xlo = b.sb("ln_xlo", [128, D], BF16, st) if want_lo is not None else None
    sqd = xlo if want_lo is not None else b.sb("ln_sqd", [128, D], BF16, st)
    s1 = b.sb("ln_s1", [128, 1], F32, st)
    ssq = b.sb("ln_ssq", [128, 1], F32, st)
    mean = b.sb("ln_mean", [128, 1], F32, st)
    var = b.sb("ln_var", [128, 1], F32, st)
    rstd = b.sb("ln_rstd", [128, 1], F32, st)
    nmr = b.sb("ln_nmr", [128, 1], F32, st)
    negmean = b.sb("ln_negmean", [128, 1], F32, st)
    psb = b.ps("ln_psb", [128, 1024], BF16, st)

    for it in range(n_tiles):
        buf = xs[it % nxs]
        bname = f"lnx{it % nxs}"

        def _(sy, buf=buf, it=it, bname=bname):
            b.gate_wait(sy, bname)
            b.syd.inc(sy.dma_start(buf[:, :],
                                   src_ap[s0 + 128 * it:s0 + 128 * (it + 1), :]), 16)
        b.sec("sync", _)
        d_n = b.syd.n

        def _(v, buf=buf, d_n=d_n):
            v.wait_ge(b.syd.h, d_n)
            b.gate_wait(v, "ln_var")
            b.vec.inc(v.reduce_sum(s1[:, :], buf[:, :], axis=AX.X))
        b.sec("vector", _)
        v_n = b.vec.n

        def _(s, buf=buf, v_n=v_n):
            s.wait_ge(b.vec.h, v_n)
            s.activation(sqd[:, :], buf[:, :], AF.Square)
            b.act.inc(s.activation(var[:, :], s1[:, :], AF.Square, scale=1.0 / D))
            b.gate_set("ln_var", b.act)
        b.sec("scalar", _)
        a_n = b.act.n

        def _(v, a_n=a_n):
            v.wait_ge(b.act.h, a_n)
            v.reduce_sum(ssq[:, :], sqd[:, :], axis=AX.X)
            v.tensor_scalar(negmean[:, :], s1[:, :], -1.0 / D, None, ALU.mult)
            b.vec.inc(v.scalar_tensor_tensor(var[:, :], ssq[:, :], 1.0 / D, var[:, :],
                                             ALU.mult, ALU.subtract))
        b.sec("vector", _)
        v_n = b.vec.n

        def _(s, v_n=v_n):
            s.wait_ge(b.vec.h, v_n)
            b.gate_wait(s, "ln_nrm")
            s.activation(rstd[:, :], var[:, :], AF.Ln, bias=LN_EPS)
            b.ssp(s)
            b.act.inc(s.activation(rstd[:, :], rstd[:, :], AF.Exp, scale=-0.5))
        b.sec("scalar", _)
        a_n = b.act.n

        def _(v, buf=buf, bname=bname, a_n=a_n):
            v.wait_ge(b.act.h, a_n)
            b.gate_wait(v, "ln_xb")
            v.tensor_tensor(nmr[:, :], negmean[:, :], rstd[:, :], ALU.mult)
            b.vsp(v)
            if want_lo is None:
                ins = v.tensor_scalar(xb[:, :], buf[:, :], rstd[:, :], nmr[:, :],
                                      ALU.mult, ALU.add)
            else:
                v.tensor_scalar(xf[:, :], buf[:, :], rstd[:, :], nmr[:, :],
                                ALU.mult, ALU.add)
                v.tensor_copy(xb[:, :], xf[:, :])
                ins = v.tensor_tensor(xlo[:, :], xf[:, :], xb[:, :], ALU.subtract)
            b.vec.inc(ins)
            b.gate_set(bname, b.vec)
            b.gate_set("ln_nrm", b.vec)
            b.vsp(v)
            b.vec.inc(v.tensor_copy(b.consts["spacer"][:, 0:32],
                                    b.consts["spacer"][:, 32:64]))
        b.sec("vector", _)
        v_n = b.vec.n

        srcs = [(xb, xlnt, xlnt_cols, 128 * it)] + (
            [(xlo, want_lo, 128, 0)] if want_lo is not None else [])
        for si, (tm, ft, fcols, coff) in enumerate(srcs):
            for grp in range(2):
                def _(t, tm=tm, grp=grp, v_n=v_n, si=si):
                    t.wait_ge(b.vec.h, b.vec.n)
                    t.wait_ge(b.vec.h, v_n)
                    if si == 1 and grp == 0:
                        b.gate_wait(t, "ln_lo_free")
                    for j in range(8):
                        dt_ = grp * 8 + j
                        ins = t.transpose(psb[:, 128 * j:128 * (j + 1)],
                                          tm[:, 128 * dt_:128 * (dt_ + 1)],
                                          b.consts["id_bf"][:, :])
                    b.ten.inc(ins)
                b.sec("tensor", _)
                t_n = b.ten.n
                if si == len(srcs) - 1 and grp == 1:
                    b.gate_set("ln_xb", b.ten)

                def _(v, ft=ft, grp=grp, t_n=t_n, fcols=fcols, coff=coff):
                    v.wait_ge(b.ten.h, t_n)
                    dst = ft[:, :].rearrange("p (j c) -> p j c", c=fcols)[
                        :, 8 * grp:8 * grp + 8, coff:coff + 128]
                    b.vec.inc(v.tensor_copy(dst,
                                            psb[:, :].rearrange("p (j c) -> p j c", j=8)))
                b.sec("vector", _)
        if tile_cb is not None:
            tile_cb(it)
    for gname in ("ln_var", "ln_nrm", "ln_xb", "ln_lo_free", "lnx0", "lnx1"):
        b.gates.pop(gname, None)


def emit_attn_a(b, io):
    C = b.consts
    stage = STAGE
    sa = ExitStack()
    XLNT = b.sb("XLNT", [128, 16 * SH], BF16, sa)
    QH = b.sb("QH", [128, H_QK * SH], BF16, sa)
    KH = b.sb("KH", [128, H_QK * SH], BF16, sa)
    VTOK = b.sb("VTOK", [128, 8 * 2048], BF16, sa)
    CTX = b.sb("CTX", [128, H_V * SH], BF16, sa)
    Sf = b.sb("Sf", [128, H_V * DV], F32, sa)
    Sbf = b.sb("Sbf", [128, H_V * DV], BF16, sa)
    TAILS = b.sb("TAILS", [128, 32 * 3], F32, sa)
    BETA = b.sb("BETA", [128, 16 * 8], F32, sa)
    NBETA = b.sb("NBETA", [128, 16 * 8], F32, sa)
    LCT = b.sb("LCT", [128, 16 * 8], F32, sa)
    GT = b.sb("GT", [128, 16], F32, sa)
    NGT = b.sb("NGT", [128, 16], F32, sa)
    TET = b.sb("TET", [128, 16], F32, sa)
    laT = b.sb("laT", [16, 128], F32, sa)
    lcumT = b.sb("lcumT", [16, SH], F32, sa)
    ZR16 = b.sb("ZR16", [16, 128], F32, sa)

    def _(v):
        v.memset(Sf[:, :], 0.0)
        v.memset(Sbf[:, :], 0.0)
        v.memset(TAILS[:, :], 0.0)
        b.vec.inc(v.memset(ZR16[:, :], 0.0))
    b.sec("vector", _)

    for half in range(2):
        s0 = half * SH
        b.mark(f"a1.ln{half}")
        ln_st = ExitStack()
        emit_ln(b, io, io["x_in"], s0, 8, XLNT, SH, ln_st)
        b.barrier()
        ln_st.close()
        xlnt_ready = b.vec.n

        if stage < 2:
            continue
        b.mark(f"a1.ba{half}")
        ba_st = ExitStack()
        ps_ba = b.ps("ps_ba", [128, 512], F32, ba_st)
        ps_tr = b.ps("ps_tr", [128, 512], F32, ba_st)
        zz = b.sb("ba_z", [128, 16], F32, ba_st)
        az = b.sb("ba_az", [128, 16], F32, ba_st)
        sp = b.sb("ba_sp", [128, 16], F32, ba_st)
        for c in range(8):
            def _(t, c=c):
                if c == 0:
                    t.wait_ge(b.vec.h, xlnt_ready)
                else:
                    t.wait_ge(b.vec.h, b.vec.n)
                    t.wait_ge(b.act.h, b.act.n)
                for k in range(16):
                    ins = t.matmul(ps_ba[:, 0:32], XLNT[:, k * SH + 128 * c:k * SH + 128 * (c + 1)],
                                   b.bufs["wbaT"][:, 32 * k:32 * (k + 1)],
                                   start=(k == 0), stop=(k == 15))
                b.ten.inc(ins)
            b.sec("tensor", _)
            t_n = b.ten.n

            def _(s, c=c, t_n=t_n):
                s.wait_ge(b.ten.h, t_n)
                b.act.inc(s.activation(BETA[:, 16 * c:16 * (c + 1)], ps_ba[:, 0:16], AF.Sigmoid))
            b.sec("scalar", _)
            a_n = b.act.n

            def _(v, c=c, a_n=a_n):
                v.wait_ge(b.act.h, a_n)
                v.tensor_scalar(NBETA[:, 16 * c:16 * (c + 1)], BETA[:, 16 * c:16 * (c + 1)],
                                -1.0, None, ALU.mult)
                b.vec.inc(v.tensor_tensor(zz[:, :], ps_ba[:, 16:32], C["dtb_bc"][:, :], ALU.add))
            b.sec("vector", _)
            v_n = b.vec.n

            def _(s, v_n=v_n):
                s.wait_ge(b.vec.h, v_n)
                s.activation(az[:, :], zz[:, :], AF.Abs)
                b.ssp(s)
                s.activation(az[:, :], az[:, :], AF.Exp, scale=-1.0)
                b.ssp(s)
                b.act.inc(s.activation(az[:, :], az[:, :], AF.Ln, bias=1.0))
            b.sec("scalar", _)
            a_n = b.act.n

            def _(v, c=c, a_n=a_n):
                v.wait_ge(b.act.h, a_n)
                v.tensor_scalar(sp[:, :], zz[:, :], 0.0, None, ALU.max)
                b.vsp(v)
                v.tensor_tensor(sp[:, :], sp[:, :], az[:, :], ALU.add)
                b.vsp(v)
                b.vec.inc(v.tensor_tensor(sp[:, :], sp[:, :], C["negA_bc"][:, :], ALU.mult))
            b.sec("vector", _)
            v_n = b.vec.n

            def _(t, c=c):
                t.wait_ge(b.vec.h, b.vec.n)
                b.ten.inc(t.transpose(ps_tr[0:16, 0:128], sp[:, :], C["id_f32"][:, :]))
            b.sec("tensor", _)
            t_n = b.ten.n

            def _(v, c=c, t_n=t_n):
                v.wait_ge(b.ten.h, t_n)
                v.tensor_copy(laT[:, :], ps_tr[0:16, 0:128])
                b.vsp(v)
                b.vec.inc(v.tensor_tensor_scan(lcumT[:, 128 * c:128 * (c + 1)],
                                               laT[:, :], ZR16[:, :],
                                               0.0, ALU.add, ALU.add))
            b.sec("vector", _)

        for c in range(8):
            def _(t, c=c):
                t.wait_ge(b.vec.h, b.vec.n)
                b.ten.inc(t.transpose(ps_tr[:, 128:144], lcumT[:, 128 * c:128 * (c + 1)],
                                      C["id_f32"][0:16, 0:16]))
            b.sec("tensor", _)
            t_n = b.ten.n

            def _(v, c=c, t_n=t_n):
                v.wait_ge(b.ten.h, t_n)
                b.vec.inc(v.tensor_copy(LCT[:, 16 * c:16 * (c + 1)], ps_tr[:, 128:144]))
            b.sec("vector", _)
        v_n = b.vec.n

        def _(sy, v_n=v_n):
            sy.wait_ge(b.vec.h, v_n)
            b.syd.inc(sy.dma_start(io["rowscr"][0:16, 0:SH], lcumT[:, :]), 16)
        b.sec("sync", _)
        lcum_hbm = b.syd.n
        b.barrier()
        ba_st.close()

        if DEBUG_TAPS and half == 0:
            def _(g):
                g.wait_ge(b.vec.h, b.vec.n)
                b.gpd.inc(g.dma_start(io["dbg_ba"][:, 0:128], BETA[:, :]), 16)
                b.gpd.inc(g.dma_start(io["dbg_ba"][:, 128:256], LCT[:, :]), 16)
                if STAGE == 2:
                    b.gpd.inc(g.dma_start(io["dbg_v"], XLNT[:, :]), 16)
                    b.gpd.inc(g.dma_start(io["dbg_q"][:, 0:512], b.bufs["wbaT"][:, :]), 16)
            b.sec("gpsimd", _)
            b.barrier()

        if stage < 3:
            continue
        b.mark(f"a1.proj{half}")
        pj_st = ExitStack()
        wk = [b.sb(f"pj_wk{i}", [128, 16 * 128], BF16, pj_st) for i in range(2)]
        cw = b.sb("pj_cw", [128, 4 * 2], F32, pj_st)
        fwin = b.sb("pj_fwin", [128, 516], F32, pj_st)
        acc = [b.sb(f"pj_acc{i}", [128, 512], F32, pj_st) for i in range(2)]
        raw = b.sb("pj_raw", [128, 512], BF16, pj_st)
        q2 = b.sb("pj_q2", [128, 512], BF16, pj_st)
        ps_f = [b.ps(f"ps_f{i}", [128, 512], F32, pj_st) for i in range(2)]
        ps_r = b.ps("ps_r", [128, 512], F32, pj_st)
        ps_b = b.ps("ps_bc", [128, 512], F32, pj_st)
        row_f = b.sb("row_f", [1, 512], F32, pj_st)
        vstg = b.sb("pj_vstg", [128, 512], BF16, pj_st)
        ps_vb = b.ps("pj_psvb", [128, 512], BF16, pj_st)

        for ft in range(32):
            wb = wk[ft % 2]
            wname = f"pjw{ft % 2}"

            def _(g, ft=ft, wb=wb, wname=wname):
                b.gate_wait(g, wname)
                src3 = io["wqkvT"][:, 128 * ft:128 * (ft + 1)].rearrange(
                    "(k p) c -> p k c", p=128)
                b.gpd.inc(g.dma_start(wb[:, :].rearrange("p (k c) -> p k c", k=16), src3), 16)
                b.gpd.inc(g.dma_start(cw[:, 4 * (ft % 2):4 * (ft % 2) + 4],
                                      io["conv_w"][128 * ft:128 * (ft + 1), :]), 16)
            b.sec("gpsimd", _)
            g_n = b.gpd.n

            for sc in range(2):
                pf = ps_f[sc]
                soff = 512 * sc

                def _(t, ft=ft, sc=sc, pf=pf, soff=soff, g_n=g_n):
                    t.wait_ge(b.gpd.h, g_n)
                    if ft == 0 and sc == 0:
                        t.wait_ge(b.vec.h, xlnt_ready)
                    b.gate_wait(t, f"psf{sc}")
                    for k in range(16):
                        ins = t.matmul(pf[:, :], wb[:, 128 * k:128 * (k + 1)],
                                       XLNT[:, k * SH + soff:k * SH + soff + 512],
                                       start=(k == 0), stop=(k == 15))
                    b.ten.inc(ins)
                b.sec("tensor", _)
                t_n = b.ten.n
                if sc == 1:
                    b.gate_set(wname, b.ten)

                cwc = 4 * (ft % 2)

                def _(v, ft=ft, sc=sc, pf=pf, t_n=t_n):
                    v.wait_ge(b.ten.h, t_n)
                    v.tensor_copy(fwin[:, 0:3], TAILS[:, 3 * ft:3 * ft + 3])
                    b.vec.inc(v.tensor_copy(fwin[:, 3:515], pf[:, :]))
                    b.gate_set(f"psf{sc}", b.vec)
                b.sec("vector", _)
                v_n = b.vec.n

                def _(v, ft=ft, cwc=cwc):
                    v.tensor_scalar(acc[0][:, :], fwin[:, 3:515], cw[:, cwc + 3:cwc + 4],
                                    None, ALU.mult)
                    v.scalar_tensor_tensor(acc[1][:, :], fwin[:, 2:514],
                                           cw[:, cwc + 2:cwc + 3], acc[0][:, :],
                                           ALU.mult, ALU.add)
                    v.tensor_copy(TAILS[:, 3 * ft:3 * ft + 3], fwin[:, 512:515])
                    b.vec.inc(v.scalar_tensor_tensor(acc[0][:, :], fwin[:, 1:513],
                                                     cw[:, cwc + 1:cwc + 2], acc[1][:, :],
                                                     ALU.mult, ALU.add))
                    b.vec.inc(v.scalar_tensor_tensor(acc[1][:, :], fwin[:, 0:512],
                                                     cw[:, cwc + 0:cwc + 1], acc[0][:, :],
                                                     ALU.mult, ALU.add))
                b.sec("vector", _)
                v_n = b.vec.n

                if ft >= 16:
                    hh = ft - 16

                    def _(s, v_n=v_n):
                        s.wait_ge(b.vec.h, v_n)
                        b.gate_wait(s, "vstg")
                        b.act.inc(s.activation(vstg[:, :], acc[1][:, :], AF.Silu))
                    b.sec("scalar", _)
                    a_n = b.act.n

                    def _(t, a_n=a_n):
                        t.wait_ge(b.act.h, a_n)
                        b.gate_wait(t, "pjpsb")
                        for j in range(4):
                            ins = t.transpose(ps_vb[:, 128 * j:128 * (j + 1)],
                                              vstg[:, 128 * j:128 * (j + 1)],
                                              b.consts["id_bf"][:, :])
                        b.ten.inc(ins)
                    b.sec("tensor", _)
                    t_n = b.ten.n
                    b.gate_set("vstg", b.ten)

                    def _(v, hh=hh, sc=sc, t_n=t_n):
                        v.wait_ge(b.ten.h, t_n)
                        dst = VTOK[:, :].rearrange("p (c x) -> p c x", c=8)[
                            :, 4 * sc:4 * sc + 4, 128 * hh:128 * (hh + 1)]
                        b.vec.inc(v.tensor_copy(
                            dst, ps_vb[:, :].rearrange("p (j x) -> p j x", j=4)))
                    b.sec("vector", _)
                    b.gate_set("pjpsb", b.vec)
                else:
                    isq = ft < 8
                    dstb = QH if isq else KH
                    hq = ft % 8

                    def _(s, v_n=v_n):
                        s.wait_ge(b.vec.h, v_n)
                        s.activation(raw[:, :], acc[1][:, :], AF.Silu)
                        b.act.inc(s.activation(q2[:, :], raw[:, :], AF.Square))
                    b.sec("scalar", _)
                    a_n = b.act.n

                    def _(t, a_n=a_n):
                        t.wait_ge(b.act.h, a_n)
                        b.gate_wait(t, "psr")
                        b.ten.inc(t.matmul(ps_r[0:1, :], C["onescol_bf"][:, :], q2[:, :],
                                           start=True, stop=True))
                    b.sec("tensor", _)
                    t_n = b.ten.n

                    def _(s, isq=isq, t_n=t_n):
                        s.wait_ge(b.ten.h, t_n)
                        s.activation(row_f[:, :], ps_r[0:1, :], AF.Ln, bias=A_EPS)
                        b.ssp(s)
                        bias = -0.5 * math.log(128.0) if isq else 0.0
                        b.act.inc(s.activation(row_f[:, :], row_f[:, :], AF.Exp,
                                               scale=-0.5, bias=bias))
                    b.sec("scalar", _)
                    a_n = b.act.n
                    b.gate_set("psr", b.act)

                    def _(t, a_n=a_n):
                        t.wait_ge(b.act.h, a_n)
                        b.gate_wait(t, "psb")
                        b.ten.inc(t.matmul(ps_b[:, :], C["ones_f32"][:, :], row_f[:, :],
                                           start=True, stop=True))
                    b.sec("tensor", _)
                    t_n = b.ten.n

                    def _(v, dstb=dstb, hq=hq, sc=sc, t_n=t_n):
                        v.wait_ge(b.ten.h, t_n)
                        b.vec.inc(v.tensor_tensor(
                            dstb[:, hq * SH + 512 * sc:hq * SH + 512 * (sc + 1)],
                            raw[:, :], ps_b[:, :], ALU.mult))
                    b.sec("vector", _)
                    b.gate_set("psb", b.vec)
        b.barrier()
        pj_st.close()
        qkv_ready_v = b.vec.n
        qkv_ready_a = b.act.n

        if DEBUG_TAPS and half == 0:
            def _(g):
                g.wait_ge(b.vec.h, b.vec.n)
                g.wait_ge(b.act.h, b.act.n)
                b.gpd.inc(g.dma_start(io["dbg_q"], QH[:, :]), 16)
                b.gpd.inc(g.dma_start(io["dbg_k"], KH[:, :]), 16)
                b.gpd.inc(g.dma_start(io["dbg_v"], VTOK[:, :]), 16)
            b.sec("gpsimd", _)
            b.barrier()

        if stage < 4:
            continue
        b.mark(f"a1.scan{half}")
        sc_st = ExitStack()
        BAL = [b.sb("sc_bal0", [128, 2048], F32, sc_st)]
        DT = b.sb("sc_dt", [128, 512], F32, sc_st)
        SCR = b.sb("sc_scr", [128, 512], F32, sc_st)
        NALL = b.sb("sc_nall", [128, 2048], BF16, sc_st)
        WALL = b.sb("sc_wall", [128, 2048], BF16, sc_st)
        QG = b.sb("sc_qg", [128, 2048], BF16, sc_st)
        EA = b.sb("sc_ea", [128, 16 * 128], BF16, sc_st)
        EB = b.sb("sc_eb", [128, 16 * 128], BF16, sc_st)
        FB = b.sb("sc_fb", [128, 16 * 128], BF16, sc_st)
        RK = [b.sb(f"sc_rk{i}", [128, 16 * 128], BF16, sc_st) for i in range(2)]
        KTG = b.sb("sc_ktg", [128, 16 * 128], BF16, sc_st)
        CT4 = b.sb("sc_ct4", [128, 512], BF16, sc_st)
        ssq4 = b.sb("sc_ssq4", [128, 4], F32, sc_st)
        rstd4 = b.sb("sc_rstd4", [128, 4], F32, sc_st)
        ps0 = b.ps("sc_ps0", [128, 512], F32, sc_st)
        ps1 = b.ps("sc_ps1", [128, 512], F32, sc_st)
        psA = b.ps("sc_psA", [128, 512], F32, sc_st)
        psB = b.ps("sc_psB", [128, 512], F32, sc_st)
        psC = b.ps("sc_psC", [128, 512], F32, sc_st)
        ps4 = b.ps("sc_ps4", [128, 512], F32, sc_st)
        ps5 = b.ps("sc_ps5", [128, 512], F32, sc_st)
        psb = b.ps("sc_psb", [128, 1024], BF16, sc_st)

        for c in range(8):
            co = 128 * c
            bal = BAL[0]
            balname = "bal0"

            def _(sy, c=c, bal=bal, balname=balname, lcum_hbm=lcum_hbm):
                sy.wait_ge(b.syd.h, lcum_hbm)
                b.gate_wait(sy, balname)
                src = io["rowscr"][0:16, 128 * c:128 * (c + 1)].unsqueeze(
                    0).to_broadcast([128, 16, 128])
                b.syd.inc(sy.dma_start(
                    bal[:, :].rearrange("p (h s) -> p h s", h=16), src), 16)
            b.sec("sync", _)
            bal_n = b.syd.n

            def _(s, c=c, bal_n=bal_n):
                s.wait_ge(b.syd.h, bal_n)
                b.gate_wait(s, "gt")
                b.act.inc(s.activation(GT[:, :], LCT[:, 16 * c:16 * (c + 1)], AF.Exp))
            b.sec("scalar", _)
            a_n = b.act.n

            def _(v, c=c, bal=bal, a_n=a_n):
                v.wait_ge(b.act.h, a_n)
                v.tensor_scalar(NGT[:, :], GT[:, :], -1.0, None, ALU.mult)
                b.vec.inc(v.tensor_tensor(TET[:, :], bal[:, 127::128],
                                          LCT[:, 16 * c:16 * (c + 1)], ALU.subtract))
            b.sec("vector", _)
            v_n = b.vec.n

            def _(s, v_n=v_n):
                s.wait_ge(b.vec.h, v_n)
                b.act.inc(s.activation(TET[:, :], TET[:, :], AF.Exp))
            b.sec("scalar", _)
            eg_n = b.act.n

            for hg in range(4):
                go = 512 * hg

                def _(t, c=c, hg=hg, co=co):
                    if c == 0 and hg == 0:
                        t.wait_ge(b.vec.h, qkv_ready_v)
                        t.wait_ge(b.act.h, qkv_ready_a)
                    b.gate_wait(t, "sc01")
                    for hh in range(4):
                        h = 4 * hg + hh
                        hq = h // 2
                        ksl = KH[:, hq * SH + co:hq * SH + co + 128]
                        t.matmul(ps0[:, 128 * hh:128 * (hh + 1)], ksl, ksl,
                                 start=True, stop=True)
                        ins = t.matmul(ps1[:, 128 * hh:128 * (hh + 1)], ksl,
                                       QH[:, hq * SH + co:hq * SH + co + 128],
                                       start=True, stop=True)
                    b.ten.inc(ins)
                b.sec("tensor", _)
                t_n = b.ten.n

                def _(v, c=c, hg=hg, go=go, bal=bal, t_n=t_n, eg_n=eg_n):
                    v.wait_ge(b.ten.h, t_n)
                    v.wait_ge(b.act.h, eg_n)
                    lcslab = LCT[:, 16 * c + 4 * hg:16 * c + 4 * hg + 4].unsqueeze(
                        2).to_broadcast([128, 4, 128])
                    bal3 = bal[:, go:go + 512].rearrange("p (h s) -> p h s", h=4)
                    v.tensor_tensor(DT[:, :].rearrange("p (h s) -> p h s", h=4),
                                    lcslab, bal3, ALU.subtract)
                    msl = C["mask_sl"][:, :].unsqueeze(1).to_broadcast([128, 4, 128])
                    b.vec.inc(v.tensor_tensor(SCR[:, :].rearrange("p (h s) -> p h s", h=4),
                                              DT[:, :].rearrange("p (h s) -> p h s", h=4),
                                              msl, ALU.add))
                b.sec("vector", _)
                v_n = b.vec.n

                def _(s, v_n=v_n, bal=bal, go=go):
                    s.wait_ge(b.vec.h, v_n)
                    s.activation(SCR[:, :], SCR[:, :], AF.Exp)
                    b.act.inc(s.activation(bal[:, go:go + 512], bal[:, go:go + 512], AF.Exp))
                b.sec("scalar", _)
                a_n = b.act.n

                def _(v, c=c, hg=hg, go=go, a_n=a_n):
                    v.wait_ge(b.act.h, a_n)
                    v.tensor_tensor(SCR[:, :], SCR[:, :], ps0[:, :], ALU.mult)
                    nbsl = NBETA[:, 16 * c + 4 * hg:16 * c + 4 * hg + 4].unsqueeze(
                        2).to_broadcast([128, 4, 128])
                    v.tensor_tensor(NALL[:, go:go + 512].rearrange("p (h s) -> p h s", h=4),
                                    SCR[:, :].rearrange("p (h s) -> p h s", h=4),
                                    nbsl, ALU.mult)
                    miu = C["mask_iu"][:, :].unsqueeze(1).to_broadcast([128, 4, 128])
                    b.vec.inc(v.tensor_tensor(SCR[:, :].rearrange("p (h s) -> p h s", h=4),
                                              miu, DT[:, :].rearrange("p (h s) -> p h s", h=4),
                                              ALU.subtract))
                b.sec("vector", _)
                v_n = b.vec.n

                def _(s, v_n=v_n):
                    s.wait_ge(b.vec.h, v_n)
                    b.act.inc(s.activation(SCR[:, :], SCR[:, :], AF.Exp))
                b.sec("scalar", _)
                a_n = b.act.n

                def _(v, c=c, hg=hg, go=go, co=co, a_n=a_n):
                    v.wait_ge(b.act.h, a_n)
                    v.tensor_tensor(WALL[:, go:go + 512], SCR[:, :], ps1[:, :], ALU.mult)
                    for hh in range(4):
                        h = 4 * hg + hh
                        hq = h // 2
                        ins = v.tensor_tensor(QG[:, 128 * h:128 * (h + 1)],
                                              QH[:, hq * SH + co:hq * SH + co + 128],
                                              bal[:, 128 * h:128 * (h + 1)], ALU.mult)
                    b.vec.inc(ins)
                b.sec("vector", _)
                b.gate_set("sc01", b.vec)
            b.gate_set("gt", b.vec)
            build_v = b.vec.n

            for grp in range(4):
                def _(t, co=co, grp=grp, build_v=build_v):
                    t.wait_ge(b.vec.h, b.vec.n)
                    for j in range(8):
                        idx = grp * 8 + j
                        if idx < 16:
                            src = NALL[:, 128 * idx:128 * (idx + 1)]
                        else:
                            hq_ = (idx - 16) // 2
                            src = KH[:, hq_ * SH + co:hq_ * SH + co + 128]
                        ins = t.transpose(psb[:, 128 * j:128 * (j + 1)], src,
                                          C["id_bf"][:, :])
                    b.ten.inc(ins)
                b.sec("tensor", _)
                t_n = b.ten.n

                def _(v, grp=grp, t_n=t_n):
                    v.wait_ge(b.ten.h, t_n)
                    if grp < 2:
                        b.vec.inc(v.tensor_copy(EA[:, 1024 * grp:1024 * (grp + 1)], psb[:, :]))
                    else:
                        g2 = grp - 2
                        for j in range(8):
                            h = g2 * 8 + j
                            ins = v.tensor_scalar(KTG[:, 128 * h:128 * (h + 1)],
                                                  psb[:, 128 * j:128 * (j + 1)],
                                                  TET[:, h:h + 1], None, ALU.mult)
                        b.vec.inc(ins)
                b.sec("vector", _)
            tr_v = b.vec.n

            for wave in range(4):
                def _(t, co=co, wave=wave, tr_v=tr_v):
                    t.wait_ge(b.vec.h, tr_v)
                    b.gate_wait(t, "ps4")
                    for hh in range(4):
                        h = 4 * wave + hh
                        hq = h // 2
                        ins = t.matmul(ps4[:, 128 * hh:128 * (hh + 1)],
                                       KH[:, hq * SH + co:hq * SH + co + 128],
                                       Sbf[:, 128 * h:128 * (h + 1)],
                                       start=True, stop=True)
                    b.ten.inc(ins)
                b.sec("tensor", _)
                t_n = b.ten.n

                def _(v, c=c, wave=wave, t_n=t_n):
                    v.wait_ge(b.ten.h, t_n)
                    for hh in range(4):
                        h = 4 * wave + hh
                        v.scalar_tensor_tensor(SCR[:, 128 * hh:128 * (hh + 1)],
                                               ps4[:, 128 * hh:128 * (hh + 1)],
                                               NGT[:, h:h + 1],
                                               VTOK[:, 2048 * c + 128 * h:2048 * c + 128 * (h + 1)],
                                               ALU.mult, ALU.add)
                    for hh in range(4):
                        h = 4 * wave + hh
                        ins = v.tensor_scalar(RK[0][:, 128 * h:128 * (h + 1)],
                                              SCR[:, 128 * hh:128 * (hh + 1)],
                                              BETA[:, 16 * c + h:16 * c + h + 1],
                                              None, ALU.mult)
                    b.vec.inc(ins)
                b.sec("vector", _)
                b.gate_set("ps4", b.vec)

            ecur, fcur = EA, NALL
            enxt, fnxt = EB, FB
            psets = [(psA, psB, psC), (ps0, ps1, ps4)]
            for lvl in range(7):
                rk_in = RK[lvl % 2]
                rk_out = RK[(lvl + 1) % 2]
                for wave in range(4):
                    wo = 512 * wave
                    spar = (4 * lvl + wave) % 2
                    pA, pB, pC = psets[spar]

                    def _(t, lvl=lvl, wave=wave, wo=wo, ecur=ecur, fcur=fcur,
                          rk_in=rk_in, pA=pA, pB=pB, pC=pC, spar=spar):
                        b.gate_wait(t, f"invs{spar}")
                        if spar == 1:
                            b.gate_wait(t, "ps4")
                            b.gate_wait(t, "sc01")
                        for hh in range(4):
                            h = 4 * wave + hh
                            esl = ecur[:, 128 * h:128 * (h + 1)]
                            ins = t.matmul(pA[:, 128 * hh:128 * (hh + 1)], esl,
                                           rk_in[:, 128 * h:128 * (h + 1)],
                                           start=True, stop=True)
                            if lvl < 6:
                                fsl = fcur[:, 128 * h:128 * (h + 1)]
                                t.matmul(pB[:, 128 * hh:128 * (hh + 1)], esl, fsl,
                                         start=True, stop=True)
                                ins = t.matmul(pC[:, 128 * hh:128 * (hh + 1)], fsl, esl,
                                               start=True, stop=True)
                        b.ten.inc(ins)
                    b.sec("tensor", _)
                    t_n = b.ten.n

                    def _(v, lvl=lvl, wo=wo, t_n=t_n, rk_in=rk_in, rk_out=rk_out,
                          enxt=enxt, fnxt=fnxt, pA=pA, pB=pB, pC=pC, spar=spar):
                        v.wait_ge(b.ten.h, t_n)
                        ins = v.tensor_tensor(rk_out[:, wo:wo + 512], pA[:, :],
                                              rk_in[:, wo:wo + 512], ALU.add)
                        if lvl < 6:
                            v.tensor_copy(fnxt[:, wo:wo + 512], pB[:, :])
                            ins = v.tensor_copy(enxt[:, wo:wo + 512], pC[:, :])
                        b.vec.inc(ins)
                        b.gate_set(f"invs{spar}", b.vec)
                        if spar == 1:
                            b.gate_set("sc01", b.vec)
                            b.gate_set("ps4", b.vec)
                    b.sec("vector", _)
                ecur, fcur, enxt, fnxt = enxt, fnxt, ecur, fcur
                if lvl == 0:
                    fnxt = FB if fcur is not FB else NALL
            U = RK[1]
            inv_v = b.vec.n

            for wave in range(4):
                wo = 512 * wave

                def _(t, wave=wave, wo=wo, inv_v=inv_v):
                    t.wait_ge(b.vec.h, inv_v)
                    b.gate_wait(t, "ps5")
                    for hh in range(4):
                        h = 4 * wave + hh
                        t.matmul(ps5[:, 128 * hh:128 * (hh + 1)],
                                 QG[:, 128 * h:128 * (h + 1)],
                                 Sbf[:, 128 * h:128 * (h + 1)], start=True, stop=False)
                        ins = t.matmul(ps5[:, 128 * hh:128 * (hh + 1)],
                                       WALL[:, 128 * h:128 * (h + 1)],
                                       U[:, 128 * h:128 * (h + 1)], start=False, stop=True)
                    b.ten.inc(ins)
                b.sec("tensor", _)
                t_n = b.ten.n

                def _(s, t_n=t_n):
                    s.wait_ge(b.ten.h, t_n)
                    for hh in range(4):
                        ins = s.activation(SCR[:, 128 * hh:128 * (hh + 1)],
                                           ps5[:, 128 * hh:128 * (hh + 1)], AF.Square,
                                           accum_out=ssq4[:, hh:hh + 1])
                    b.act.inc(ins)
                b.sec("scalar", _)
                a_n = b.act.n

                def _(s, a_n=a_n):
                    b.ssp(s)
                    s.activation(rstd4[:, :], ssq4[:, :], AF.Ln, scale=1.0 / DV, bias=A_EPS)
                    b.ssp(s)
                    b.act.inc(s.activation(rstd4[:, :], rstd4[:, :], AF.Exp, scale=-0.5))
                b.sec("scalar", _)
                a_n = b.act.n

                def _(v, wave=wave, a_n=a_n):
                    v.wait_ge(b.act.h, a_n)
                    rmb = C["rmsw_bc"][:, :].unsqueeze(1).to_broadcast([128, 4, 128])
                    v.tensor_tensor(SCR[:, :].rearrange("p (h s) -> p h s", h=4),
                                    ps5[:, :].rearrange("p (h s) -> p h s", h=4),
                                    rmb, ALU.mult)
                    b.vsp(v)
                    for hh in range(4):
                        ins = v.tensor_scalar(CT4[:, 128 * hh:128 * (hh + 1)],
                                              SCR[:, 128 * hh:128 * (hh + 1)],
                                              rstd4[:, hh:hh + 1], None, ALU.mult)
                    b.vec.inc(ins)
                b.sec("vector", _)
                v_n = b.vec.n
                b.gate_set("ps5", b.vec)

                def _(t, wave=wave, v_n=v_n):
                    t.wait_ge(b.vec.h, v_n)
                    b.gate_wait(t, "psbo")
                    for hh in range(4):
                        ins = t.transpose(psb[:, 512 + 128 * hh:512 + 128 * (hh + 1)],
                                          CT4[:, 128 * hh:128 * (hh + 1)], C["id_bf"][:, :])
                    b.ten.inc(ins)
                b.sec("tensor", _)
                t_n = b.ten.n

                def _(v, c=c, wave=wave, co=co, t_n=t_n):
                    v.wait_ge(b.ten.h, t_n)
                    dst = CTX[:, :].rearrange("p (h s) -> p h s", h=16)[
                        :, 4 * wave:4 * wave + 4, co:co + 128]
                    b.vec.inc(v.tensor_copy(
                        dst, psb[:, 512:1024].rearrange("p (h s) -> p h s", h=4)))
                b.sec("vector", _)
                b.gate_set("psbo", b.vec)

            for wave in range(4):
                def _(t, wave=wave):
                    t.wait_ge(b.vec.h, b.vec.n)
                    b.gate_wait(t, "ps4")
                    for hh in range(4):
                        h = 4 * wave + hh
                        ins = t.matmul(ps4[:, 128 * hh:128 * (hh + 1)],
                                       KTG[:, 128 * h:128 * (h + 1)],
                                       U[:, 128 * h:128 * (h + 1)], start=True, stop=True)
                    b.ten.inc(ins)
                b.sec("tensor", _)
                t_n = b.ten.n

                def _(v, wave=wave, t_n=t_n, bal=bal):
                    v.wait_ge(b.ten.h, t_n)
                    for hh in range(4):
                        h = 4 * wave + hh
                        v.scalar_tensor_tensor(Sf[:, 128 * h:128 * (h + 1)],
                                               Sf[:, 128 * h:128 * (h + 1)],
                                               bal[:, 128 * h + 127:128 * h + 128],
                                               ps4[:, 128 * hh:128 * (hh + 1)],
                                               ALU.mult, ALU.add)
                    for hh in range(4):
                        h = 4 * wave + hh
                        ins = v.tensor_copy(Sbf[:, 128 * h:128 * (h + 1)],
                                            Sf[:, 128 * h:128 * (h + 1)])
                    b.vec.inc(ins)
                b.sec("vector", _)
                b.gate_set("ps4", b.vec)
            b.gate_set(balname, b.vec)

        b.barrier()
        sc_st.close()
        scan_v = b.vec.n

        if DEBUG_TAPS and half == 0:
            def _(g):
                g.wait_ge(b.vec.h, b.vec.n)
                b.gpd.inc(g.dma_start(io["dbg_ctx"], CTX[:, :]), 16)
            b.sec("gpsimd", _)
            b.barrier()

        if stage < 5:
            continue
        b.mark(f"a1.gate{half}")
        gt_st = ExitStack()
        wg = [b.sb(f"gt_wg{i}", [128, 16 * 128], BF16, gt_st) for i in range(2)]
        gsb = [b.sb(f"gt_gsb{i}", [128, 512], BF16, gt_st) for i in range(2)]
        ps_g = [b.ps(f"gt_psg{i}", [128, 512], F32, gt_st) for i in range(2)]
        for ft in range(16):
            wb = wg[ft % 2]
            wname = f"gtw{ft % 2}"

            def _(g, ft=ft, wb=wb, wname=wname):
                b.gate_wait(g, wname)
                src3 = io["wgateT"][:, 128 * ft:128 * (ft + 1)].rearrange(
                    "(k p) c -> p k c", p=128)
                b.gpd.inc(g.dma_start(wb[:, :].rearrange("p (k c) -> p k c", k=16), src3), 16)
            b.sec("gpsimd", _)
            g_n = b.gpd.n

            for sc in range(2):
                pg = ps_g[sc]
                gb = gsb[sc]

                def _(t, ft=ft, sc=sc, wb=wb, pg=pg, g_n=g_n):
                    t.wait_ge(b.gpd.h, g_n)
                    b.gate_wait(t, f"psg{sc}")
                    for k in range(16):
                        ins = t.matmul(pg[:, :], wb[:, 128 * k:128 * (k + 1)],
                                       XLNT[:, k * SH + 512 * sc:k * SH + 512 * (sc + 1)],
                                       start=(k == 0), stop=(k == 15))
                    b.ten.inc(ins)
                b.sec("tensor", _)
                t_n = b.ten.n
                if sc == 1:
                    b.gate_set(wname, b.ten)

                def _(s, sc=sc, pg=pg, gb=gb, t_n=t_n):
                    s.wait_ge(b.ten.h, t_n)
                    b.gate_wait(s, f"gsb{sc}")
                    b.act.inc(s.activation(gb[:, :], pg[:, :], AF.Silu))
                b.sec("scalar", _)
                a_n = b.act.n
                b.gate_set(f"psg{sc}", b.act)

                def _(v, ft=ft, sc=sc, gb=gb, a_n=a_n):
                    v.wait_ge(b.act.h, a_n)
                    sl = CTX[:, ft * SH + 512 * sc:ft * SH + 512 * (sc + 1)]
                    b.vec.inc(v.tensor_tensor(sl, gb[:, :], sl, ALU.mult))
                b.sec("vector", _)
                b.gate_set(f"gsb{sc}", b.vec)
        b.barrier()
        gt_st.close()

        if stage < 6:
            continue
        b.mark(f"a1.out{half}")
        op_st = ExitStack()
        wo = [b.sb(f"op_wo{i}", [128, 16 * 128], BF16, op_st) for i in range(2)]
        osb = [b.sb(f"op_osb{i}", [128, 512], F32, op_st) for i in range(3)]
        rsb = [b.sb(f"op_rsb{i}", [128, 512], F32, op_st) for i in range(3)]
        ot = [b.sb(f"op_ot{i}", [128, 512], F32, op_st) for i in range(3)]
        ps_o = [b.ps(f"op_pso{i}", [128, 512], F32, op_st) for i in range(3)]
        ps_t = [b.ps(f"op_pst{i}", [128, 512], F32, op_st) for i in range(3)]
        ocnt = [0]
        for dt_ in range(16):
            wb = wo[dt_ % 2]
            wname = f"opw{dt_ % 2}"

            def _(g, dt_=dt_, wb=wb, wname=wname):
                b.gate_wait(g, wname)
                src3 = io["woutT"][:, 128 * dt_:128 * (dt_ + 1)].rearrange(
                    "(k p) c -> p k c", p=128)
                b.gpd.inc(g.dma_start(wb[:, :].rearrange("p (k c) -> p k c", k=16), src3), 16)
            b.sec("gpsimd", _)
            g_n = b.gpd.n

            for sc in range(2):
                soff = 512 * sc
                obk = ocnt[0] % 3
                ocnt[0] += 1
                po = ps_o[obk]
                pt = ps_t[obk]
                ob = osb[obk]
                otb = ot[obk]
                rb = rsb[obk]

                def _(sy, dt_=dt_, sc=sc, s0=s0, soff=soff, rb=rb, obk=obk):
                    b.gate_wait(sy, f"oprs{obk}")
                    src3 = io["x_in"][s0 + soff:s0 + soff + 512,
                                      128 * dt_:128 * (dt_ + 1)].rearrange(
                        "(j p) c -> p j c", p=128)
                    b.syd.inc(sy.dma_start(rb[:, :].rearrange("p (j c) -> p j c", j=4),
                                           src3), 16)
                b.sec("sync", _)
                r_n = b.syd.n

                def _(t, dt_=dt_, sc=sc, wb=wb, soff=soff, po=po, obk=obk, g_n=g_n):
                    t.wait_ge(b.gpd.h, g_n)
                    b.gate_wait(t, f"pso{obk}")
                    for k in range(16):
                        ins = t.matmul(po[:, :], wb[:, 128 * k:128 * (k + 1)],
                                       CTX[:, k * SH + soff:k * SH + soff + 512],
                                       start=(k == 0), stop=(k == 15))
                    b.ten.inc(ins)
                b.sec("tensor", _)
                t_n = b.ten.n
                if sc == 1:
                    b.gate_set(wname, b.ten)

                def _(v, sc=sc, po=po, ob=ob, obk=obk, t_n=t_n):
                    v.wait_ge(b.ten.h, t_n)
                    b.gate_wait(v, f"osb{obk}")
                    b.vec.inc(v.tensor_copy(ob[:, :], po[:, :]))
                b.sec("vector", _)
                v_n = b.vec.n
                b.gate_set(f"pso{obk}", b.vec)

                def _(t, sc=sc, ob=ob, pt=pt, obk=obk, v_n=v_n):
                    t.wait_ge(b.vec.h, v_n)
                    b.gate_wait(t, f"pst{obk}")
                    for j in range(4):
                        ins = t.transpose(pt[:, 128 * j:128 * (j + 1)],
                                          ob[:, 128 * j:128 * (j + 1)], C["id_f32"][:, :])
                    b.ten.inc(ins)
                b.sec("tensor", _)
                t_n = b.ten.n
                b.gate_set(f"osb{obk}", b.ten)

                def _(v, sc=sc, pt=pt, otb=otb, obk=obk, t_n=t_n, r_n=r_n):
                    v.wait_ge(b.ten.h, t_n)
                    v.wait_ge(b.syd.h, r_n)
                    b.gate_wait(v, f"opot{obk}")
                    b.vec.inc(v.tensor_tensor(otb[:, :], pt[:, :], rb[:, :], ALU.add))
                b.sec("vector", _)
                v_n = b.vec.n
                b.gate_set(f"pst{obk}", b.vec)
                b.gate_set(f"oprs{obk}", b.vec)

                def _(sy, dt_=dt_, sc=sc, s0=s0, soff=soff, otb=otb, v_n=v_n):
                    sy.wait_ge(b.vec.h, v_n)
                    dst3 = io["res"][s0 + soff:s0 + soff + 512,
                                     128 * dt_:128 * (dt_ + 1)].rearrange(
                        "(j p) c -> p j c", p=128)
                    b.syd.inc(sy.dma_start(dst3, otb[:, :].rearrange("p (j c) -> p j c", j=4)),
                              16)
                b.sec("sync", _)
                b.gate_set(f"opot{obk}", b.syd)
        b.barrier()
        op_st.close()
    sa.close()


def emit_ffn(b, io, src_res, dst_res, tag):
    C = b.consts
    sa = ExitStack()
    HP = b.sb("fHP", [128, NR * 8 * SH], BF16, sa)
    HS = b.sb("fHS", [128, 16 * SH], BF16, sa)
    wg_hi = b.sb("f_wghi", [128, 16 * NE], BF16, sa)
    wg_lo = b.sb("f_wglo", [128, 16 * NE], BF16, sa)
    bias_c = b.sb("f_biasc", [128, 128], F32, sa)

    pp_st = ExitStack()
    wgf = b.sb("f_wgf", [128, 16 * NE], F32, pp_st)
    bst = b.sb("f_bst", [6, 2 * HF], F32, pp_st)
    bst2 = b.sb("f_bst2", [1, 2 * D], F32, pp_st)
    ps_p = b.ps("f_psp", [128, 512], F32, pp_st)

    def _(g):
        src3 = io["Wg"][:, :].rearrange("(k p) c -> p k c", p=128)
        b.gpd.inc(g.dma_start(wgf[:, :].rearrange("p (k c) -> p k c", k=16), src3), 16)
        b.gpd.inc(g.dma_start(bst[:, 0:HF], io["bgate"]), 16)
        b.gpd.inc(g.dma_start(bst[:, HF:2 * HF], io["bup"]), 16)
        b.gpd.inc(g.dma_start(bst2[:, 0:D], io["bs_up"]), 16)
        b.gpd.inc(g.dma_start(bst2[:, D:2 * D], io["bs_down"]), 16)
    b.sec("gpsimd", _)
    g_n = b.gpd.n

    def _(t, g_n=g_n):
        t.wait_ge(b.gpd.h, g_n)
        for ft in range(8):
            t.transpose(ps_p[:, 6 * ft:6 * (ft + 1)], bst[:, 128 * ft:128 * (ft + 1)],
                        C["id_f32"][0:6, 0:6])
            t.transpose(ps_p[:, 48 + 6 * ft:48 + 6 * (ft + 1)],
                        bst[:, HF + 128 * ft:HF + 128 * (ft + 1)], C["id_f32"][0:6, 0:6])
        for ft in range(16):
            t.transpose(ps_p[:, 96 + ft:97 + ft], bst2[:, 128 * ft:128 * (ft + 1)],
                        C["id_f32"][0:1, 0:1])
            ins = t.transpose(ps_p[:, 112 + ft:113 + ft],
                              bst2[:, D + 128 * ft:D + 128 * (ft + 1)],
                              C["id_f32"][0:1, 0:1])
        b.ten.inc(ins)
    b.sec("tensor", _)
    t_n = b.ten.n

    def _(v, g_n=g_n, t_n=t_n):
        v.wait_ge(b.ten.h, t_n)
        v.tensor_copy(bias_c[:, :], ps_p[:, 0:128])
        v.tensor_copy(wg_hi[:, :], wgf[:, :])
        b.vsp(v)
        b.vec.inc(v.tensor_tensor(wg_lo[:, :], wgf[:, :], wg_hi[:, :], ALU.subtract))
    b.sec("vector", _)
    b.barrier()
    pp_st.close()

    for half in range(2):
        s0 = half * SH
        b.mark(f"{tag}.ln{half}")
        xh_st = ExitStack()
        XLNT = b.sb("fXLNT", [128, 16 * SH], BF16, xh_st)
        ln_st = ExitStack()
        XLO_t = b.sb("fXLOt", [128, 16 * 128], BF16, ln_st)
        ps_r = b.ps("fr_ps", [128, 512], F32, ln_st)
        ps_t = b.ps("fr_pst", [128, 512], F32, ln_st)
        g8 = b.sb("fr_g8", [128, 8], F32, ln_st)
        g8b = b.sb("fr_g8b", [128, 8], F32, ln_st)
        e8 = b.sb("fr_e8", [128, 8], F32, ln_st)
        m1 = b.sb("fr_m1", [128, 1], F32, ln_st)
        m2 = b.sb("fr_m2", [128, 1], F32, ln_st)
        mx = b.sb("fr_mx", [128, 1], F32, ln_st)
        s8 = b.sb("fr_s8", [128, 1], F32, ln_st)
        keep = b.sb("fr_keep", [128, 8], F32, ln_st)
        w_tok = b.sb("fr_wtok", [128, 8], F32, ln_st)
        wrowT = b.sb("fr_wrowT", [8, SH], F32, ln_st)

        def router_tile(it):
            def _(t, it=it):
                t.wait_ge(b.vec.h, b.vec.n)
                b.gate_wait(t, "fr_ps")
                soff = it * 128
                for ki in range(16):
                    lhs_hi = XLNT[:, ki * SH + soff:ki * SH + soff + 128]
                    lhs_lo = XLO_t[:, 128 * ki:128 * (ki + 1)]
                    t.matmul(ps_r[:, 0:8], lhs_hi, wg_hi[:, 8 * ki:8 * (ki + 1)],
                             start=(ki == 0), stop=False)
                    t.matmul(ps_r[:, 0:8], lhs_hi, wg_lo[:, 8 * ki:8 * (ki + 1)],
                             start=False, stop=False)
                    ins = t.matmul(ps_r[:, 0:8], lhs_lo, wg_hi[:, 8 * ki:8 * (ki + 1)],
                                   start=False, stop=(ki == 15))
                b.ten.inc(ins)
                b.gate_set("ln_lo_free", b.ten)
            b.sec("tensor", _)
            t_n = b.ten.n

            def _(v, t_n=t_n):
                v.wait_ge(b.ten.h, t_n)
                v.tensor_tensor(g8[:, :], ps_r[:, 0:8], C["bg_bc"][:, :], ALU.add)
                b.vsp(v)
                v.tensor_reduce(m1[:, :], g8[:, :], axis=AX.X, op=ALU.min)
                v.tensor_reduce(mx[:, :], g8[:, :], axis=AX.X, op=ALU.max)
                b.vsp(v)
                v.tensor_scalar(g8b[:, :], g8[:, :], m1[:, :], None, ALU.is_equal)
                v.tensor_scalar(mx[:, :], mx[:, :], -1.0, None, ALU.mult)
                b.vsp(v)
                v.scalar_tensor_tensor(g8b[:, :], g8b[:, :], 1e30, g8[:, :],
                                       ALU.mult, ALU.add)
                b.vsp(v)
                b.vec.inc(v.tensor_reduce(m2[:, :], g8b[:, :], axis=AX.X, op=ALU.min))
            b.sec("vector", _)
            v_n = b.vec.n
            b.gate_set("fr_ps", b.vec)

            def _(s, v_n=v_n):
                s.wait_ge(b.vec.h, v_n)
                b.act.inc(s.activation(e8[:, :], g8[:, :], AF.Exp, bias=mx[:, :]))
            b.sec("scalar", _)
            a_n = b.act.n

            def _(v, a_n=a_n):
                v.wait_ge(b.act.h, a_n)
                v.tensor_scalar(keep[:, :], g8[:, :], m2[:, :], None, ALU.is_gt)
                b.vsp(v)
                v.tensor_tensor(e8[:, :], e8[:, :], keep[:, :], ALU.mult)
                b.vsp(v)
                v.tensor_reduce(s8[:, :], e8[:, :], axis=AX.X, op=ALU.add)
                b.vsp(v)
                v.reciprocal(s8[:, :], s8[:, :])
                b.vsp(v)
                b.vec.inc(v.tensor_scalar(w_tok[:, :], e8[:, :], s8[:, :], None, ALU.mult))
            b.sec("vector", _)
            v_n = b.vec.n

            def _(t, it=it, v_n=v_n):
                t.wait_ge(b.vec.h, v_n)
                b.gate_wait(t, "fr_pst")
                b.ten.inc(t.transpose(ps_t[0:8, 0:128], w_tok[:, :], C["id_f32"][:, :]))
            b.sec("tensor", _)
            t_n = b.ten.n

            def _(v, it=it, t_n=t_n):
                v.wait_ge(b.ten.h, t_n)
                b.vec.inc(v.tensor_copy(wrowT[:, 128 * it:128 * (it + 1)], ps_t[0:8, 0:128]))
            b.sec("vector", _)
            b.gate_set("fr_pst", b.vec)

        emit_ln(b, io, src_res, s0, 8, XLNT, SH, ln_st, want_lo=XLO_t,
                tile_cb=router_tile)
        v_n = b.vec.n

        def _(sy, v_n=v_n):
            sy.wait_ge(b.vec.h, v_n)
            b.syd.inc(sy.dma_start(io["rowscr"][32:40, 0:SH], wrowT[:, :]), 16)
        b.sec("sync", _)
        wrow_hbm = b.syd.n

        if DEBUG_TAPS and half == 0 and tag == "f1":
            def _(g):
                g.wait_ge(b.vec.h, b.vec.n)
                b.gpd.inc(g.dma_start(io["dbg_ba"][:, 0:8], w_tok[:, :]), 16)
            b.sec("gpsimd", _)

        b.barrier()
        ln_st.close()

        b.mark(f"{tag}.up{half}")
        up_st = ExitStack()
        wkg = [b.sb(f"fu_wg{i}", [128, 16 * 128], BF16, up_st) for i in range(2)]
        wku = [b.sb(f"fu_wu{i}", [128, 16 * 128], BF16, up_st) for i in range(2)]
        wbc = b.sb("fu_wbc", [128, 512 * 2], F32, up_st)
        wrow_sb = b.sb("fu_wrow", [1, 512], F32, up_st)
        sg = [b.sb(f"fu_sg{i}", [128, 512], BF16, up_st) for i in range(3)]
        ps_g = [b.ps(f"fu_psg{i}", [128, 512], F32, up_st) for i in range(3)]
        ps_u = [b.ps(f"fu_psu{i}", [128, 512], F32, up_st) for i in range(3)]
        ps_w = b.ps("fu_psw", [128, 512], F32, up_st)
        ucnt = [0]

        for ft in range(16):
            wb = wkg[ft % 2]
            wname = f"fuw{ft % 2}"

            def _(g, ft=ft, wb=wb, wname=wname):
                b.gate_wait(g, wname)
                src3 = io["Ws_up"][:, 128 * ft:128 * (ft + 1)].rearrange(
                    "(k p) c -> p k c", p=128)
                b.gpd.inc(g.dma_start(wb[:, :].rearrange("p (k c) -> p k c", k=16), src3), 16)
            b.sec("gpsimd", _)
            g_n = b.gpd.n

            for sc in range(2):
                bk = ucnt[0] % 3
                ucnt[0] += 1
                pb = ps_g[bk]

                def _(t, ft=ft, sc=sc, wb=wb, pb=pb, bk=bk, g_n=g_n):
                    t.wait_ge(b.gpd.h, g_n)
                    b.gate_wait(t, f"fu_psg{bk}")
                    for k in range(16):
                        ins = t.matmul(pb[:, :], wb[:, 128 * k:128 * (k + 1)],
                                       XLNT[:, k * SH + 512 * sc:k * SH + 512 * (sc + 1)],
                                       start=(k == 0), stop=(k == 15))
                    b.ten.inc(ins)
                b.sec("tensor", _)
                t_n = b.ten.n
                if sc == 1:
                    b.gate_set(wname, b.ten)

                def _(s, ft=ft, sc=sc, pb=pb, t_n=t_n):
                    s.wait_ge(b.ten.h, t_n)
                    b.act.inc(s.activation(HS[:, ft * SH + 512 * sc:ft * SH + 512 * (sc + 1)],
                                           pb[:, :], AF.Silu, bias=bias_c[:, 96 + ft:97 + ft]))
                b.sec("scalar", _)
                b.gate_set(f"fu_psg{bk}", b.act)

        for n in range(NR):
            for ft in range(8):
                wbg_ = wkg[ft % 2]
                wbu_ = wku[ft % 2]
                wname = f"fuw{ft % 2}"
                uname = f"fuu{ft % 2}"

                def _(g, n=n, ft=ft, wbg_=wbg_, wbu_=wbu_, wname=wname, uname=uname):
                    b.gate_wait(g, wname)
                    b.gate_wait(g, uname)
                    src3 = io["Wgate"][n][:, 128 * ft:128 * (ft + 1)].rearrange(
                        "(k p) c -> p k c", p=128)
                    b.gpd.inc(g.dma_start(
                        wbg_[:, :].rearrange("p (k c) -> p k c", k=16), src3), 16)
                    src3 = io["Wup"][n][:, 128 * ft:128 * (ft + 1)].rearrange(
                        "(k p) c -> p k c", p=128)
                    b.gpd.inc(g.dma_start(
                        wbu_[:, :].rearrange("p (k c) -> p k c", k=16), src3), 16)
                b.sec("gpsimd", _)
                g_n = b.gpd.n

                for sc in range(2):
                    if ft == 0:
                        def _(sy, n=n, sc=sc, wrow_hbm=wrow_hbm):
                            sy.wait_ge(b.syd.h, wrow_hbm)
                            b.gate_wait(sy, f"fu_wrow")
                            b.syd.inc(sy.dma_start(
                                wrow_sb[:, :], io["rowscr"][32 + n:33 + n,
                                                            512 * sc:512 * (sc + 1)]), 16)
                        b.sec("sync", _)
                        d_n = b.syd.n

                        def _(t, sc=sc, d_n=d_n):
                            t.wait_ge(b.syd.h, d_n)
                            b.gate_wait(t, "fu_psw")
                            b.ten.inc(t.matmul(ps_w[:, :], C["ones_f32"][:, :],
                                               wrow_sb[:, :], start=True, stop=True))
                        b.sec("tensor", _)
                        t_n = b.ten.n
                        b.gate_set("fu_wrow", b.ten)

                        def _(v, sc=sc, t_n=t_n):
                            v.wait_ge(b.ten.h, t_n)
                            b.gate_wait(v, f"fu_wbc{sc}")
                            b.vec.inc(v.tensor_copy(wbc[:, 512 * sc:512 * (sc + 1)],
                                                    ps_w[:, :]))
                        b.sec("vector", _)
                        b.gate_set("fu_psw", b.vec)

                    bk = ucnt[0] % 3
                    ucnt[0] += 1
                    pg = ps_g[bk]
                    pu = ps_u[bk]
                    sgb = sg[bk]

                    def _(t, n=n, ft=ft, sc=sc, wbg_=wbg_, wbu_=wbu_, pg=pg, pu=pu,
                          bk=bk, g_n=g_n):
                        t.wait_ge(b.gpd.h, g_n)
                        if ft == 0:
                            t.wait_ge(b.vec.h, b.vec.n)
                        b.gate_wait(t, f"fu_psg{bk}")
                        b.gate_wait(t, f"fu_psu{bk}")
                        for k in range(16):
                            t.matmul(pg[:, :], wbg_[:, 128 * k:128 * (k + 1)],
                                     XLNT[:, k * SH + 512 * sc:k * SH + 512 * (sc + 1)],
                                     start=(k == 0), stop=(k == 15))
                        for k in range(16):
                            ins = t.matmul(pu[:, :], wbu_[:, 128 * k:128 * (k + 1)],
                                           XLNT[:, k * SH + 512 * sc:k * SH + 512 * (sc + 1)],
                                           start=(k == 0), stop=(k == 15))
                        b.ten.inc(ins)
                    b.sec("tensor", _)
                    t_n = b.ten.n
                    if sc == 1:
                        b.gate_set(wname, b.ten)
                        b.gate_set(uname, b.ten)

                    def _(s, n=n, ft=ft, sc=sc, pg=pg, sgb=sgb, bk=bk, t_n=t_n):
                        s.wait_ge(b.ten.h, t_n)
                        b.gate_wait(s, f"fu_sg{bk}")
                        b.act.inc(s.activation(sgb[:, :], pg[:, :], AF.Silu,
                                               bias=bias_c[:, 6 * ft + n:6 * ft + n + 1]))
                    b.sec("scalar", _)
                    a_n = b.act.n
                    b.gate_set(f"fu_psg{bk}", b.act)

                    def _(v, n=n, ft=ft, sc=sc, pu=pu, sgb=sgb, a_n=a_n):
                        v.wait_ge(b.act.h, a_n)
                        dst = HP[:, (n * 8 + ft) * SH + 512 * sc:(n * 8 + ft) * SH + 512 * (sc + 1)]
                        v.scalar_tensor_tensor(dst, pu[:, :],
                                               bias_c[:, 48 + 6 * ft + n:48 + 6 * ft + n + 1],
                                               wbc[:, 512 * sc:512 * (sc + 1)],
                                               ALU.add, ALU.mult)
                        b.vsp(v)
                        b.vec.inc(v.tensor_tensor(dst, sgb[:, :], dst, ALU.mult))
                    b.sec("vector", _)
                    b.gate_set(f"fu_psu{bk}", b.vec)
                    b.gate_set(f"fu_sg{bk}", b.vec)
                    b.gate_set(f"fu_wbc{sc}", b.vec)
        b.barrier()
        if DEBUG_TAPS and half == 0 and tag == "f1":
            def _(sy):
                b.syd.inc(sy.dma_start(io["dbg_hp"], HP[:, :]), 16)
                b.syd.inc(sy.dma_start(io["dbg_hs"], HS[:, :]), 16)
            b.sec("sync", _)
            b.barrier()
        up_st.close()
        xh_st.close()

        b.mark(f"{tag}.down{half}")
        dn_st = ExitStack()
        wkd = [b.sb(f"fd_wd{i}", [128, (NR * 8 + 16) * 128], BF16, dn_st) for i in range(2)]
        osb = [b.sb(f"fd_osb{i}", [128, 512], F32, dn_st) for i in range(3)]
        rsb = [b.sb(f"fd_rsb{i}", [128, 512], F32, dn_st) for i in range(3)]
        ot = [b.sb(f"fd_ot{i}", [128, 512], F32, dn_st) for i in range(3)]
        ps_o = [b.ps(f"fd_pso{i}", [128, 512], F32, dn_st) for i in range(3)]
        ps_t2 = [b.ps(f"fd_pst{i}", [128, 512], F32, dn_st) for i in range(3)]
        dcnt = [0]
        bdown_bf = b.sb("f_bdownbf", [6, D], BF16, dn_st)
        w6 = b.sb("f_w6", [6, SH], BF16, dn_st)

        def _(g, wrow_hbm=wrow_hbm):
            g.wait_ge(b.syd.h, wrow_hbm)
            b.gpd.inc(g.dma_start(w6[:, :], io["rowscr"][32:38, 0:SH]), 16)
            b.gpd.inc(g.dma_start(bdown_bf[:, :], io["bdown"]), 16)
        b.sec("gpsimd", _)
        w6_n = b.gpd.n

        for dt_ in range(16):
            wb = wkd[dt_ % 2]
            wname = f"fdw{dt_ % 2}"

            def _(g, dt_=dt_, wb=wb, wname=wname):
                b.gate_wait(g, wname)
                for n in range(NR):
                    src3 = io["Wdown"][n][:, 128 * dt_:128 * (dt_ + 1)].rearrange(
                        "(k p) c -> p k c", p=128)
                    b.gpd.inc(g.dma_start(
                        wb[:, n * 1024:(n + 1) * 1024].rearrange("p (k c) -> p k c", k=8),
                        src3), 16)
                src3 = io["Ws_down"][:, 128 * dt_:128 * (dt_ + 1)].rearrange(
                    "(k p) c -> p k c", p=128)
                b.gpd.inc(g.dma_start(
                    wb[:, 6144:8192].rearrange("p (k c) -> p k c", k=16), src3), 16)
            b.sec("gpsimd", _)
            g_n = b.gpd.n

            for sc in range(2):
                soff = 512 * sc
                dbk = dcnt[0] % 3
                dcnt[0] += 1
                po = ps_o[dbk]
                pt2 = ps_t2[dbk]
                ob = osb[dbk]
                otb = ot[dbk]
                rb = rsb[dbk]

                def _(sy, dt_=dt_, sc=sc, s0=s0, soff=soff, rb=rb, dbk=dbk):
                    b.gate_wait(sy, f"fdrs{dbk}")
                    src3 = src_res[s0 + soff:s0 + soff + 512,
                                   128 * dt_:128 * (dt_ + 1)].rearrange(
                        "(j p) c -> p j c", p=128)
                    b.syd.inc(sy.dma_start(rb[:, :].rearrange("p (j c) -> p j c", j=4),
                                           src3), 16)
                b.sec("sync", _)
                r_n = b.syd.n

                def _(t, dt_=dt_, sc=sc, wb=wb, soff=soff, po=po, dbk=dbk, g_n=g_n,
                      w6_n=w6_n):
                    t.wait_ge(b.gpd.h, g_n)
                    if dt_ == 0 and sc == 0:
                        t.wait_ge(b.gpd.h, w6_n)
                    b.gate_wait(t, f"fd_pso{dbk}")
                    for n in range(NR):
                        for k in range(8):
                            t.matmul(po[:, :],
                                     wb[:, n * 1024 + 128 * k:n * 1024 + 128 * (k + 1)],
                                     HP[:, (n * 8 + k) * SH + soff:(n * 8 + k) * SH + soff + 512],
                                     start=(n == 0 and k == 0), stop=False)
                    for k in range(16):
                        t.matmul(po[:, :], wb[:, 6144 + 128 * k:6144 + 128 * (k + 1)],
                                 HS[:, k * SH + soff:k * SH + soff + 512],
                                 start=False, stop=False)
                    ins = t.matmul(po[:, :], bdown_bf[:, 128 * dt_:128 * (dt_ + 1)],
                                   w6[:, 512 * sc:512 * (sc + 1)],
                                   start=False, stop=True)
                    b.ten.inc(ins)
                b.sec("tensor", _)
                t_n = b.ten.n
                if sc == 1:
                    b.gate_set(wname, b.ten)

                def _(v, dt_=dt_, sc=sc, po=po, ob=ob, dbk=dbk, t_n=t_n):
                    v.wait_ge(b.ten.h, t_n)
                    b.gate_wait(v, f"fd_osb{dbk}")
                    b.vec.inc(v.tensor_scalar(ob[:, :], po[:, :], 1.0,
                                              bias_c[:, 112 + dt_:113 + dt_],
                                              ALU.mult, ALU.add))
                b.sec("vector", _)
                v_n = b.vec.n
                b.gate_set(f"fd_pso{dbk}", b.vec)

                def _(t, sc=sc, ob=ob, pt2=pt2, dbk=dbk, v_n=v_n):
                    t.wait_ge(b.vec.h, v_n)
                    b.gate_wait(t, f"fd_pst{dbk}")
                    for j in range(4):
                        ins = t.transpose(pt2[:, 128 * j:128 * (j + 1)],
                                          ob[:, 128 * j:128 * (j + 1)], C["id_f32"][:, :])
                    b.ten.inc(ins)
                b.sec("tensor", _)
                t_n = b.ten.n
                b.gate_set(f"fd_osb{dbk}", b.ten)

                def _(v, sc=sc, pt2=pt2, otb=otb, dbk=dbk, t_n=t_n, r_n=r_n):
                    v.wait_ge(b.ten.h, t_n)
                    v.wait_ge(b.syd.h, r_n)
                    b.gate_wait(v, f"fd_ot{dbk}")
                    b.vec.inc(v.tensor_tensor(otb[:, :], pt2[:, :], rb[:, :], ALU.add))
                b.sec("vector", _)
                v_n = b.vec.n
                b.gate_set(f"fd_pst{dbk}", b.vec)
                b.gate_set(f"fdrs{dbk}", b.vec)

                def _(sy, dt_=dt_, sc=sc, s0=s0, soff=soff, otb=otb, v_n=v_n):
                    sy.wait_ge(b.vec.h, v_n)
                    dst3 = dst_res[s0 + soff:s0 + soff + 512,
                                   128 * dt_:128 * (dt_ + 1)].rearrange(
                        "(j p) c -> p j c", p=128)
                    b.syd.inc(sy.dma_start(dst3, otb[:, :].rearrange("p (j c) -> p j c", j=4)),
                              16)
                b.sec("sync", _)
                b.gate_set(f"fd_ot{dbk}", b.syd)
        b.barrier()
        dn_st.close()
    sa.close()


def emit_attn_b(b, io, src_res, dst_res):
    C = b.consts
    sa = ExitStack()
    XLNT = b.sb("bXLNT", [128, 16 * S], BF16, sa)
    YT = b.sb("bYT", [128, 16 * S], BF16, sa)

    b.mark("ab.ln")
    ln_st = ExitStack()
    emit_ln(b, io, src_res, 0, 16, XLNT, S, ln_st)
    b.barrier()
    ln_st.close()

    at_st = ExitStack()
    QT2 = [b.sb("ab_qt0", [128, S], BF16, at_st)] * 2
    KT2 = [b.sb("ab_kt0", [128, S], BF16, at_st)] * 2
    VT2 = [b.sb("ab_vt0", [128, 16 * 128], BF16, at_st)] * 2
    WQ2 = [b.sb("ab_wq0", [128, 16 * 128], BF16, at_st)] * 2
    WK2 = [b.sb("ab_wk0", [128, 16 * 128], BF16, at_st)] * 2
    WV2 = [b.sb("ab_wv0", [128, 16 * 128], BF16, at_st)] * 2
    TH2 = [b.sb("ab_th0", [128, S], F32, at_st)] * 2
    SS2 = [b.sb(f"ab_ss{i}", [128, S], F32, at_st) for i in range(2)]
    PP2 = [b.sb(f"ab_pp{i}", [128, S], BF16, at_st) for i in range(2)]
    PT2 = [b.sb(f"ab_pt{i}", [128, S], BF16, at_st) for i in range(2)]
    MX2 = [b.sb(f"ab_mx{i}", [128, 1], F32, at_st) for i in range(2)]
    MXC2 = [b.sb(f"ab_mxc{i}", [128, 4], F32, at_st) for i in range(2)]
    NMX2 = [b.sb(f"ab_nmx{i}", [128, 1], F32, at_st) for i in range(2)]
    DEN2 = [b.sb(f"ab_den{i}", [128, 1], F32, at_st) for i in range(2)]
    RR2 = [b.sb(f"ab_rr{i}", [128, 1], F32, at_st) for i in range(2)]
    DG2 = [b.sb(f"ab_dg{i}", [128, 128], BF16, at_st) for i in range(2)]
    ps_q = b.ps("ab_psq", [128, 512], F32, at_st)
    ps_s = [b.ps(f"ab_pss{i}", [128, 512], F32, at_st) for i in range(2)]
    ps_ys = [b.ps(f"ab_psy{i}", [128, 512], F32, at_st) for i in range(2)]
    ps_pts = [b.ps(f"ab_pspt{i}", [128, 512], F32, at_st) for i in range(2)]
    psvb = b.ps("ab_psvb", [128, 512], BF16, at_st)
    vstg = b.sb("ab_vstg", [128, 512], BF16, at_st)

    for h in range(NH_B):
        b.mark(f"ab.h{h}")
        slope = 2.0 ** (-8.0 / (h + 1))
        par = h % 2
        QT, KT, VT = QT2[par], KT2[par], VT2[par]
        WQ, WK, WV, TH = WQ2[par], WK2[par], WV2[par], TH2[par]

        def _(g, h=h, par=par, WQ=WQ, WK=WK, WV=WV):
            b.gate_wait(g, "ab_w0")
            for (dst, coff) in ((WQ, 128 * h), (WK, D + 128 * h), (WV, 2 * D + 128 * h)):
                src3 = io["Wqkv_b"][:, coff:coff + 128].rearrange("(k p) c -> p k c", p=128)
                ins = g.dma_start(dst[:, :].rearrange("p (k c) -> p k c", k=16), src3)
                b.gpd.inc(ins, 16)
        b.sec("gpsimd", _)
        g_n = b.gpd.n

        def _(sy, h=h, par=par, TH=TH):
            b.gate_wait(sy, "ab_th0")
            b.syd.inc(sy.dma_start(TH[:, :], io["c_alibi_t"][h]), 16)
        b.sec("sync", _)
        th_n = b.syd.n

        for (dst, W_, scale, bank, bname) in (
                (QT, WQ, 1.0 / math.sqrt(128.0), ps_q, "ab_psq"),
                (KT, WK, None, ps_pts[0], "ab_pt0")):
            for sc in range(4):
                def _(t, sc=sc, W_=W_, bank=bank, bname=bname, g_n=g_n):
                    t.wait_ge(b.gpd.h, g_n)
                    b.gate_wait(t, bname)
                    for k in range(16):
                        ins = t.matmul(bank[:, :], W_[:, 128 * k:128 * (k + 1)],
                                       XLNT[:, k * S + 512 * sc:k * S + 512 * (sc + 1)],
                                       start=(k == 0), stop=(k == 15))
                    b.ten.inc(ins)
                b.sec("tensor", _)
                t_n = b.ten.n

                def _(v, dst=dst, sc=sc, scale=scale, bank=bank, t_n=t_n):
                    v.wait_ge(b.ten.h, t_n)
                    if scale is None:
                        ins = v.tensor_copy(dst[:, 512 * sc:512 * (sc + 1)], bank[:, :])
                    else:
                        ins = v.tensor_scalar(dst[:, 512 * sc:512 * (sc + 1)], bank[:, :],
                                              scale, None, ALU.mult)
                    b.vec.inc(ins)
                b.sec("vector", _)
                b.gate_set(bname, b.vec)
        for sc in range(4):
            def _(t, sc=sc, WV=WV, g_n=g_n):
                t.wait_ge(b.gpd.h, g_n)
                b.gate_wait(t, "ab_psy0")
                for k in range(16):
                    ins = t.matmul(ps_ys[0][:, :], WV[:, 128 * k:128 * (k + 1)],
                                   XLNT[:, k * S + 512 * sc:k * S + 512 * (sc + 1)],
                                   start=(k == 0), stop=(k == 15))
                b.ten.inc(ins)
            b.sec("tensor", _)
            t_n = b.ten.n

            def _(v, sc=sc, t_n=t_n):
                v.wait_ge(b.ten.h, t_n)
                b.gate_wait(v, "ab_vstg")
                b.vec.inc(v.tensor_copy(vstg[:, :], ps_ys[0][:, :]))
            b.sec("vector", _)
            v_n = b.vec.n
            b.gate_set("ab_psy0", b.vec)

            def _(t, v_n=v_n):
                t.wait_ge(b.vec.h, v_n)
                b.gate_wait(t, "ab_psvb")
                for j in range(4):
                    ins = t.transpose(psvb[:, 128 * j:128 * (j + 1)],
                                      vstg[:, 128 * j:128 * (j + 1)], C["id_bf"][:, :])
                b.ten.inc(ins)
            b.sec("tensor", _)
            t_n = b.ten.n
            b.gate_set("ab_vstg", b.ten)

            def _(v, sc=sc, VT=VT, t_n=t_n):
                v.wait_ge(b.ten.h, t_n)
                b.vec.inc(v.tensor_copy(VT[:, 512 * sc:512 * (sc + 1)], psvb[:, :]))
            b.sec("vector", _)
            b.gate_set("ab_psvb", b.vec)
        qkv_v = b.vec.n
        b.gate_set("ab_w0", b.ten)

        def scores_softmax(qt):
            par = qt % 2
            SS_, PP_ = SS2[par], PP2[par]
            MXC_, MX_, NMX_, DEN_, RR_, DG_ = (MXC2[par], MX2[par], NMX2[par],
                                               DEN2[par], RR2[par], DG2[par])
            K_ = 128 * (qt + 1)
            fs = max(0, K_ - 512)
            chunks = [(i, min(i + 512, K_ - 512)) for i in range(0, K_ - 512, 512)]
            chunks.append((fs, K_))
            for ci, (c0, c1) in enumerate(chunks):
                pss = ps_s[ci % 2]

                def _(t, qt=qt, c0=c0, c1=c1, pss=pss, ci=ci):
                    if qt == 0 and ci == 0:
                        t.wait_ge(b.vec.h, qkv_v)
                        t.wait_ge(b.syd.h, th_n)
                    b.gate_wait(t, f"ab_s{ci % 2}")
                    b.ten.inc(t.matmul(pss[:, 0:c1 - c0], QT[:, 128 * qt:128 * (qt + 1)],
                                       KT[:, c0:c1], start=True, stop=True))
                b.sec("tensor", _)
                t_n = b.ten.n
                final = (ci == len(chunks) - 1)

                def _(v, c0=c0, c1=c1, pss=pss, ci=ci, t_n=t_n, final=final,
                      SS_=SS_, MXC_=MXC_, par=par):
                    v.wait_ge(b.ten.h, t_n)
                    if ci == 0:
                        b.gate_wait(v, f"ab_ssr{par}")
                    v.tensor_tensor(SS_[:, c0:c1], pss[:, 0:c1 - c0], TH[:, c0:c1],
                                    ALU.add)
                    b.vsp(v)
                    if final:
                        v.tensor_tensor(SS_[:, c1 - 128:c1], SS_[:, c1 - 128:c1],
                                        C["mask_su"][:, :], ALU.add)
                        b.vsp(v)
                    b.vec.inc(v.reduce_max(MXC_[:, ci:ci + 1], SS_[:, c0:c1], axis=AX.X))
                b.sec("vector", _)
                b.gate_set(f"ab_s{ci % 2}", b.vec)

            def _(v, K_=K_, nch=len(chunks), MXC_=MXC_, MX_=MX_, NMX_=NMX_):
                b.vsp(v)
                v.reduce_max(MX_[:, :], MXC_[:, 0:nch], axis=AX.X)
                b.vsp(v)
                v.tensor_scalar(NMX_[:, :], MX_[:, :], -1.0, None, ALU.mult)
                b.vsp(v)
                b.vec.inc(v.tensor_copy(b.consts["spacer"][:, 0:32],
                                        b.consts["spacer"][:, 32:64]))
            b.sec("vector", _)
            v_n = b.vec.n

            def _(s, K_=K_, v_n=v_n, par=par, SS_=SS_, PP_=PP_, NMX_=NMX_, DEN_=DEN_):
                s.wait_ge(b.vec.h, v_n)
                b.gate_wait(s, f"ab_pp{par}")
                b.act.inc(s.activation(PP_[:, 0:K_], SS_[:, 0:K_], AF.Exp,
                                       bias=NMX_[:, :], accum_out=DEN_[:, :]))
                b.gate_set(f"ab_ssr{par}", b.act)
            b.sec("scalar", _)
            a_n = b.act.n

            def _(v, a_n=a_n, DEN_=DEN_, RR_=RR_, DG_=DG_):
                v.wait_ge(b.act.h, a_n)
                b.vsp(v)
                v.reciprocal(RR_[:, :], DEN_[:, :])
                b.vsp(v)
                b.vec.inc(v.tensor_scalar(DG_[:, :], C["id_bf"][:, :], RR_[:, :],
                                          None, ALU.mult))
            b.sec("vector", _)

        def pv(qt):
            par = qt % 2
            PP_, DG_, PT_ = PP2[par], DG2[par], PT2[par]
            psy = ps_ys[par]
            dg_v = b.vec.n
            nkt = qt + 1
            for tg in range(ceil_div(nkt, 4)):
                ptb = ps_pts[tg % 2]
                pname = f"ab_pt{tg % 2}"

                def _(t, qt=qt, tg=tg, nkt=nkt, PP_=PP_, DG_=DG_, ptb=ptb,
                      pname=pname, dg_v=dg_v):
                    t.wait_ge(b.vec.h, dg_v)
                    b.gate_wait(t, pname)
                    for j in range(min(4, nkt - 4 * tg)):
                        kt = 4 * tg + j
                        ins = t.matmul(ptb[:, 128 * j:128 * (j + 1)],
                                       PP_[:, 128 * kt:128 * (kt + 1)], DG_[:, :],
                                       start=True, stop=True)
                    b.ten.inc(ins)
                b.sec("tensor", _)
                t_n = b.ten.n
                if tg == ceil_div(nkt, 4) - 1:
                    b.gate_set(f"ab_pp{par}", b.ten)

                def _(v, tg=tg, nkt=nkt, PT_=PT_, ptb=ptb, t_n=t_n):
                    v.wait_ge(b.ten.h, t_n)
                    nj = min(4, nkt - 4 * tg)
                    b.vec.inc(v.tensor_copy(PT_[:, 512 * tg:512 * tg + 128 * nj],
                                            ptb[:, 0:128 * nj]))
                b.sec("vector", _)
                b.gate_set(pname, b.vec)
            pt_v = b.vec.n

            def _(t, qt=qt, nkt=nkt, PT_=PT_, psy=psy, par=par, pt_v=pt_v):
                t.wait_ge(b.vec.h, pt_v)
                b.gate_wait(t, f"ab_psy{par}")
                for kt in range(nkt):
                    ins = t.matmul(psy[:, 0:128], VT[:, 128 * kt:128 * (kt + 1)],
                                   PT_[:, 128 * kt:128 * (kt + 1)],
                                   start=(kt == 0), stop=(kt == nkt - 1))
                b.ten.inc(ins)
            b.sec("tensor", _)
            t_n = b.ten.n

            def _(v, h=h, qt=qt, psy=psy, par=par, t_n=t_n):
                v.wait_ge(b.ten.h, t_n)
                b.vec.inc(v.tensor_copy(YT[:, h * S + 128 * qt:h * S + 128 * (qt + 1)],
                                        psy[:, 0:128]))
            b.sec("vector", _)
            b.gate_set(f"ab_psy{par}", b.vec)

        for qt in range(17):
            if qt < 16:
                scores_softmax(qt)
            if qt >= 1:
                pv(qt - 1)
        b.gate_set("ab_th0", b.vec)
    b.barrier()
    if DEBUG_TAPS:
        def _(sy):
            b.syd.inc(sy.dma_start(io["dbg_hp"][:, 0:16 * S], YT[:, :]), 16)
        b.sec("sync", _)
        b.barrier()
    at_st.close()

    b.mark("ab.out")
    op_st = ExitStack()
    wo = [b.sb(f"bo_wo{i}", [128, 16 * 128], BF16, op_st) for i in range(2)]
    osb = [b.sb(f"bo_osb{i}", [128, 512], F32, op_st) for i in range(2)]
    rsb = [b.sb(f"bo_rsb{i}", [128, 512], F32, op_st) for i in range(2)]
    ot = [b.sb(f"bo_ot{i}", [128, 512], F32, op_st) for i in range(2)]
    ps_o = [b.ps(f"bo_pso{i}", [128, 512], F32, op_st) for i in range(2)]
    ps_t = [b.ps(f"bo_pst{i}", [128, 512], F32, op_st) for i in range(2)]
    for dt_ in range(16):
        wb = wo[dt_ % 2]
        wname = f"bow{dt_ % 2}"

        def _(g, dt_=dt_, wb=wb, wname=wname):
            b.gate_wait(g, wname)
            src3 = io["Wo_b"][:, 128 * dt_:128 * (dt_ + 1)].rearrange(
                "(k p) c -> p k c", p=128)
            b.gpd.inc(g.dma_start(wb[:, :].rearrange("p (k c) -> p k c", k=16), src3), 16)
        b.sec("gpsimd", _)
        g_n = b.gpd.n

        for sc in range(4):
            soff = 512 * sc
            rb = rsb[sc % 2]

            def _(sy, dt_=dt_, sc=sc, soff=soff, rb=rb):
                b.gate_wait(sy, f"bors{sc % 2}")
                src3 = src_res[soff:soff + 512, 128 * dt_:128 * (dt_ + 1)].rearrange(
                    "(j p) c -> p j c", p=128)
                b.syd.inc(sy.dma_start(rb[:, :].rearrange("p (j c) -> p j c", j=4), src3), 16)
            b.sec("sync", _)
            r_n = b.syd.n

            par2 = sc % 2
            po = ps_o[par2]
            pt = ps_t[par2]
            ob = osb[par2]
            otb = ot[par2]

            def _(t, dt_=dt_, sc=sc, wb=wb, soff=soff, po=po, par2=par2, g_n=g_n):
                t.wait_ge(b.gpd.h, g_n)
                b.gate_wait(t, f"bo_pso{par2}")
                for k in range(16):
                    ins = t.matmul(po[:, :], wb[:, 128 * k:128 * (k + 1)],
                                   YT[:, k * S + soff:k * S + soff + 512],
                                   start=(k == 0), stop=(k == 15))
                b.ten.inc(ins)
            b.sec("tensor", _)
            t_n = b.ten.n
            if sc == 3:
                b.gate_set(wname, b.ten)

            def _(v, po=po, ob=ob, par2=par2, t_n=t_n):
                v.wait_ge(b.ten.h, t_n)
                b.gate_wait(v, f"bo_osb{par2}")
                b.vec.inc(v.tensor_copy(ob[:, :], po[:, :]))
            b.sec("vector", _)
            v_n = b.vec.n
            b.gate_set(f"bo_pso{par2}", b.vec)

            def _(t, ob=ob, pt=pt, par2=par2, v_n=v_n):
                t.wait_ge(b.vec.h, v_n)
                b.gate_wait(t, f"bo_pst{par2}")
                for j in range(4):
                    ins = t.transpose(pt[:, 128 * j:128 * (j + 1)],
                                      ob[:, 128 * j:128 * (j + 1)], C["id_f32"][:, :])
                b.ten.inc(ins)
            b.sec("tensor", _)
            t_n = b.ten.n
            b.gate_set(f"bo_osb{par2}", b.ten)

            def _(v, pt=pt, otb=otb, par2=par2, t_n=t_n, r_n=r_n):
                v.wait_ge(b.ten.h, t_n)
                v.wait_ge(b.syd.h, r_n)
                b.gate_wait(v, f"bo_ot{par2}")
                b.vec.inc(v.tensor_tensor(otb[:, :], pt[:, :], rb[:, :], ALU.add))
            b.sec("vector", _)
            v_n = b.vec.n
            b.gate_set(f"bo_pst{par2}", b.vec)
            b.gate_set(f"bors{sc % 2}", b.vec)

            def _(sy, dt_=dt_, sc=sc, soff=soff, otb=otb, v_n=v_n):
                sy.wait_ge(b.vec.h, v_n)
                dst3 = dst_res[soff:soff + 512, 128 * dt_:128 * (dt_ + 1)].rearrange(
                    "(j p) c -> p j c", p=128)
                b.syd.inc(sy.dma_start(dst3, otb[:, :].rearrange("p (j c) -> p j c", j=4)), 16)
            b.sec("sync", _)
            b.gate_set(f"bo_ot{par2}", b.syd)
    b.barrier()
    op_st.close()
    sa.close()


def build_program(debug_taps=False, stage=9, layers=("a1", "f1", "ab", "f2")):
    global DEBUG_TAPS, STAGE
    DEBUG_TAPS = debug_taps
    STAGE = stage
    nc = bass.Bass("TRN2", target_bir_lowering=False)
    io = declare_io(nc, debug_taps)
    with ExitStack() as st:
        block = st.enter_context(nc.Block())
        b = B(nc, block, st)
        load_consts(b, io)
        if "a1" in layers:
            wp_st = ExitStack()
            b.mark("wprep")
            emit_wprep_attn_a(b, io, wp_st)
            b.barrier()
            wp_st.close()
            if STAGE >= 1:
                emit_attn_a(b, io)
        if "f1" in layers:
            emit_ffn(b, io, io["res"], io["res"], "f1")
        if "ab" in layers:
            emit_attn_b(b, io, io["res"], io["res"])
        if "f2" in layers:
            emit_ffn(b, io, io["res"], io["out"], "f2")
        b.barrier()
        if "f2" not in layers:
            cp_st = ExitStack()
            cp = b.sb("cp", [128, D], F32, cp_st)
            csrc = io["res"] if (STAGE >= 6 and ("a1" in layers or "f1" in layers)) \
                else io["x_in"]
            for it in range(16):
                def _(sy, it=it):
                    if it == 0:
                        sy.wait_ge(b.syd.h, b.syd.n)
                    b.syd.inc(sy.dma_start(cp[:, :], csrc[128 * it:128 * (it + 1), :]), 16)
                    sy.wait_ge(b.syd.h, b.syd.n)
                    b.syd.inc(sy.dma_start(io["out"][128 * it:128 * (it + 1), :], cp[:, :]),
                              16)
                    sy.wait_ge(b.syd.h, b.syd.n)
                b.sec("sync", _)
            cp_st.close()
        def _(sy):
            sy.wait_ge(b.syd.h, b.syd.n)
            sy.wait_ge(b.gpd.h, b.gpd.n)
        b.sec("sync", _)
    nc._phase_marks = b.marks
    return nc, io


def make_in_maps(inputs, consts):
    x = inputs["L0_attn_a_x"]
    base = {
        "w_qkv": inputs["L0_attn_a_w_qkv"], "w_gate": inputs["L0_attn_a_w_gate"],
        "w_beta": inputs["L0_attn_a_w_beta"], "w_alpha": inputs["L0_attn_a_w_alpha"],
        "log_A": np.ascontiguousarray(inputs["L0_attn_a_log_A"].reshape(1, H_V)),
        "dt_bias": np.ascontiguousarray(inputs["L0_attn_a_dt_bias"].reshape(1, H_V)),
        "conv_w": np.ascontiguousarray(inputs["L0_attn_a_conv_w"].reshape(FUSED, 4)),
        "rms_w": np.ascontiguousarray(inputs["L0_attn_a_rms_w"].reshape(1, DV)),
        "w_out": inputs["L0_attn_a_w_out"],
        "Wqkv_b": inputs["L0_attn_b_W_qkv"], "Wo_b": inputs["L0_attn_b_W_o"],
        "Wg": inputs["L0_ffn_Wg"],
        "bg": np.ascontiguousarray(inputs["L0_ffn_bg"].reshape(1, NE)),
        "Wgate": inputs["L0_ffn_Wgate"], "bgate": inputs["L0_ffn_bgate"],
        "Wup": inputs["L0_ffn_Wup"], "bup": inputs["L0_ffn_bup"],
        "Wdown": inputs["L0_ffn_Wdown"], "bdown": inputs["L0_ffn_bdown"],
        "Ws_up": inputs["L0_ffn_Ws_up"],
        "bs_up": np.ascontiguousarray(inputs["L0_ffn_bs_up"].reshape(1, D)),
        "Ws_down": inputs["L0_ffn_Ws_down"],
        "bs_down": np.ascontiguousarray(inputs["L0_ffn_bs_down"].reshape(1, D)),
    }
    import ml_dtypes
    cc = {}
    for k, v in consts.items():
        dt = CONST_SHAPES[k][1]
        cc[k] = v.astype(ml_dtypes.bfloat16) if dt == BF16 else v
    maps = []
    for c in range(8):
        m = dict(base)
        m.update(cc)
        m["x_in"] = np.ascontiguousarray(x[c])
        maps.append(m)
    return maps


def kernel(**inputs):
    nc, io = build_program()
    maps = make_in_maps(inputs, host_constants())
    res = run_bass_kernel_spmd(nc, maps, core_ids=list(range(8)))
    out = np.stack([r["out"] for r in res.results], axis=0)
    return out
